# version 06
# baseline (speedup 1.0000x reference)
import contextlib
import math
import numpy as np
import ml_dtypes
import concourse.bass as bass
import concourse.mybir as mybir
from concourse.bass_utils import run_bass_kernel_spmd

F32 = mybir.dt.float32
BF16 = mybir.dt.bfloat16
I32 = mybir.dt.int32
AF = mybir.ActivationFunctionType
ALU = mybir.AluOpType
AX = mybir.AxisListType
BF16NP = ml_dtypes.bfloat16

D = 1024
HEADS = 16
DH = 64
T_FULL = 4096
T_CTX = 1024
T_OWN = 2048
T_PROC = 3072
NT_FULL = 32
NT_PROC = 24
NT_OWN = 16
DHID = 4096
EPS = 1e-5
HS, DS = 8, 32
HL, DL = 8, 128
HD = 64
WIN_S, WIN_L = 256, 1024
PIH = float(np.float32(np.pi / 2))
PI = float(np.float32(np.pi))
TWOPI = float(np.float32(2 * np.pi))

DEBUG = False
PHASES = 9



def _theta_tables(positions):
    n = DH // 2
    j = np.arange(n, dtype=np.float64)
    freqs = (40.0) ** (j / (n - 1))
    pos = positions.astype(np.float64) / float(T_FULL)
    theta = 2.0 * np.pi * pos[:, None] * freqs[None, :]
    red = np.remainder(theta + np.pi, 2.0 * np.pi) - np.pi
    red[positions < 0, :] = 0.0
    full = np.concatenate([-red, red], axis=1)
    hi = full.astype(np.float32)
    lo = (full - hi.astype(np.float64)).astype(np.float32)
    return hi, lo


def _win_mask_tile(qt0, kb, W):
    p = np.arange(128)[:, None]
    j = np.arange(512)[None, :]
    diff = (qt0 + j) - (kb * 128 + p)
    return ((diff >= 0) & (diff < W)).astype(BF16NP)


def _win_tiles(W):
    out = []
    for qc in range(4):
        qt0 = T_CTX + 512 * qc
        lo_kb = max(0, (qt0 - W + 1) // 128)
        hi_kb = min(NT_PROC - 1, (qt0 + 511) // 128)
        ents = []
        for kb in range(lo_kb, hi_kb + 1):
            mn = qt0 - (kb * 128 + 127)
            mx = qt0 + 511 - kb * 128
            if mn >= 0 and mx <= W - 1:
                ents.append((kb, None))
            else:
                m = _win_mask_tile(qt0, kb, W)
                if m.any():
                    ents.append((kb, m))
        out.append(ents)
    return out


WIN_S_TILES = _win_tiles(WIN_S)
WIN_L_TILES = _win_tiles(WIN_L)
_MASKS = []
_MASK_IDX = {}
for _wt in (WIN_S_TILES, WIN_L_TILES):
    for _ents in _wt:
        for _kb, _m in _ents:
            if _m is not None:
                _key = _m.tobytes()
                if _key not in _MASK_IDX:
                    _MASK_IDX[_key] = len(_MASKS)
                    _MASKS.append(_m)
N_MASKS = len(_MASKS)


def _mask_slot(m):
    return _MASK_IDX[m.tobytes()]



def build_program():
    nc = bass.Bass("TRN2", target_bir_lowering=False)

    def din(name, shape, dt=F32):
        return nc.dram_tensor(name, shape, dt, kind="ExternalInput").ap()

    def dscr(name, shape, dt=BF16):
        kind = "ExternalOutput" if DEBUG else "Internal"
        return nc.dram_tensor(name, shape, dt, kind=kind).ap()

    x_proc = din("x_proc", [T_PROC, D])
    x_full = din("x_full", [T_FULL, D])
    qkv_w = din("qkv_w", [3 * D, D])
    qkv_b = din("qkv_b", [1, 3 * D])
    proj_w = din("proj_w", [D, D])
    proj_b = din("proj_b", [1, D])
    bWqk_s = din("bWqk_s", [D, 2 * HS * DS])
    bWv_s = din("bWv_s", [D, HS * HD])
    bWqk_l = din("bWqk_l", [D, 2 * HL * DL])
    bWv_l = din("bWv_l", [D, HL * HD])
    bWproj = din("bWproj", [D, D])
    w1 = din("w1", [DHID, D])
    b1 = din("b1", [1, DHID])
    w2 = din("w2", [D, DHID])
    b2 = din("b2", [1, D])
    gam_rep = din("gam_rep", [128, D])
    bet_rep = din("bet_rep", [128, D])
    thq_hi = din("thq_hi", [T_PROC, DH])
    thq_lo = din("thq_lo", [T_PROC, DH])
    thk_hi = din("thk_hi", [T_FULL, DH])
    thk_lo = din("thk_lo", [T_FULL, DH])
    colmask = din("colmask", [128, NT_PROC])
    wmasks = din("wmasks", [128, max(1, N_MASKS) * 512], BF16)
    ident_f = din("ident_f", [128, 128])
    ones_rw = din("ones_rw", [1, 128])

    out = nc.dram_tensor("out", [T_OWN, D], F32, kind="ExternalOutput").ap()

    q_sp = dscr("q_sp", [D, T_PROC])
    k_sp = dscr("k_sp", [D, T_FULL])
    v_sp = dscr("v_sp", [8 * 128, 32 * 130])
    ks_sp = dscr("ks_sp", [HS * DS, T_PROC])
    qs_sp = dscr("qs_sp", [HS * DS, T_OWN])
    kl_sp = dscr("kl_sp", [HL * DL, T_PROC])
    ql_sp = dscr("ql_sp", [HL * DL, T_OWN])
    vs_sp = dscr("vs_sp", [4 * 128, NT_PROC * 130])
    vl_sp = dscr("vl_sp", [4 * 128, NT_PROC * 130])
    w1q_sp = dscr("w1q_sp", [D, DHID])
    w2q_sp = dscr("w2q_sp", [DHID, D])
    hq_sp = dscr("hq_sp", [NT_PROC * 128, DHID], F32)
    x1_dbg = dscr("x1_dbg", [128, NT_PROC * D], F32) if DEBUG else None
    bm_dbg = dscr("bm_dbg", [128, 2 * HEADS], F32) if DEBUG else None
    wp_dbg = dscr("wp_dbg", [128, 8 * 3 * D], BF16) if DEBUG else None
    tab_dbg = dscr("tab_dbg", [128, 2 * 56 * DH], BF16) if DEBUG else None

    with contextlib.ExitStack() as TOP:
        E = TOP.enter_context

        ident_bf = E(nc.sbuf_tensor("ident_bf", [128, 128], BF16))
        ident_fs = E(nc.sbuf_tensor("ident_fs", [128, 128], F32))
        ones_bf = E(nc.sbuf_tensor("ones_bf", [1, 128], BF16))
        ones_fs = E(nc.sbuf_tensor("ones_fs", [1, 128], F32))
        bmat = E(nc.sbuf_tensor("bmat", [128, HEADS], F32))
        bmat_b = E(nc.sbuf_tensor("bmat_b", [128, HEADS], F32))
        PS = [E(nc.psum_tensor("ps%d" % i, [128, 512], F32)) for i in range(8)]

        def psbf(i):
            return PS[i][:].bitcast(BF16)

        semnames = (["sy", "sc", "gp", "pe", "ac", "ve"]
                    + ["syp%d" % i for i in range(6)]
                    + ["gpp%d" % i for i in range(4)]
                    + ["scp%d" % i for i in range(2)])
        SEM = {n: E(nc.semaphore(n)) for n in semnames}
        CNT = {n: 0 for n in SEM}
        _RR = {"sy": 0, "gp": 0, "sc": 0}
        _POOL = {"sy": ["syp%d" % i for i in range(6)],
                 "gp": ["gpp%d" % i for i in range(4)],
                 "sc": ["scp%d" % i for i in range(2)]}

        block = E(nc.Block())

        def INC(ins, s, k=1):
            ins.then_inc(SEM[s], k)
            CNT[s] += k

        def DMA(eng, q, out, in_):
            pool = _POOL[q]
            sname = pool[_RR[q] % len(pool)]
            _RR[q] += 1
            if CNT[sname]:
                eng.wait_ge(SEM[sname], CNT[sname])
            ins = eng.dma_start(out=out, in_=in_)
            INC(ins, sname, 16)
            return (sname, CNT[sname])

        def WTOK(eng, tok):
            if tok is not None:
                eng.wait_ge(SEM[tok[0]], tok[1])

        def mark(s):
            return CNT[s]

        def vlink(v, ins):
            INC(ins, "ve")
            v.wait_ge(SEM["ve"], CNT["ve"])

        def alink(sc, ins):
            INC(ins, "ac")
            sc.wait_ge(SEM["ac"], CNT["ac"])

        def emit_barrier():
            vals = dict(CNT)
            for ename in ["sync", "gpsimd", "scalar", "tensor", "vector"]:
                def bodyf(eng, vals=vals):
                    for sn, vv in vals.items():
                        if vv:
                            eng.wait_ge(SEM[sn], vv)
                getattr(block, ename)(bodyf)

        tok_const = []

        @block.sync
        def _(sync):
            tok_const.append(DMA(sync, "sy", ident_fs[:], ident_f))
            tok_const.append(DMA(sync, "sy", ones_fs[:], ones_rw))

        tok_const_bf = []

        @block.gpsimd
        def _(g):
            for tk in tok_const:
                WTOK(g, tk)
            tok_const_bf.append(DMA(g, "gp", ident_bf[:], ident_fs[:]))
            tok_const_bf.append(DMA(g, "gp", ones_bf[:], ones_fs[:]))

        WSCOPE = contextlib.ExitStack()
        qkvWp = WSCOPE.enter_context(nc.sbuf_tensor("qkvWp", [128, 8 * 3 * D], BF16))
        qkvB_bf = WSCOPE.enter_context(nc.sbuf_tensor("qkvB_bf", [1, 3 * D], BF16))

        def emit_wt_transpose(w_ap, nrow_tiles, dst_sb, dst_cols, ES):
            wbuf = ES.enter_context(nc.sbuf_tensor("wtb_%s" % dst_sb.name, [128, 2 * D], BF16))
            ck_ld, ck_tp, ck_ev = {}, {}, {}
            for r in range(nrow_tiles):
                @block.gpsimd
                def _(g, r=r):
                    if r >= 2:
                        g.wait_ge(SEM["pe"], ck_tp[r - 2])
                    for tk in tok_const_bf:
                        WTOK(g, tk)
                    ck_ld[r] = DMA(g, "gp", wbuf[:, (r % 2) * D:(r % 2 + 1) * D],
                                   w_ap[r * 128:(r + 1) * 128, :])

                @block.tensor
                def _(t, r=r):
                    WTOK(t, ck_ld[r])
                    if r >= 2:
                        t.wait_ge(SEM["ac"], ck_ev[r - 2])
                    src = wbuf[:, (r % 2) * D:(r % 2 + 1) * D]
                    bank = psbf(6 + (r % 2))
                    for c in range(8):
                        ins = t.transpose(bank[:, c * 128:(c + 1) * 128],
                                          src[:, c * 128:(c + 1) * 128], ident_bf[:])
                    INC(ins, "pe")
                    ck_tp[r] = mark("pe")

                @block.scalar
                def _(s, r=r):
                    s.wait_ge(SEM["pe"], ck_tp[r])
                    bank = psbf(6 + (r % 2))
                    for c in range(8):
                        ins = s.copy(dst_sb[:, c * dst_cols + r * 128: c * dst_cols + (r + 1) * 128],
                                     bank[:, c * 128:(c + 1) * 128])
                    INC(ins, "ac")
                    ck_ev[r] = mark("ac")
            return ck_ev[nrow_tiles - 1]

        with contextlib.ExitStack() as WTMP:
            ck_qkvw = emit_wt_transpose(qkv_w, 24, qkvWp, 3 * D, WTMP)

        ck_biases = []

        @block.gpsimd
        def _(g):
            ck_biases.append(DMA(g, "gp", qkvB_bf[:], qkv_b))

        emit_barrier()

        with contextlib.ExitStack() as P2:
            E2 = P2.enter_context
            xt2 = E2(nc.sbuf_tensor("xt2", [128, 2 * D], F32))
            sqsc = E2(nc.sbuf_tensor("sqsc", [128, D], F32))
            stat = E2(nc.sbuf_tensor("stat", [128, 16], F32))
            xln2 = E2(nc.sbuf_tensor("xln2", [128, 2 * D], BF16))
            xf2 = E2(nc.sbuf_tensor("xf2", [128, 2 * D], BF16))
            ktm2 = E2(nc.sbuf_tensor("ktm2", [128, 2 * D], BF16))
            rt1 = E2(nc.sbuf_tensor("rt1", [128, D], BF16))
            rt2 = E2(nc.sbuf_tensor("rt2", [128, D], BF16))
            kro2 = E2(nc.sbuf_tensor("kro2", [128, 2 * D], BF16))
            kfm2 = E2(nc.sbuf_tensor("kfm2", [128, 2 * D], BF16))
            v1t2 = E2(nc.sbuf_tensor("v1t2", [128, 2 * HEADS * 65], BF16))
            nsq = E2(nc.sbuf_tensor("nsq", [128, D], BF16))
            ntmp = E2(nc.sbuf_tensor("ntmp", [128, HEADS], F32))
            nq_acc = E2(nc.sbuf_tensor("nq_acc", [128, HEADS], F32))
            nk_acc = E2(nc.sbuf_tensor("nk_acc", [128, HEADS], F32))
            kdl = E2(nc.sbuf_tensor("kdl", [128, 2 * D], BF16))
            dprod = E2(nc.sbuf_tensor("dprod", [128, D], BF16))
            ind2 = E2(nc.sbuf_tensor("ind2", [128, 2], BF16))
            dacc = E2(nc.sbuf_tensor("dacc", [128, HEADS], F32))
            Ctab = E2(nc.sbuf_tensor("Ctab", [128, (NT_FULL + NT_PROC) * DH], BF16))
            Stab = E2(nc.sbuf_tensor("Stab", [128, (NT_FULL + NT_PROC) * DH], BF16))

            NTK = NT_FULL

            @block.vector
            def _(v):
                dstp = v1t2[:].rearrange("p (k c) -> p k c", c=65)[:, :, 64:65]
                ins = v.memset(dstp, 1.0)
                INC(ins, "ve")
            ck_v1ones = mark("ve")

            with contextlib.ExitStack() as THS:
                thbuf = THS.enter_context(
                    nc.sbuf_tensor("thbuf", [128, (NT_FULL + NT_PROC) * DH], F32))
                thlo = THS.enter_context(
                    nc.sbuf_tensor("thlo", [128, (NT_FULL + NT_PROC) * DH], F32))
                thw = THS.enter_context(
                    nc.sbuf_tensor("thw", [128, (NT_FULL + NT_PROC) * DH], F32))

                tok_th = []

                @block.sync
                def _(sync):
                    def tiled(ap):
                        return ap.rearrange("(n p) c -> p n c", p=128)

                    def dst3(buf, o, nt):
                        return buf[:, o * DH:(o + nt) * DH].rearrange(
                            "p (n c) -> p n c", n=nt)
                    tok_th.append(DMA(sync, "sy", dst3(thbuf, 0, NTK), tiled(thk_hi)))
                    tok_th.append(DMA(sync, "sy", dst3(thbuf, NTK, NT_PROC), tiled(thq_hi)))
                    tok_th.append(DMA(sync, "sy", dst3(thlo, 0, NTK), tiled(thk_lo)))
                    tok_th.append(DMA(sync, "sy", dst3(thlo, NTK, NT_PROC), tiled(thq_lo)))

                @block.vector
                def _(v):
                    for tk in tok_th:
                        WTOK(v, tk)
                    vlink(v, v.tensor_add(thw[:], thbuf[:], thlo[:]))
                    vlink(v, v.tensor_scalar(thbuf[:], thw[:], PIH, None, ALU.add))
                    vlink(v, v.tensor_single_scalar(thlo[:], thbuf[:], -PI, ALU.is_ge))
                    vlink(v, v.tensor_scalar(thbuf[:], thbuf[:], TWOPI, None, ALU.add))
                    vlink(v, v.scalar_tensor_tensor(thbuf[:], thlo[:], -TWOPI, thbuf[:], ALU.mult, ALU.add))
                    vlink(v, v.tensor_single_scalar(thlo[:], thw[:], PI - PIH, ALU.is_ge))
                    vlink(v, v.scalar_tensor_tensor(thbuf[:], thlo[:], -TWOPI, thbuf[:], ALU.mult, ALU.add))
                    vlink(v, v.tensor_scalar(thbuf[:], thbuf[:], -PI, PI, ALU.max, ALU.min))
                    vlink(v, v.tensor_scalar(thw[:], thw[:], -PI, PI, ALU.max, ALU.min))
                    v.memset(nq_acc[:], 0.0)
                    ins = v.memset(nk_acc[:], 0.0)
                    INC(ins, "ve")
                ck_thv = mark("ve")

                @block.scalar
                def _(s):
                    s.wait_ge(SEM["ve"], ck_thv)
                    s.activation(Stab[:], thw[:], AF.Sin)
                    ins = s.activation(Ctab[:], thbuf[:], AF.Sin)
                    INC(ins, "ac")
                ck_tabs = mark("ac")

            ck_x, ck_rs, ck_sq, ck_ch, ck_rst, ck_ln = {}, {}, {}, {}, {}, {}
            ck_tpx, ck_evx, ck_mm, ck_kvev, ck_rope, ck_nrm = {}, {}, {}, {}, {}, {}
            ck_tpk, ck_evk, ck_spill, ck_nsq = {}, {}, {}, {}

            def p2_tile(i, is_kv):
                b = i % 2
                sb = i % 2
                st = stat[:, sb * 8:(sb + 1) * 8]
                if is_kv:
                    src = x_full[i * 128:(i + 1) * 128, :]
                    tabslice = slice(i * DH, (i + 1) * DH)
                else:
                    j = i - NT_FULL
                    src = x_proc[j * 128:(j + 1) * 128, :]
                    tabslice = slice((NTK + j) * DH, (NTK + j + 1) * DH)

                @block.sync
                def _(sync, i=i):
                    if i >= 2:
                        sync.wait_ge(SEM["ve"], ck_ln[i - 2])
                    ck_x[i] = DMA(sync, "sy", xt2[:, b * D:(b + 1) * D], src)

                @block.vector
                def _(v, i=i):
                    WTOK(v, ck_x[i])
                    if i >= 2:
                        v.wait_ge(SEM["ve"], ck_ch[i - 2])
                    ins = v.reduce_sum(st[:, 0:1], xt2[:, b * D:(b + 1) * D], AX.X)
                    INC(ins, "ve")
                    ck_rs[i] = mark("ve")

                @block.scalar
                def _(s, i=i):
                    WTOK(s, ck_x[i])
                    s.wait_ge(SEM["ve"], ck_rs[i])
                    ins = s.activation(sqsc[:], xt2[:, b * D:(b + 1) * D], AF.Square,
                                       accum_out=st[:, 1:2])
                    INC(ins, "ac")
                    ck_sq[i] = mark("ac")

                @block.vector
                def _(v, i=i):
                    v.wait_ge(SEM["ac"], ck_sq[i])
                    v.tensor_scalar(st[:, 2:3], st[:, 0:1], 1.0 / D, None, ALU.mult)
                    vlink(v, v.tensor_scalar(st[:, 3:4], st[:, 1:2], 1.0 / D, None, ALU.mult))
                    vlink(v, v.tensor_tensor(st[:, 4:5], st[:, 2:3], st[:, 2:3], ALU.mult))
                    vlink(v, v.tensor_tensor(st[:, 5:6], st[:, 3:4], st[:, 4:5], ALU.subtract))
                    vlink(v, v.tensor_scalar(st[:, 6:7], st[:, 5:6], EPS, None, ALU.add))
                    ins = v.reciprocal(st[:, 7:8], st[:, 6:7])
                    INC(ins, "ve")
                    ck_ch[i] = mark("ve")

                @block.scalar
                def _(s, i=i):
                    s.wait_ge(SEM["ve"], ck_ch[i])
                    ins = s.activation(st[:, 6:7], st[:, 7:8], AF.Sqrt)
                    INC(ins, "ac")
                    ck_rst[i] = mark("ac")

                @block.vector
                def _(v, i=i):
                    v.wait_ge(SEM["ac"], ck_rst[i])
                    if i >= 2:
                        v.wait_ge(SEM["pe"], ck_tpx[i - 2])
                    ins = v.tensor_scalar(xln2[:, b * D:(b + 1) * D], xt2[:, b * D:(b + 1) * D],
                                          st[:, 2:3], st[:, 6:7], ALU.subtract, ALU.mult)
                    INC(ins, "ve")
                    ck_ln[i] = mark("ve")

                @block.tensor
                def _(t, i=i):
                    t.wait_ge(SEM["ve"], ck_ln[i])
                    if i >= 2:
                        t.wait_ge(SEM["ac"], ck_evx[i - 2])
                    else:
                        t.wait_ge(SEM["ac"], ck_qkvw)
                    src_ = xln2[:, b * D:(b + 1) * D]
                    bank = psbf(6 + (i % 2))
                    for c in range(8):
                        ins = t.transpose(bank[:, c * 128:(c + 1) * 128],
                                          src_[:, c * 128:(c + 1) * 128], ident_bf[:])
                    INC(ins, "pe")
                    ck_tpx[i] = mark("pe")

                @block.scalar
                def _(s, i=i):
                    s.wait_ge(SEM["pe"], ck_tpx[i])
                    if i >= 2:
                        s.wait_ge(SEM["pe"], ck_mm[i - 2][-1])
                    ins = s.copy(xf2[:, b * D:(b + 1) * D], psbf(6 + (i % 2)))
                    INC(ins, "ac")
                    ck_evx[i] = mark("ac")

                slices = (range(2, 6) if is_kv else range(0, 2))
                ck_mm[i] = []
                for sn, sl in enumerate(slices):
                    bank = PS[sn % 4]

                    @block.tensor
                    def _(t, i=i, sl=sl, sn=sn, bank=bank):
                        t.wait_ge(SEM["ac"], ck_evx[i])
                        WTOK(t, ck_biases[0])
                        if i >= 1 and sn < len(ck_kvev.get(i - 1, [])):
                            t.wait_ge(SEM["ac"], ck_kvev[i - 1][sn])
                        for c in range(8):
                            t.matmul(bank[:, 0:512],
                                     xf2[:, b * D + c * 128: b * D + (c + 1) * 128],
                                     qkvWp[:, c * 3072 + sl * 512:(c * 3072) + (sl + 1) * 512],
                                     start=(c == 0), stop=False)
                        ins = t.matmul(bank[:, 0:512], ones_bf[:],
                                       qkvB_bf[:, sl * 512:(sl + 1) * 512],
                                       start=False, stop=True)
                        INC(ins, "pe")
                        ck_mm[i].append(mark("pe"))

                ck_kvev[i] = []
                for sn, sl in enumerate(slices):
                    bank = PS[sn % 4]

                    @block.scalar
                    def _(s, i=i, sl=sl, sn=sn, bank=bank, is_kv=is_kv):
                        s.wait_ge(SEM["pe"], ck_mm[i][sn])
                        if is_kv and sl >= 4:
                            if i >= 2 and len(ck_spill.get(i - 2, [])) > 1:
                                WTOK(s, ck_spill[i - 2][1])
                            else:
                                s.wait_ge(SEM["ve"], ck_v1ones)
                            half = v1t2[:, b * HEADS * 65:(b + 1) * HEADS * 65]
                            dst = half.rearrange("p (h c) -> p h c", c=65)[
                                :, (sl - 4) * 8:(sl - 4) * 8 + 8, 0:64]
                            ins = s.copy(dst, bank[:, 0:512].rearrange("p (h c) -> p h c", h=8))
                        else:
                            if i >= 2:
                                s.wait_ge(SEM["ve"], ck_rope[i - 2])
                            ins = s.copy(ktm2[:, b * D + sn * 512: b * D + (sn + 1) * 512],
                                         bank[:, 0:512])
                        INC(ins, "ac")
                        ck_kvev[i].append(mark("ac"))


                @block.vector
                def _(v, i=i, tabslice=tabslice):
                    v.wait_ge(SEM["ac"], ck_kvev[i][1])
                    v.wait_ge(SEM["ac"], ck_tabs)
                    if i >= 2:
                        v.wait_ge(SEM["pe"], ck_tpk[i - 2])
                        v.wait_ge(SEM["ac"], ck_nsq[i - 2])
                    C = Ctab[:, tabslice]
                    S = Stab[:, tabslice]

                    def heads_ap(buf, off, lo):
                        return buf[:, off:off + D].rearrange("p (h s c) -> p h s c",
                                                             h=HEADS, s=2)[:, :, lo, :]

                    def tab_ap(tb, lo):
                        t3 = tb.rearrange("p (s c) -> p s c", s=2)[:, lo:lo + 1, :]
                        return t3.broadcast_to([128, HEADS, 32])

                    src_lo = heads_ap(ktm2, b * D, 0)
                    src_hi = heads_ap(ktm2, b * D, 1)
                    t1lo = heads_ap(rt1, 0, 0)
                    t2lo = heads_ap(rt2, 0, 0)
                    t1hi = heads_ap(rt1, 0, 1)
                    t2hi = heads_ap(rt2, 0, 1)
                    v.tensor_tensor(t1lo, src_lo, tab_ap(C, 0), ALU.mult)
                    v.tensor_tensor(t2lo, src_hi, tab_ap(S, 0), ALU.mult)
                    v.tensor_tensor(t1hi, src_hi, tab_ap(C, 1), ALU.mult)
                    vlink(v, v.tensor_tensor(t2hi, src_lo, tab_ap(S, 1), ALU.mult))
                    v.tensor_tensor(heads_ap(kro2, b * D, 0), t1lo, t2lo, ALU.add)
                    ins = v.tensor_tensor(heads_ap(kro2, b * D, 1), t1hi, t2hi, ALU.add)
                    INC(ins, "ve")
                    ck_rope[i] = mark("ve")

                @block.scalar
                def _(s, i=i):
                    s.wait_ge(SEM["ve"], ck_rope[i])
                    ins = s.activation(nsq[:], kro2[:, b * D:(b + 1) * D], AF.Square)
                    INC(ins, "ac")
                    ck_nsq[i] = mark("ac")

                @block.vector
                def _(v, i=i, is_kv=is_kv):
                    v.wait_ge(SEM["ac"], ck_nsq[i])
                    vlink(v, v.reduce_sum(ntmp[:], nsq[:].rearrange("p (h c) -> p h c", h=HEADS), AX.X))
                    acc = nk_acc if is_kv else nq_acc
                    ins = v.tensor_tensor(acc[:], acc[:], ntmp[:], ALU.max)
                    INC(ins, "ve")

                @block.tensor
                def _(t, i=i):
                    t.wait_ge(SEM["ve"], ck_rope[i])
                    if i >= 2:
                        t.wait_ge(SEM["ac"], ck_evk[i - 2])
                    bank = psbf(4 + (i % 2))
                    for c in range(8):
                        ins = t.transpose(bank[:, c * 128:(c + 1) * 128],
                                          kro2[:, b * D + c * 128: b * D + (c + 1) * 128],
                                          ident_bf[:])
                    INC(ins, "pe")
                    ck_tpk[i] = mark("pe")

                @block.scalar
                def _(s, i=i):
                    s.wait_ge(SEM["pe"], ck_tpk[i])
                    if i >= 2 and ck_spill.get(i - 2):
                        WTOK(s, ck_spill[i - 2][0])
                    ins = s.copy(kfm2[:, b * D:(b + 1) * D], psbf(4 + (i % 2)))
                    INC(ins, "ac")
                    ck_evk[i] = mark("ac")

                @block.sync
                def _(sync, i=i, is_kv=is_kv):
                    sync.wait_ge(SEM["ac"], ck_evk[i])
                    dst_sp = k_sp if is_kv else q_sp
                    ti = i if is_kv else i - NT_FULL
                    dst = dst_sp.rearrange("(c p) t -> p c t", p=128)[
                        :, :, ti * 128:(ti + 1) * 128]
                    toks = [DMA(sync, "sy", dst,
                                kfm2[:, b * D:(b + 1) * D].rearrange("p (c n) -> p c n", c=8))]
                    if is_kv:
                        sync.wait_ge(SEM["ac"], ck_kvev[i][-1])
                        vdst = v_sp.rearrange("(hp p) c -> p hp c", p=128)[
                            :, :, i * 130:(i + 1) * 130]
                        vsrc = v1t2[:, b * HEADS * 65:(b + 1) * HEADS * 65].rearrange(
                            "p (hp c) -> p hp c", c=130)
                        toks.append(DMA(sync, "sy", vdst, vsrc))
                    ck_spill[i] = toks

            for i in range(NT_FULL):
                p2_tile(i, True)

            @block.vector
            def _(v):
                v.memset(dacc[:], 1e30)
                vlink(v, v.memset(ind2[:], 0.0))
                vlink(v, v.memset(ind2[0:64, 0:1], 1.0))
                ins = v.memset(ind2[64:128, 1:2], 1.0)
                vlink(v, ins)
                ins2 = v.engine_nop()
                INC(ins2, "ve")
            ck_ind2 = mark("ve")

            ck_kld, ck_dtt, ck_dmm, ck_dev = {}, {}, {}, {}
            for i in range(NT_FULL, NT_FULL + NT_PROC):
                p2_tile(i, False)

                ti = i - NT_FULL
                kt = ti + 8
                b_ = i % 2

                @block.gpsimd
                def _(g, i=i, b_=b_, kt=kt):
                    WTOK(g, ck_spill[kt][0])
                    if i >= NT_FULL + 2:
                        g.wait_ge(SEM["ve"], ck_dtt[i - 2])
                    ck_kld[i] = DMA(g, "gp", kdl[:, b_ * D:(b_ + 1) * D],
                                    k_sp.rearrange("(c p) t -> p c t", p=128)[
                                        :, :, kt * 128:(kt + 1) * 128])

                @block.vector
                def _(v, i=i, b_=b_):
                    v.wait_ge(SEM["ac"], ck_evk[i])
                    WTOK(v, ck_kld[i])
                    if i >= NT_FULL + 1:
                        v.wait_ge(SEM["pe"], ck_dmm[i - 1])
                    ins = v.tensor_tensor(dprod[:], kfm2[:, b_ * D:(b_ + 1) * D],
                                          kdl[:, b_ * D:(b_ + 1) * D], ALU.mult)
                    INC(ins, "ve")
                    ck_dtt[i] = mark("ve")

                @block.tensor
                def _(t, i=i):
                    t.wait_ge(SEM["ve"], ck_dtt[i])
                    if i == NT_FULL:
                        t.wait_ge(SEM["ac"], mark("ac"))
                        t.wait_ge(SEM["ve"], ck_ind2)
                    else:
                        t.wait_ge(SEM["ve"], ck_dev[i - 1])
                    for c in range(8):
                        ins = t.matmul(PS[2][:, 2 * c:2 * c + 2],
                                       dprod[:, c * 128:(c + 1) * 128],
                                       ind2[:], start=True, stop=True)
                    INC(ins, "pe")
                    ck_dmm[i] = mark("pe")

                @block.vector
                def _(v, i=i):
                    v.wait_ge(SEM["pe"], ck_dmm[i])
                    ins = v.tensor_tensor(dacc[:], dacc[:], PS[2][:, 0:16], ALU.min)
                    vlink(v, ins)
                    ins2 = v.engine_nop()
                    INC(ins2, "ve")
                    ck_dev[i] = mark("ve")

            @block.vector
            def _(v):
                v.wait_ge(SEM["ve"], mark("ve"))
                ins = v.tensor_scalar(sqsc[:, 0:16], dacc[:], -1.0, None, ALU.mult)
                INC(ins, "ve")
            ck_bneg = mark("ve")

            @block.tensor
            def _(t):
                t.wait_ge(SEM["ve"], mark("ve"))
                t.transpose(PS[0][0:16, 0:128], nq_acc[:], ident_fs[:])
                t.transpose(PS[0][0:16, 128:256], nk_acc[:], ident_fs[:])
                ins = t.transpose(PS[0][0:16, 256:384], sqsc[:, 0:16], ident_fs[:])
                INC(ins, "pe")
            ck_bt = mark("pe")

            @block.vector
            def _(v):
                v.wait_ge(SEM["pe"], ck_bt)
                v.reduce_max(ntmp[0:16, 0:1], PS[0][0:16, 0:128].rearrange("h (o t) -> h o t", o=1), AX.X)
                vlink(v, v.reduce_max(ntmp[0:16, 1:2], PS[0][0:16, 128:256].rearrange("h (o t) -> h o t", o=1), AX.X))
                ins = v.reduce_max(ntmp[0:16, 6:7], PS[0][0:16, 256:384].rearrange("h (o t) -> h o t", o=1), AX.X)
                INC(ins, "ve")
            ck_bred = mark("ve")

            @block.scalar
            def _(s):
                s.wait_ge(SEM["ve"], ck_bred)
                ins = s.activation(ntmp[0:16, 2:4], ntmp[0:16, 0:2], AF.Sqrt)
                INC(ins, "ac")
            ck_bsq = mark("ac")

            @block.vector
            def _(v):
                v.wait_ge(SEM["ac"], ck_bsq)
                vlink(v, v.tensor_tensor(ntmp[0:16, 4:5], ntmp[0:16, 2:3], ntmp[0:16, 3:4], ALU.mult))
                vlink(v, v.tensor_scalar(ntmp[0:16, 5:6], ntmp[0:16, 4:5], -0.125, None, ALU.mult))
                vlink(v, v.tensor_scalar(ntmp[0:16, 7:8], ntmp[0:16, 6:7], 0.125, None, ALU.mult))
                vlink(v, v.tensor_scalar(ntmp[0:16, 7:8], ntmp[0:16, 7:8], -85.0, None, ALU.add))
                ins = v.tensor_tensor(ntmp[0:16, 5:6], ntmp[0:16, 5:6], ntmp[0:16, 7:8], ALU.max)
                INC(ins, "ve")
            ck_bnb = mark("ve")

            @block.tensor
            def _(t):
                t.wait_ge(SEM["ve"], ck_bnb)
                ins = t.transpose(PS[1][0:1, 0:16], ntmp[0:16, 5:6], ident_fs[0:16, 0:16])
                INC(ins, "pe")
            ck_bt2 = mark("pe")

            @block.scalar
            def _(s):
                s.wait_ge(SEM["pe"], ck_bt2)
                ins = s.copy(ntmp[0:1, 0:16], PS[1][0:1, 0:16])
                INC(ins, "ac")
            ck_bt3 = mark("ac")

            @block.tensor
            def _(t):
                t.wait_ge(SEM["ac"], ck_bt3)
                ins = t.matmul(PS[2][:, 0:16], ones_fs[:], ntmp[0:1, 0:16], start=True, stop=True)
                INC(ins, "pe")
            ck_bt4 = mark("pe")

            @block.scalar
            def _(s):
                s.wait_ge(SEM["pe"], ck_bt4)
                ins = s.copy(bmat[:], PS[2][:, 0:16])
                INC(ins, "ac")
            ck_bmat = mark("ac")

            if DEBUG:
                @block.sync
                def _(sync):
                    sync.wait_ge(SEM["ac"], ck_bmat)
                    DMA(sync, "sy", bm_dbg[:, 0:16], bmat[:])
                    DMA(sync, "sy", bm_dbg[:, 16:32], bmat[:])
                    DMA(sync, "sy", wp_dbg, qkvWp[:])
                    DMA(sync, "sy", tab_dbg[:, 0:56 * DH], Ctab[:])
                    DMA(sync, "sy", tab_dbg[:, 56 * DH:], Stab[:])


        WSCOPE.close()
        x_buf = E(nc.sbuf_tensor("x_buf", [128, NT_PROC * D], F32))
        AF_SCOPE = contextlib.ExitStack()
        attn_fm = AF_SCOPE.enter_context(nc.sbuf_tensor("attn_fm", [128, 8 * T_PROC], BF16))
        emit_barrier()

        with contextlib.ExitStack() as P3:
            E3 = P3.enter_context
            khp = E3(nc.sbuf_tensor("khp", [128, 2 * T_FULL], BF16))
            qhp = E3(nc.sbuf_tensor("qhp", [128, 2 * T_PROC], BF16))
            vhp = E3(nc.sbuf_tensor("vhp", [128, 2 * 32 * 130], BF16))
            PT = E3(nc.sbuf_tensor("PT", [128, 6 * 512], BF16))
            recs = E3(nc.sbuf_tensor("recs", [2, 2 * 512], F32))
            rbc = E3(nc.sbuf_tensor("rbc", [128, 2 * 512], F32))

            NKB = 32
            NQT = 6

            ck_kvload = {}
            bank_free_ac = {0: 0, 1: 0}
            pt_free_pe = {i: 0 for i in range(6)}
            pv_free_ve = {0: 0, 1: 0}
            recs_free_pe = 0
            ck_hp_pv_done = {}

            for hp in range(8 if PHASES >= 3 else 0):
                bb = hp % 2

                @block.sync
                def _(sync, hp=hp, bb=bb):
                    if hp >= 2:
                        sync.wait_ge(SEM["pe"], ck_hp_pv_done[hp - 2])
                    toks = [DMA(sync, "sy", khp[:, bb * T_FULL:(bb + 1) * T_FULL],
                                k_sp[hp * 128:(hp + 1) * 128, :]),
                            DMA(sync, "sy", qhp[:, bb * T_PROC:(bb + 1) * T_PROC],
                                q_sp[hp * 128:(hp + 1) * 128, :])]
                    toks.append(DMA(sync, "sy", vhp[:, bb * 32 * 130:(bb + 1) * 32 * 130],
                                    v_sp[hp * 128:(hp + 1) * 128, :]))
                    ck_kvload[hp] = toks

                for qt in range(NQT):
                    pvb = 4 + (qt % 2) * 2
                    cka_k = {}
                    sl_of = {}

                    def emit_pv(kb, qt=qt, bb=bb, pvb=pvb):
                        slA, slB = sl_of[kb]
                        w_pv = pv_free_ve[qt % 2] if kb == 0 else 0

                        @block.tensor
                        def _(t, qt=qt, kb=kb, bb=bb, slA=slA, slB=slB, pvb=pvb,
                              cka=cka_k[kb], w_pv=w_pv):
                            t.wait_ge(SEM["ac"], cka)
                            if w_pv:
                                t.wait_ge(SEM["ve"], w_pv)
                            vA = vhp[:, bb * 32 * 130 + kb * 130: bb * 32 * 130 + kb * 130 + 65]
                            vB = vhp[:, bb * 32 * 130 + kb * 130 + 65: bb * 32 * 130 + (kb + 1) * 130]
                            t.matmul(PS[pvb][0:65, 0:512], vA, PT[:, slA * 512:(slA + 1) * 512],
                                     start=(kb == 0), stop=(kb == NKB - 1))
                            ins = t.matmul(PS[pvb + 1][0:65, 0:512], vB,
                                           PT[:, slB * 512:(slB + 1) * 512],
                                           start=(kb == 0), stop=(kb == NKB - 1))
                            INC(ins, "pe")
                        pt_free_pe[slA] = mark("pe")
                        pt_free_pe[slB] = mark("pe")

                    for kb in range(NKB):
                        bp = kb % 2
                        slA = (kb % 3) * 2
                        slB = slA + 1
                        sl_of[kb] = (slA, slB)
                        w_bank = bank_free_ac[bp]
                        w_slA = pt_free_pe[slA]
                        w_slB = pt_free_pe[slB]

                        @block.tensor
                        def _(t, hp=hp, qt=qt, kb=kb, bp=bp, bb=bb,
                              w_bank=w_bank, w_slA=w_slA, w_slB=w_slB):
                            if qt == 0 and kb == 0:
                                for tk in ck_kvload[hp]:
                                    WTOK(t, tk)
                            if w_bank:
                                t.wait_ge(SEM["ac"], w_bank)
                            kB = khp[:, bb * T_FULL + kb * 128: bb * T_FULL + (kb + 1) * 128]
                            qA = qhp[0:64, bb * T_PROC + qt * 512: bb * T_PROC + (qt + 1) * 512]
                            qB = qhp[64:128, bb * T_PROC + qt * 512: bb * T_PROC + (qt + 1) * 512]
                            t.matmul(PS[bp * 2][:, 0:512], kB[0:64, :], qA, start=True, stop=True)
                            ins = t.matmul(PS[bp * 2 + 1][:, 0:512], kB[64:128, :], qB,
                                           start=True, stop=True, tile_position=(64, 0))
                            INC(ins, "pe")
                        ck_smm = mark("pe")

                        @block.scalar
                        def _(s, hp=hp, bp=bp, slA=slA, slB=slB, ck_smm=ck_smm,
                              w_slA=w_slA, w_slB=w_slB):
                            s.wait_ge(SEM["pe"], ck_smm)
                            if w_slA:
                                s.wait_ge(SEM["pe"], w_slA)
                            if w_slB:
                                s.wait_ge(SEM["pe"], w_slB)
                            s.activation(PT[:, slA * 512:(slA + 1) * 512], PS[bp * 2][:, 0:512],
                                         AF.Exp, bias=bmat[:, 2 * hp:2 * hp + 1], scale=0.125)
                            ins = s.activation(PT[:, slB * 512:(slB + 1) * 512],
                                               PS[bp * 2 + 1][:, 0:512],
                                               AF.Exp, bias=bmat[:, 2 * hp + 1:2 * hp + 2],
                                               scale=0.125)
                            INC(ins, "ac")
                        bank_free_ac[bp] = mark("ac")
                        cka_k[kb] = mark("ac")

                        if kb >= 2:
                            emit_pv(kb - 2)

                    emit_pv(NKB - 2)
                    emit_pv(NKB - 1)

                    ck_pv_last = mark("pe")
                    if qt == NQT - 1:
                        ck_hp_pv_done[hp] = ck_pv_last

                    @block.vector
                    def _(v, ck=ck_pv_last, pvb=pvb, rfp=recs_free_pe):
                        v.wait_ge(SEM["pe"], ck)
                        if rfp:
                            v.wait_ge(SEM["pe"], rfp)
                        v.reciprocal(recs[0:1, 0:512], PS[pvb][64:65, 0:512])
                        ins = v.reciprocal(recs[0:1, 512:1024], PS[pvb + 1][64:65, 0:512])
                        INC(ins, "ve")
                    ck_recip = mark("ve")

                    @block.tensor
                    def _(t, ck=ck_recip, b0=bank_free_ac[0]):
                        t.wait_ge(SEM["ve"], ck)
                        t.wait_ge(SEM["ac"], b0)
                        t.matmul(PS[0][:, 0:512], ones_fs[:], recs[0:1, 0:512],
                                 start=True, stop=True)
                        ins = t.matmul(PS[1][:, 0:512], ones_fs[:], recs[0:1, 512:1024],
                                       start=True, stop=True)
                        INC(ins, "pe")
                    recs_free_pe = mark("pe")
                    ck_bc = mark("pe")

                    @block.scalar
                    def _(s, ck=ck_bc, w0=pv_free_ve[0], w1=pv_free_ve[1]):
                        s.wait_ge(SEM["pe"], ck)
                        if w0:
                            s.wait_ge(SEM["ve"], w0)
                        if w1:
                            s.wait_ge(SEM["ve"], w1)
                        s.copy(rbc[:, 0:512], PS[0][:, 0:512])
                        ins = s.copy(rbc[:, 512:1024], PS[1][:, 0:512])
                        INC(ins, "ac")
                    bank_free_ac[0] = mark("ac")
                    ck_rbc = mark("ac")

                    @block.vector
                    def _(v, hp=hp, qt=qt, pvb=pvb, ck=ck_rbc):
                        v.wait_ge(SEM["ac"], ck)
                        dst = attn_fm[:, hp * T_PROC + qt * 512: hp * T_PROC + (qt + 1) * 512]
                        v.tensor_tensor(dst[0:64, :], PS[pvb][0:64, 0:512],
                                        rbc[0:64, 0:512], ALU.mult)
                        ins = v.tensor_tensor(dst[64:128, :], PS[pvb + 1][0:64, 0:512],
                                              rbc[64:128, 512:1024], ALU.mult)
                        INC(ins, "ve")
                    pv_free_ve[qt % 2] = mark("ve")

        emit_barrier()

        with contextlib.ExitStack() as P4:
            E4 = P4.enter_context
            projWp = E4(nc.sbuf_tensor("projWp", [128, 8 * D], BF16))
            projB_bf = E4(nc.sbuf_tensor("projB_bf", [1, D], BF16))
            with contextlib.ExitStack() as WTMP2:
                if PHASES >= 4:
                    emit_wt_transpose(proj_w, 8, projWp, D, WTMP2)

            ck_pbias = []

            @block.gpsimd
            def _(g):
                ck_pbias.append(DMA(g, "gp", projB_bf[:], proj_b))
            emit_barrier()

            xp2 = E4(nc.sbuf_tensor("xp2", [128, 2 * D], F32))
            ck_xp, ck_pmm, ck_pres = {}, {}, {}
            for t_ in range(NT_PROC if PHASES >= 4 else 0):
                bb = t_ % 2

                @block.sync
                def _(sync, t_=t_, bb=bb):
                    if t_ >= 2:
                        sync.wait_ge(SEM["ve"], ck_pres[t_ - 2])
                    ck_xp[t_] = DMA(sync, "sy", xp2[:, bb * D:(bb + 1) * D],
                                    x_proc[t_ * 128:(t_ + 1) * 128, :])

                @block.tensor
                def _(t, t_=t_, bb=bb):
                    WTOK(t, ck_pbias[0])
                    if t_ >= 2:
                        t.wait_ge(SEM["ve"], ck_pres[t_ - 2])
                    for half in range(2):
                        bank = PS[bb * 2 + half]
                        for c in range(8):
                            t.matmul(bank[:, 0:512],
                                     attn_fm[:, c * T_PROC + t_ * 128: c * T_PROC + (t_ + 1) * 128],
                                     projWp[:, c * D + half * 512: c * D + (half + 1) * 512],
                                     start=(c == 0), stop=False)
                        ins = t.matmul(bank[:, 0:512], ones_bf[:],
                                       projB_bf[:, half * 512:(half + 1) * 512],
                                       start=False, stop=True)
                    INC(ins, "pe")
                    ck_pmm[t_] = mark("pe")

                @block.vector
                def _(v, t_=t_, bb=bb):
                    v.wait_ge(SEM["pe"], ck_pmm[t_])
                    WTOK(v, ck_xp[t_])
                    for half in range(2):
                        ins = v.tensor_tensor(
                            x_buf[:, t_ * D + half * 512: t_ * D + (half + 1) * 512],
                            PS[bb * 2 + half][:, 0:512],
                            xp2[:, bb * D + half * 512: bb * D + (half + 1) * 512],
                            ALU.add)
                    INC(ins, "ve")
                    ck_pres[t_] = mark("ve")

            if DEBUG and PHASES >= 4:
                @block.sync
                def _(sync):
                    sync.wait_ge(SEM["ve"], mark("ve"))
                    DMA(sync, "sy", x1_dbg, x_buf[:])


        AF_SCOPE.close()
        swv = E(nc.sbuf_tensor("swv", [128, 2], F32))

        emit_barrier()
        with contextlib.ExitStack() as P5W:
          if PHASES >= 5:
            E5 = P5W.enter_context
            wld = E5(nc.sbuf_tensor("wld", [128, 2 * DHID], F32))
            zq = E5(nc.sbuf_tensor("zq", [128, DHID], F32))
            iq = E5(nc.sbuf_tensor("iq", [128, DHID], I32))
            wq = E5(nc.sbuf_tensor("wq", [128, DHID], F32))
            wqt = E5(nc.sbuf_tensor("wqt", [128, DHID], BF16))
            sacc = E5(nc.sbuf_tensor("sacc", [128, 4], F32))
            srow = E5(nc.sbuf_tensor("srow", [1, 128], F32))
            sred = E5(nc.sbuf_tensor("sred", [1, 4], F32))
            ck_wx, ck_q, ck_tq, ck_sp5 = {}, {}, {}, {}

            @block.vector
            def _(v):
                ins = v.memset(sacc[:], 0.0)
                INC(ins, "ve")
            ck_z5 = mark("ve")

            passA = [(w1, 32, D, 0), (w2, 8, DHID, 1)]
            ck_redA = {}
            redseq = []
            for wi, (w_ap, ntile, wcols, col) in enumerate(passA):
                for r in range(ntile):
                    @block.sync
                    def _(sync, w_ap=w_ap, r=r, wcols=wcols, wi=wi):
                        if len(redseq) >= 2:
                            sync.wait_ge(SEM["ve"], redseq[-2])
                        ck_wx[(wi, r)] = DMA(
                            sync, "sy", wld[:, (r % 2) * DHID:(r % 2) * DHID + wcols],
                            w_ap[r * 128:(r + 1) * 128, :])

                    @block.vector
                    def _(v, w_ap=w_ap, r=r, wcols=wcols, col=col, wi=wi):
                        WTOK(v, ck_wx[(wi, r)])
                        v.wait_ge(SEM["ve"], ck_z5)
                        v.wait_ge(SEM["ve"], CNT["ve"])
                        vlink(v, v.reduce_sum(sacc[:, 2:3], wld[:, (r % 2) * DHID:(r % 2) * DHID + wcols],
                                              AX.X, apply_absolute_value=True))
                        ins = v.tensor_tensor(sacc[:, col:col + 1], sacc[:, col:col + 1],
                                              sacc[:, 2:3], ALU.add)
                        INC(ins, "ve")
                        ck_redA[(wi, r)] = mark("ve")
                        redseq.append(mark("ve"))
            ck_pA = mark("ve")

            for col in range(2):
                @block.tensor
                def _(t, col=col):
                    t.wait_ge(SEM["ve"], mark("ve"))
                    ins = t.transpose(PS[0][0:1, 0:128], sacc[:, col:col + 1], ident_fs[:])
                    INC(ins, "pe")
                ck_t5 = mark("pe")

                @block.scalar
                def _(s, ck=ck_t5):
                    s.wait_ge(SEM["pe"], ck)
                    ins = s.copy(srow[:], PS[0][0:1, 0:128])
                    INC(ins, "ac")
                ck_e5 = mark("ac")

                @block.vector
                def _(v, ck=ck_e5, col=col):
                    v.wait_ge(SEM["ac"], ck)
                    n = float(DHID * D)
                    vlink(v, v.reduce_sum(sred[0:1, 0:1], srow[:].rearrange("o (a c) -> o a c", a=1), AX.X))
                    vlink(v, v.tensor_scalar(sred[0:1, 1:2], sred[0:1, 0:1], 1.0 / n, EPS,
                                             ALU.mult, ALU.max))
                    ins = v.reciprocal(sred[0:1, 2:3], sred[0:1, 1:2])
                    INC(ins, "ve")
                ck_r5 = mark("ve")

                @block.tensor
                def _(t, ck=ck_r5):
                    t.wait_ge(SEM["ve"], ck)
                    ins = t.matmul(PS[1][:, 0:1], ones_fs[:], sred[0:1, 2:3], start=True, stop=True)
                    INC(ins, "pe")
                ck_b5 = mark("pe")

                @block.scalar
                def _(s, ck=ck_b5, col=col):
                    s.wait_ge(SEM["pe"], ck)
                    s.copy(sacc[:, 2 + col:3 + col], PS[1][:, 0:1])
                    ins = s.copy(swv[:, col:col + 1], PS[1][:, 0:1])
                    INC(ins, "ac")
            ck_sw = mark("ac")

            wqt2 = E5(nc.sbuf_tensor("wqt2", [128, DHID], BF16))
            ck_zB, ck_qB, ck_tB, ck_eB, ck_spB = {}, {}, {}, {}, {}
            bank_ev = {6: 0, 7: 0}

            zBseq = []

            def wquant_tile(wi, w_ap, r, wcols, col, dst_sp):
                @block.sync
                def _(sync, wi=wi, w_ap=w_ap, r=r, wcols=wcols):
                    if len(zBseq) >= 2:
                        sync.wait_ge(SEM["ac"], zBseq[-2])
                    else:
                        sync.wait_ge(SEM["ve"], redseq[-1])
                    ck_wx[("B", wi, r)] = DMA(
                        sync, "sy", wld[:, (r % 2) * DHID:(r % 2) * DHID + wcols],
                        w_ap[r * 128:(r + 1) * 128, :])

                @block.scalar
                def _(s, wi=wi, r=r, wcols=wcols, col=col):
                    WTOK(s, ck_wx[("B", wi, r)])
                    s.wait_ge(SEM["ac"], ck_sw)
                    prev = ck_qB.get((wi, r - 1)) or ck_qB.get((1 - wi, "last"))
                    if prev:
                        s.wait_ge(SEM["ve"], prev)
                    ins = s.activation(zq[:, 0:wcols], wld[:, (r % 2) * DHID:(r % 2) * DHID + wcols],
                                       AF.Copy, scale=sacc[:, 2 + col:3 + col], bias=0.5)
                    INC(ins, "ac")
                    ck_zB[(wi, r)] = mark("ac")
                    zBseq.append(mark("ac"))

                @block.vector
                def _(v, wi=wi, r=r, wcols=wcols):
                    v.wait_ge(SEM["ac"], ck_zB[(wi, r)])
                    prevt = ck_tB.get((wi, r - 1))
                    if prevt:
                        v.wait_ge(SEM["pe"], prevt)
                    vlink(v, v.tensor_copy(iq[:, 0:wcols], zq[:, 0:wcols]))
                    vlink(v, v.tensor_copy(wq[:, 0:wcols], iq[:, 0:wcols]))
                    vlink(v, v.tensor_tensor(zq[:, 0:wcols], zq[:, 0:wcols], wq[:, 0:wcols], ALU.subtract))
                    vlink(v, v.scalar_tensor_tensor(wq[:, 0:wcols], zq[:, 0:wcols], 1.0,
                                                    wq[:, 0:wcols], ALU.is_ge, ALU.add))
                    vlink(v, v.scalar_tensor_tensor(wq[:, 0:wcols], zq[:, 0:wcols], 0.0,
                                                    wq[:, 0:wcols], ALU.is_ge, ALU.add))
                    vlink(v, v.tensor_scalar(wq[:, 0:wcols], wq[:, 0:wcols], 1.0, None, ALU.subtract))
                    ins = v.tensor_scalar(wqt[:, 0:wcols], wq[:, 0:wcols], -1.0, 1.0, ALU.max, ALU.min)
                    INC(ins, "ve")
                    ck_qB[(wi, r)] = mark("ve")
                    ck_qB[(wi, "last")] = mark("ve")

                nblk = wcols // 128
                for g in range(nblk // 8):
                    bk = 6 + (g % 2)

                    @block.tensor
                    def _(t, wi=wi, r=r, g=g, bk=bk):
                        t.wait_ge(SEM["ve"], ck_qB[(wi, r)])
                        if bank_ev[bk]:
                            t.wait_ge(SEM["ac"], bank_ev[bk])
                        bank = psbf(bk)
                        for j in range(8):
                            c = g * 8 + j
                            ins = t.transpose(bank[:, j * 128:(j + 1) * 128],
                                              wqt[:, c * 128:(c + 1) * 128], ident_bf[:])
                        INC(ins, "pe")
                        ck_tB[(wi, r, g)] = mark("pe")
                        ck_tB[(wi, r)] = mark("pe")

                    @block.scalar
                    def _(s, wi=wi, r=r, g=g, bk=bk):
                        s.wait_ge(SEM["pe"], ck_tB[(wi, r, g)])
                        for tk in ck_spB.get("last", []):
                            WTOK(s, tk)
                        ins = s.copy(wqt2[:, g * 1024:(g + 1) * 1024], psbf(bk))
                        INC(ins, "ac")
                        ck_eB[(wi, r, g)] = mark("ac")
                    bank_ev[bk] = mark("ac")

                @block.sync
                def _(sync, wi=wi, r=r, nblk=nblk, dst_sp=dst_sp):
                    sync.wait_ge(SEM["ac"], ck_eB[(wi, r, nblk // 8 - 1)])
                    toks = []
                    for g in range(nblk // 8):
                        dst = dst_sp.rearrange("(c p) n -> p c n", p=128)[
                            :, g * 8:(g + 1) * 8, r * 128:(r + 1) * 128]
                        src = wqt2[:, g * 1024:(g + 1) * 1024].rearrange("p (c n) -> p c n", c=8)
                        toks.append(DMA(sync, "sy", dst, src))
                    ck_spB[(wi, r)] = toks[-1]
                    ck_spB["last"] = toks

            for r in range(32):
                wquant_tile(0, w1, r, D, 0, w1q_sp)
            for r in range(8):
                wquant_tile(1, w2, r, DHID, 1, w2q_sp)

        shtab_g = E(nc.sbuf_tensor("shtab_g", [128, 32], F32))

        def emit_ffn(tiles, tag):
            emit_barrier()
            with contextlib.ExitStack() as PF:
                EF = PF.enter_context
                gamr = EF(nc.sbuf_tensor("gamr_%s" % tag, [128, D], BF16))
                betr = EF(nc.sbuf_tensor("betr_%s" % tag, [128, D], BF16))
                b1r = EF(nc.sbuf_tensor("b1r_%s" % tag, [1, DHID], BF16))
                stat = EF(nc.sbuf_tensor("stat_%s" % tag, [128, 2 * 16], F32))
                yf = EF(nc.sbuf_tensor("yf_%s" % tag, [128, D], F32))
                sqs = yf
                strow = EF(nc.sbuf_tensor("strow_%s" % tag, [1, 256], BF16))
                xqb = EF(nc.sbuf_tensor("xqb_%s" % tag, [128, D], BF16))
                xqf = EF(nc.sbuf_tensor("xqf_%s" % tag, [128, D], BF16))
                hbf = EF(nc.sbuf_tensor("hbf_%s" % tag, [128, DHID], F32))
                zp = EF(nc.sbuf_tensor("zp_%s" % tag, [128, 512], F32))
                ip_ = EF(nc.sbuf_tensor("ip_%s" % tag, [128, 512], I32))
                wp_ = EF(nc.sbuf_tensor("wp_%s" % tag, [128, 512], F32))
                w1qr = EF(nc.sbuf_tensor("w1qr_%s" % tag, [128, 8 * DHID], BF16))

                tok_c = []

                @block.sync
                def _(sync):
                    for c in range(8):
                        tok_c.append(DMA(sync, "sy", w1qr[:, c * DHID:(c + 1) * DHID],
                                         w1q_sp[c * 128:(c + 1) * 128, :]))

                @block.gpsimd
                def _(g):
                    tok_c.append(DMA(g, "gp", gamr[:], gam_rep))
                    tok_c.append(DMA(g, "gp", betr[:], bet_rep))
                    tok_c.append(DMA(g, "gp", b1r[:], b1))

                ck_st5, ck_sq5, ck_ch5, ck_s25 = {}, {}, {}, {}
                ck_sx5, ck_qx5, ck_tx5, ck_ex5, ck_zx5 = {}, {}, {}, {}, {}
                ck_m15, ck_g5, ck_hm5, ck_sp5 = {}, {}, {}, {}

                for ti, t_ in enumerate(tiles):
                    bb = ti % 2
                    st = stat[:, bb * 16:(bb + 1) * 16]
                    xa = x_buf[:, t_ * D:(t_ + 1) * D]

                    @block.vector
                    def _(v, ti=ti, t_=t_, st=st, xa=xa):
                        if ti >= 2:
                            v.wait_ge(SEM["ve"], ck_ch5[ti - 2])
                        if ti == 0:
                            for tk in tok_c:
                                WTOK(v, tk)
                        ins = v.reduce_sum(st[:, 0:1], xa, AX.X)
                        INC(ins, "ve")
                        ck_st5[ti] = mark("ve")

                    @block.scalar
                    def _(s, ti=ti, st=st, xa=xa):
                        s.wait_ge(SEM["ve"], ck_st5[ti])
                        ins = s.activation(sqs[:], xa, AF.Square, accum_out=st[:, 1:2])
                        INC(ins, "ac")
                        ck_sq5[ti] = mark("ac")

                    @block.vector
                    def _(v, ti=ti, st=st):
                        v.wait_ge(SEM["ac"], ck_sq5[ti])
                        v.tensor_scalar(st[:, 2:3], st[:, 0:1], 1.0 / D, None, ALU.mult)
                        vlink(v, v.tensor_scalar(st[:, 3:4], st[:, 1:2], 1.0 / D, None, ALU.mult))
                        vlink(v, v.tensor_tensor(st[:, 4:5], st[:, 2:3], st[:, 2:3], ALU.mult))
                        vlink(v, v.tensor_tensor(st[:, 5:6], st[:, 3:4], st[:, 4:5], ALU.subtract))
                        vlink(v, v.tensor_scalar(st[:, 6:7], st[:, 5:6], EPS, None, ALU.add))
                        vlink(v, v.reciprocal(st[:, 7:8], st[:, 6:7]))
                        vlink(v, v.tensor_tensor(st[:, 8:9], st[:, 5:6], st[:, 7:8], ALU.mult))
                        vlink(v, v.tensor_scalar(st[:, 9:10], st[:, 8:9], EPS, None, ALU.add))
                        ins = v.reciprocal(st[:, 10:11], st[:, 9:10])
                        INC(ins, "ve")
                        ck_ch5[ti] = mark("ve")

                    @block.scalar
                    def _(s, ti=ti, st=st):
                        s.wait_ge(SEM["ve"], ck_ch5[ti])
                        s.activation(st[:, 11:12], st[:, 7:8], AF.Sqrt)
                        ins = s.activation(st[:, 12:13], st[:, 10:11], AF.Sqrt)
                        INC(ins, "ac")
                        ck_s25[ti] = mark("ac")

                    @block.vector
                    def _(v, ti=ti, st=st, xa=xa, bb=bb):
                        v.wait_ge(SEM["ac"], ck_s25[ti])
                        if ti >= 1:
                            v.wait_ge(SEM["ac"], ck_zx5[(ti - 1, 1)])
                        vlink(v, v.tensor_tensor(st[:, 13:14], st[:, 11:12], st[:, 12:13], ALU.mult))
                        vlink(v, v.tensor_scalar(yf[:], xa, st[:, 2:3], st[:, 13:14],
                                                 ALU.subtract, ALU.mult))
                        vlink(v, v.tensor_tensor(yf[:], yf[:], gamr[:], ALU.mult))
                        vlink(v, v.tensor_tensor(yf[:], yf[:], betr[:], ALU.add))
                        vlink(v, v.reduce_max(st[:, 14:15], yf[:],
                                              AX.X, apply_absolute_value=True))
                        vlink(v, v.tensor_scalar(st[:, 14:15], st[:, 14:15], EPS, None, ALU.max))
                        vlink(v, v.reciprocal(st[:, 15:16], st[:, 14:15]))
                        vlink(v, v.tensor_scalar(st[:, 15:16], st[:, 15:16], 127.0, None, ALU.mult))
                        vlink(v, v.tensor_tensor(st[:, 4:5], st[:, 15:16], swv[:, 0:1], ALU.mult))
                        ins = v.reciprocal(st[:, 5:6], st[:, 4:5])
                        INC(ins, "ve")
                        ck_sx5[ti] = mark("ve")

                    for sub in range(2):
                        @block.scalar
                        def _(s, ti=ti, st=st, sub=sub):
                            s.wait_ge(SEM["ve"], ck_sx5[ti])
                            if sub == 1:
                                s.wait_ge(SEM["ve"], ck_qx5[(ti, 0)])
                            elif ti >= 1:
                                s.wait_ge(SEM["ve"], ck_qx5[(ti - 1, 1)])
                            ins = s.activation(zp[:], yf[:, sub * 512:(sub + 1) * 512],
                                               AF.Copy, scale=st[:, 15:16], bias=0.5)
                            INC(ins, "ac")
                            ck_zx5[(ti, sub)] = mark("ac")

                        @block.vector
                        def _(v, ti=ti, sub=sub):
                            v.wait_ge(SEM["ac"], ck_zx5[(ti, sub)])
                            if sub == 0 and ti >= 1:
                                v.wait_ge(SEM["pe"], ck_tx5[ti - 1])
                            vlink(v, v.tensor_copy(ip_[:], zp[:]))
                            vlink(v, v.tensor_copy(wp_[:], ip_[:]))
                            vlink(v, v.tensor_tensor(zp[:], zp[:], wp_[:], ALU.subtract))
                            vlink(v, v.scalar_tensor_tensor(wp_[:], zp[:], 1.0, wp_[:], ALU.is_ge, ALU.add))
                            vlink(v, v.scalar_tensor_tensor(wp_[:], zp[:], 0.0, wp_[:], ALU.is_ge, ALU.add))
                            ins = v.tensor_scalar(xqb[:, sub * 512:(sub + 1) * 512], wp_[:],
                                                  1.0, None, ALU.subtract)
                            INC(ins, "ve")
                            ck_qx5[(ti, sub)] = mark("ve")

                    @block.tensor
                    def _(t, ti=ti, st=st, bb=bb):
                        t.wait_ge(SEM["ve"], ck_qx5[(ti, 1)])
                        if ti >= 2:
                            t.wait_ge(SEM["ac"], ck_ex5[ti - 2])
                        bank = psbf(6 + (ti % 2))
                        for c in range(8):
                            ins = t.transpose(bank[:, c * 128:(c + 1) * 128],
                                              xqb[:, c * 128:(c + 1) * 128],
                                              ident_bf[:])
                        ins = t.transpose(PS[5][0:1, (ti % 2) * 128:(ti % 2) * 128 + 128],
                                          st[:, 4:5], ident_fs[:])
                        INC(ins, "pe")
                        ck_tx5[ti] = mark("pe")

                    @block.scalar
                    def _(s, ti=ti, bb=bb):
                        s.wait_ge(SEM["pe"], ck_tx5[ti])
                        if ti >= 1:
                            s.wait_ge(SEM["pe"], ck_m15[ti - 1][-1])
                        s.copy(xqf[:], psbf(6 + (ti % 2)))
                        ins = s.copy(strow[0:1, (ti % 2) * 128:(ti % 2) * 128 + 128],
                                     PS[5][0:1, (ti % 2) * 128:(ti % 2) * 128 + 128])
                        INC(ins, "ac")
                        ck_ex5[ti] = mark("ac")

                    ck_m15[ti] = []
                    for sl in range(8):
                        bank = PS[sl % 4]

                        @block.tensor
                        def _(t, ti=ti, sl=sl, bb=bb, bank=bank):
                            t.wait_ge(SEM["ac"], ck_ex5[ti])
                            if ti >= 1 and sl < len(ck_g5.get(ti - 1, [])):
                                t.wait_ge(SEM["ac"], ck_g5[ti - 1][sl])
                            if sl >= 4:
                                t.wait_ge(SEM["ac"], ck_g5[ti][sl - 4])
                            for c in range(8):
                                t.matmul(bank[:, 0:512],
                                         xqf[:, c * 128:(c + 1) * 128],
                                         w1qr[:, c * DHID + sl * 512: c * DHID + (sl + 1) * 512],
                                         start=(c == 0), stop=False)
                            ins = t.matmul(bank[:, 0:512],
                                           strow[0:1, (ti % 2) * 128:(ti % 2) * 128 + 128],
                                           b1r[:, sl * 512:(sl + 1) * 512], start=False, stop=True)
                            INC(ins, "pe")
                            ck_m15[ti].append(mark("pe"))

                        ck_g5.setdefault(ti, [])

                        @block.scalar
                        def _(s, ti=ti, sl=sl, bb=bb, bank=bank, st=st):
                            s.wait_ge(SEM["pe"], ck_m15[ti][sl])
                            if ti >= 1 and sl == 0:
                                WTOK(s, ck_sp5[ti - 1])
                                s.wait_ge(SEM["ve"], ck_hm5[ti - 1])
                            ins = s.activation(hbf[:, sl * 512:(sl + 1) * 512],
                                               bank[:, 0:512], AF.Gelu, scale=st[:, 5:6])
                            INC(ins, "ac")
                            ck_g5[ti].append(mark("ac"))

                    @block.vector
                    def _(v, ti=ti):
                        v.wait_ge(SEM["ac"], ck_g5[ti][7])
                        ins = v.reduce_max(shtab_g[:, ti:ti + 1], hbf[:],
                                           AX.X, apply_absolute_value=True)
                        INC(ins, "ve")
                        ck_hm5[ti] = mark("ve")

                    @block.sync
                    def _(sync, ti=ti, t_=t_):
                        sync.wait_ge(SEM["ac"], ck_g5[ti][7])
                        ck_sp5[ti] = DMA(sync, "sy",
                                         hq_sp.rearrange("(n p) c -> p n c", p=128)[:, t_, :],
                                         hbf[:])

            emit_barrier()
            with contextlib.ExitStack() as PF2:
                EF = PF2.enter_context
                w2qr = EF(nc.sbuf_tensor("w2qr_%s" % tag, [128, 32 * D], BF16))
                b2r2 = EF(nc.sbuf_tensor("b2r2_%s" % tag, [1, D], BF16))
                hld = EF(nc.sbuf_tensor("hld_%s" % tag, [128, 2 * D], F32))
                hqq = EF(nc.sbuf_tensor("hqq_%s" % tag, [128, D], BF16))
                hqi = EF(nc.sbuf_tensor("hqi2_%s" % tag, [128, DHID], BF16))
                zp = EF(nc.sbuf_tensor("zp2_%s" % tag, [128, 512], F32))
                ip_ = EF(nc.sbuf_tensor("ip2_%s" % tag, [128, 512], I32))
                wp_ = EF(nc.sbuf_tensor("wp2_%s" % tag, [128, 512], F32))
                st2 = EF(nc.sbuf_tensor("st2_%s" % tag, [128, 8], F32))
                strow2 = EF(nc.sbuf_tensor("strow2_%s" % tag, [1, 256], BF16))
                tok_c2 = []

                @block.sync
                def _(sync):
                    for c in range(32):
                        tok_c2.append(DMA(sync, "sy", w2qr[:, c * D:(c + 1) * D],
                                          w2q_sp[c * 128:(c + 1) * 128, :]))

                @block.gpsimd
                def _(g):
                    tok_c2.append(DMA(g, "gp", b2r2[:], b2))

                ck_hx, ck_m25, ck_r25 = {}, {}, {}
                ck_ah2, ck_zq2, ck_vq2, ck_tq2b, ck_eq2 = {}, {}, {}, {}, {}
                def emit_qload(ti, t_, qtr):
                    @block.gpsimd
                    def _(g, ti=ti, t_=t_, qtr=qtr):
                        prev = ck_zq2.get((ti, qtr - 2, 1)) if qtr >= 2 else (
                            ck_zq2.get((ti - 1, qtr + 2, 1)) if ti >= 1 else None)
                        if prev:
                            g.wait_ge(SEM["ac"], prev)
                        ck_hx[(ti, qtr)] = DMA(
                            g, "gp", hld[:, (qtr % 2) * D:(qtr % 2 + 1) * D],
                            hq_sp.rearrange("(n p) c -> p n c", p=128)[
                                :, t_, qtr * D:(qtr + 1) * D])

                for ti, t_ in enumerate(tiles):
                    bb = ti % 2
                    emit_qload(ti, t_, 0)
                    emit_qload(ti, t_, 1)

                    @block.vector
                    def _(v, ti=ti, bb=bb):
                        if ti >= 2:
                            v.wait_ge(SEM["ve"], ck_r25[ti - 2])
                        if ti == 0:
                            for tk in tok_c2:
                                WTOK(v, tk)
                        vlink(v, v.tensor_copy(st2[:, bb * 4:bb * 4 + 1], shtab_g[:, ti:ti + 1]))
                        vlink(v, v.tensor_scalar(st2[:, bb * 4:bb * 4 + 1], st2[:, bb * 4:bb * 4 + 1],
                                                 EPS, None, ALU.max))
                        vlink(v, v.reciprocal(st2[:, bb * 4 + 1:bb * 4 + 2], st2[:, bb * 4:bb * 4 + 1]))
                        vlink(v, v.tensor_scalar(st2[:, bb * 4 + 1:bb * 4 + 2], st2[:, bb * 4 + 1:bb * 4 + 2],
                                                 127.0, None, ALU.mult))
                        vlink(v, v.tensor_tensor(st2[:, bb * 4 + 2:bb * 4 + 3],
                                                 st2[:, bb * 4 + 1:bb * 4 + 2], swv[:, 1:2], ALU.mult))
                        ins = v.reciprocal(st2[:, bb * 4 + 3:bb * 4 + 4], st2[:, bb * 4 + 2:bb * 4 + 3])
                        INC(ins, "ve")
                        ck_ah2[ti] = mark("ve")

                    for qtr in range(4):
                        for sub in range(2):
                            @block.scalar
                            def _(s, ti=ti, bb=bb, qtr=qtr, sub=sub):
                                s.wait_ge(SEM["ve"], ck_ah2[ti])
                                WTOK(s, ck_hx[(ti, qtr)])
                                prev = ck_vq2.get((ti, qtr, sub - 1)) if sub > 0 else (
                                    ck_vq2.get((ti, qtr - 1, 1)) if qtr > 0 else (
                                        ck_vq2.get((ti - 1, 3, 1)) if ti >= 1 else None))
                                if prev:
                                    s.wait_ge(SEM["ve"], prev)
                                ins = s.activation(
                                    zp[:],
                                    hld[:, (qtr % 2) * D + sub * 512: (qtr % 2) * D + (sub + 1) * 512],
                                    AF.Copy, scale=st2[:, bb * 4 + 1:bb * 4 + 2], bias=0.5)
                                INC(ins, "ac")
                                ck_zq2[(ti, qtr, sub)] = mark("ac")

                            @block.vector
                            def _(v, ti=ti, bb=bb, qtr=qtr, sub=sub):
                                v.wait_ge(SEM["ac"], ck_zq2[(ti, qtr, sub)])
                                if sub == 0:
                                    prev = ck_tq2b.get((ti, qtr - 1)) if qtr > 0 else (
                                        ck_tq2b.get((ti - 1, 3)) if ti >= 1 else None)
                                    if prev:
                                        v.wait_ge(SEM["pe"], prev)
                                vlink(v, v.tensor_copy(ip_[:], zp[:]))
                                vlink(v, v.tensor_copy(wp_[:], ip_[:]))
                                vlink(v, v.tensor_tensor(zp[:], zp[:], wp_[:], ALU.subtract))
                                vlink(v, v.scalar_tensor_tensor(wp_[:], zp[:], 1.0, wp_[:], ALU.is_ge, ALU.add))
                                vlink(v, v.scalar_tensor_tensor(wp_[:], zp[:], 0.0, wp_[:], ALU.is_ge, ALU.add))
                                ins = v.tensor_scalar(hqq[:, sub * 512:(sub + 1) * 512],
                                                      wp_[:], 1.0, None, ALU.subtract)
                                INC(ins, "ve")
                                ck_vq2[(ti, qtr, sub)] = mark("ve")

                        @block.tensor
                        def _(t, ti=ti, bb=bb, g=qtr):
                            t.wait_ge(SEM["ve"], ck_vq2[(ti, g, 1)])
                            prev = ck_eq2.get((ti, g - 2)) if g >= 2 else (
                                ck_eq2.get((ti - 1, 3)) if ti >= 1 else None)
                            if prev:
                                t.wait_ge(SEM["ac"], prev)
                            bank = psbf(6 + (g % 2))
                            for j in range(8):
                                ins = t.transpose(bank[:, j * 128:(j + 1) * 128],
                                                  hqq[:, j * 128:(j + 1) * 128],
                                                  ident_bf[:])
                            if g == 0:
                                ins = t.transpose(PS[5][0:1, bb * 128:bb * 128 + 128],
                                                  st2[:, bb * 4 + 2:bb * 4 + 3], ident_fs[:])
                            INC(ins, "pe")
                            ck_tq2b[(ti, g)] = mark("pe")

                        @block.scalar
                        def _(s, ti=ti, g=qtr):
                            s.wait_ge(SEM["pe"], ck_tq2b[(ti, g)])
                            if ti >= 1 and g == 0:
                                s.wait_ge(SEM["pe"], ck_m25[ti - 1][-1])
                            ins = s.copy(hqi[:, g * 1024:(g + 1) * 1024], psbf(6 + (g % 2)))
                            if g == 0:
                                ins = s.copy(strow2[0:1, (ti % 2) * 128:(ti % 2) * 128 + 128],
                                             PS[5][0:1, (ti % 2) * 128:(ti % 2) * 128 + 128])
                            INC(ins, "ac")
                            ck_eq2[(ti, g)] = mark("ac")

                        if qtr + 2 < 4:
                            emit_qload(ti, t_, qtr + 2)

                    ck_m25[ti] = []
                    for half in range(2):
                        @block.tensor
                        def _(t, ti=ti, half=half, bb=bb):
                            t.wait_ge(SEM["ac"], ck_eq2[(ti, 3)])
                            if ti >= 2:
                                t.wait_ge(SEM["ve"], ck_r25[ti - 2])
                            bank = PS[bb * 2 + half]
                            for c in range(32):
                                t.matmul(bank[:, 0:512],
                                         hqi[:, c * 128:(c + 1) * 128],
                                         w2qr[:, c * D + half * 512: c * D + (half + 1) * 512],
                                         start=(c == 0), stop=False)
                            ins = t.matmul(bank[:, 0:512],
                                           strow2[0:1, (ti % 2) * 128:(ti % 2) * 128 + 128],
                                           b2r2[:, half * 512:(half + 1) * 512], start=False, stop=True)
                            INC(ins, "pe")
                            ck_m25[ti].append(mark("pe"))

                    @block.vector
                    def _(v, ti=ti, t_=t_, bb=bb):
                        v.wait_ge(SEM["pe"], ck_m25[ti][-1])
                        for half in range(2):
                            ins = v.scalar_tensor_tensor(
                                x_buf[:, t_ * D + half * 512: t_ * D + (half + 1) * 512],
                                PS[bb * 2 + half][:, 0:512],
                                st2[:, bb * 4 + 3:bb * 4 + 4],
                                x_buf[:, t_ * D + half * 512: t_ * D + (half + 1) * 512],
                                ALU.mult, ALU.add)
                        INC(ins, "ve")
                        ck_r25[ti] = mark("ve")

        if PHASES >= 5:
            emit_ffn(list(range(NT_PROC)), "f1")

        if DEBUG and 5 <= PHASES <= 6:
            @block.sync
            def _(sync):
                for sn, vv in CNT.items():
                    if vv:
                        sync.wait_ge(SEM[sn], vv)
                DMA(sync, "sy", x1_dbg, x_buf[:])


        emit_barrier()
        with contextlib.ExitStack() as P6S:
          if PHASES >= 6:
            E6 = P6S.enter_context
            bQKs = E6(nc.sbuf_tensor("bQKs", [128, 8 * 512], BF16))
            bVs = E6(nc.sbuf_tensor("bVs", [128, 8 * 512], BF16))
            bQKl = E6(nc.sbuf_tensor("bQKl", [128, 8 * 2048], BF16))
            bVl = E6(nc.sbuf_tensor("bVl", [128, 8 * 512], BF16))
            xt6 = E6(nc.sbuf_tensor("xt6", [128, D], BF16))
            x6f = E6(nc.sbuf_tensor("x6f", [128, D], BF16))
            st6 = E6(nc.sbuf_tensor("st6", [128, 16], F32))
            sq6 = E6(nc.sbuf_tensor("sq6", [128, 2560], F32))
            tmS = E6(nc.sbuf_tensor("tmS", [128, 512], BF16))
            tmL = E6(nc.sbuf_tensor("tmL", [128, 2048], BF16))
            vs65 = E6(nc.sbuf_tensor("vs65", [128, 520], BF16))
            vl65 = E6(nc.sbuf_tensor("vl65", [128, 520], BF16))
            fmst = E6(nc.sbuf_tensor("fmst", [128, 2560], BF16))
            ntm6 = E6(nc.sbuf_tensor("ntm6", [128, 16], F32))
            nacc = E6(nc.sbuf_tensor("nacc", [128, 32], F32))

            tok_w6 = []

            @block.gpsimd
            def _(g):
                for c in range(8):
                    tok_w6.append(DMA(g, "gp", bQKs[:, c * 512:(c + 1) * 512],
                                      bWqk_s[c * 128:(c + 1) * 128, :]))
                    tok_w6.append(DMA(g, "gp", bVs[:, c * 512:(c + 1) * 512],
                                      bWv_s[c * 128:(c + 1) * 128, :]))
                    tok_w6.append(DMA(g, "gp", bQKl[:, c * 2048:(c + 1) * 2048],
                                      bWqk_l[c * 128:(c + 1) * 128, :]))
                    tok_w6.append(DMA(g, "gp", bVl[:, c * 512:(c + 1) * 512],
                                      bWv_l[c * 128:(c + 1) * 128, :]))

            @block.vector
            def _(v):
                v.memset(nacc[:], 0.0)
                v.memset(vs65[:].rearrange("p (h c) -> p h c", c=65)[:, :, 64:65], 1.0)
                ins = v.memset(vl65[:].rearrange("p (h c) -> p h c", c=65)[:, :, 64:65], 1.0)
                INC(ins, "ve")
            ck_z6 = mark("ve")

            ck6 = {}
            for i in range(NT_PROC):
                own = i >= 8
                xa = x_buf[:, i * D:(i + 1) * D]

                @block.vector
                def _(v, i=i, xa=xa):
                    if i >= 1:
                        v.wait_ge(SEM["ve"], ck6[("ch", i - 1)])
                    ins = v.reduce_sum(st6[:, 0:1], xa, AX.X)
                    INC(ins, "ve")
                    ck6[("rs", i)] = mark("ve")

                @block.scalar
                def _(s, i=i, xa=xa):
                    s.wait_ge(SEM["ve"], ck6[("rs", i)])
                    ins = s.activation(sq6[:, 0:D], xa, AF.Square,
                                       accum_out=st6[:, 1:2])
                    INC(ins, "ac")
                    ck6[("sq", i)] = mark("ac")

                @block.vector
                def _(v, i=i):
                    v.wait_ge(SEM["ac"], ck6[("sq", i)])
                    v.tensor_scalar(st6[:, 2:3], st6[:, 0:1], 1.0 / D, None, ALU.mult)
                    vlink(v, v.tensor_scalar(st6[:, 3:4], st6[:, 1:2], 1.0 / D, None, ALU.mult))
                    vlink(v, v.tensor_tensor(st6[:, 4:5], st6[:, 2:3], st6[:, 2:3], ALU.mult))
                    vlink(v, v.tensor_tensor(st6[:, 5:6], st6[:, 3:4], st6[:, 4:5], ALU.subtract))
                    vlink(v, v.tensor_scalar(st6[:, 6:7], st6[:, 5:6], EPS, None, ALU.add))
                    ins = v.reciprocal(st6[:, 7:8], st6[:, 6:7])
                    INC(ins, "ve")
                    ck6[("ch", i)] = mark("ve")

                @block.scalar
                def _(s, i=i):
                    s.wait_ge(SEM["ve"], ck6[("ch", i)])
                    ins = s.activation(st6[:, 8:9], st6[:, 7:8], AF.Sqrt)
                    INC(ins, "ac")
                    ck6[("rst", i)] = mark("ac")

                @block.vector
                def _(v, i=i, xa=xa):
                    v.wait_ge(SEM["ac"], ck6[("rst", i)])
                    if i >= 1:
                        v.wait_ge(SEM["pe"], ck6[("tpx", i - 1)])
                    ins = v.tensor_scalar(xt6[:], xa, st6[:, 2:3], st6[:, 8:9],
                                          ALU.subtract, ALU.mult)
                    INC(ins, "ve")
                    ck6[("ln", i)] = mark("ve")

                @block.tensor
                def _(t, i=i):
                    t.wait_ge(SEM["ve"], ck6[("ln", i)])
                    if i >= 1:
                        t.wait_ge(SEM["ac"], ck6[("evx", i - 1)])
                    bank = psbf(6)
                    for c in range(8):
                        ins = t.transpose(bank[:, c * 128:(c + 1) * 128],
                                          xt6[:, c * 128:(c + 1) * 128], ident_bf[:])
                    INC(ins, "pe")
                    ck6[("tpx", i)] = mark("pe")

                @block.scalar
                def _(s, i=i):
                    s.wait_ge(SEM["pe"], ck6[("tpx", i)])
                    if i >= 1:
                        s.wait_ge(SEM["pe"], ck6[("mm", i - 1)][-1])
                    ins = s.copy(x6f[:], psbf(6))
                    INC(ins, "ac")
                    ck6[("evx", i)] = mark("ac")

                slmap = [("qks", bQKs, 0, 512), ("vs", bVs, 0, 512),
                         ("ql0", bQKl, 0, 512), ("ql1", bQKl, 512, 512),
                         ("kl0", bQKl, 1024, 512), ("kl1", bQKl, 1536, 512),
                         ("vl", bVl, 0, 512)]
                ck6[("mm", i)] = []
                ck6[("ev", i)] = []
                for sn, (nm, wt, off, wid) in enumerate(slmap):
                    bank = PS[sn % 4]

                    @block.tensor
                    def _(t, i=i, sn=sn, wt=wt, off=off, bank=bank):
                        t.wait_ge(SEM["ac"], ck6[("evx", i)])
                        if i == 0 and sn == 0:
                            for tk in tok_w6:
                                WTOK(t, tk)
                        if sn >= 4:
                            t.wait_ge(SEM["ac"], ck6[("ev", i)][sn - 4])
                        elif i >= 1:
                            prev_user = {0: 4, 1: 5, 2: 6, 3: 3}[sn]
                            t.wait_ge(SEM["ac"], ck6[("ev", i - 1)][prev_user])
                        for c in range(8):
                            base = c * (2048 if wt is bQKl else 512)
                            ins = t.matmul(bank[:, 0:512], x6f[:, c * 128:(c + 1) * 128],
                                           wt[:, base + off: base + off + 512],
                                           start=(c == 0), stop=(c == 7))
                        INC(ins, "pe")
                        ck6[("mm", i)].append(mark("pe"))

                    @block.scalar
                    def _(s, i=i, sn=sn, nm=nm, bank=bank):
                        s.wait_ge(SEM["pe"], ck6[("mm", i)][sn])
                        if i >= 1 and sn == 0:
                            s.wait_ge(SEM["pe"], ck6[("tpf", i - 1)][-1])
                            for t2 in ck6[("spl", i - 1)]:
                                WTOK(s, t2)
                        if nm == "qks":
                            ins = s.copy(tmS[:], bank[:, 0:512])
                        elif nm == "vs":
                            ins = s.copy(vs65[:].rearrange("p (h c) -> p h c", c=65)[:, :, 0:64],
                                         bank[:, 0:512].rearrange("p (h c) -> p h c", h=8))
                        elif nm.startswith("ql"):
                            ins = s.copy(tmL[:, int(nm[2]) * 512:(int(nm[2]) + 1) * 512], bank[:, 0:512])
                        elif nm.startswith("kl"):
                            ins = s.copy(tmL[:, 1024 + int(nm[2]) * 512: 1024 + (int(nm[2]) + 1) * 512],
                                         bank[:, 0:512])
                        else:
                            ins = s.copy(vl65[:].rearrange("p (h c) -> p h c", c=65)[:, :, 0:64],
                                         bank[:, 0:512].rearrange("p (h c) -> p h c", h=8))
                        INC(ins, "ac")
                        ck6[("ev", i)].append(mark("ac"))

                @block.scalar
                def _(s, i=i, own=own):
                    s.wait_ge(SEM["ac"], ck6[("ev", i)][6])
                    s.activation(sq6[:, 0:512], tmS[:], AF.Square)
                    ins = s.activation(sq6[:, 512:2560], tmL[:], AF.Square)
                    INC(ins, "ac")
                    ck6[("nsq", i)] = mark("ac")

                @block.vector
                def _(v, i=i, own=own):
                    v.wait_ge(SEM["ac"], ck6[("nsq", i)])
                    if own:
                        vlink(v, v.reduce_sum(ntm6[:, 0:8], sq6[:, 0:256].rearrange(
                            "p (h c) -> p h c", h=8), AX.X))
                        vlink(v, v.tensor_tensor(nacc[:, 0:8], nacc[:, 0:8], ntm6[:, 0:8], ALU.max))
                        vlink(v, v.reduce_sum(ntm6[:, 0:8], sq6[:, 512:1536].rearrange(
                            "p (h c) -> p h c", h=8), AX.X))
                        vlink(v, v.tensor_tensor(nacc[:, 16:24], nacc[:, 16:24], ntm6[:, 0:8], ALU.max))
                    vlink(v, v.reduce_sum(ntm6[:, 0:8], sq6[:, 256:512].rearrange(
                        "p (h c) -> p h c", h=8), AX.X))
                    vlink(v, v.tensor_tensor(nacc[:, 8:16], nacc[:, 8:16], ntm6[:, 0:8], ALU.max))
                    vlink(v, v.reduce_sum(ntm6[:, 0:8], sq6[:, 1536:2560].rearrange(
                        "p (h c) -> p h c", h=8), AX.X))
                    ins = v.tensor_tensor(nacc[:, 24:32], nacc[:, 24:32], ntm6[:, 0:8], ALU.max)
                    INC(ins, "ve")
                    ck6[("nrm", i)] = mark("ve")

                nt_list = [("ks", tmS, 256, 2, 0)] + ([("qs", tmS, 0, 2, 256)] if own else [])
                nt_list += [("kl", tmL, 1024, 8, 512)] + ([("ql", tmL, 0, 8, 2560 - 2048)] if own else [])
                fmoff = {"ks": 0, "qs": 256, "kl": 512, "ql": 1536}
                ck6[("tpf", i)] = []
                for gi, (nm, src, soff, nblk, _x) in enumerate(nt_list):
                    @block.tensor
                    def _(t, i=i, gi=gi, nm=nm, src=src, soff=soff, nblk=nblk):
                        t.wait_ge(SEM["ve"], ck6[("nrm", i)])
                        prev = ck6.get(("evf", i, gi - 1)) or (ck6.get(("evf", i - 1, "last")) if i >= 1 else None)
                        if prev:
                            t.wait_ge(SEM["ac"], prev)
                        bank = psbf(7)
                        for j in range(nblk):
                            ins = t.transpose(bank[:, j * 128:(j + 1) * 128],
                                              src[:, soff + j * 128: soff + (j + 1) * 128],
                                              ident_bf[:])
                        INC(ins, "pe")
                        ck6[("tpf", i)].append(mark("pe"))

                    @block.scalar
                    def _(s, i=i, gi=gi, nm=nm, nblk=nblk):
                        s.wait_ge(SEM["pe"], ck6[("tpf", i)][gi])
                        if i >= 1 and gi == 0:
                            for t2 in ck6[("spl", i - 1)]:
                                WTOK(s, t2)
                        ins = s.copy(fmst[:, fmoff[nm]:fmoff[nm] + nblk * 128], psbf(7)[:, 0:nblk * 128])
                        INC(ins, "ac")
                        ck6[("evf", i, gi)] = mark("ac")
                        ck6[("evf", i, "last")] = mark("ac")

                @block.sync
                def _(sync, i=i, own=own):
                    sync.wait_ge(SEM["ac"], ck6[("evf", i, "last")])
                    sync.wait_ge(SEM["ac"], ck6[("ev", i)][-1])
                    toks = []
                    toks.append(DMA(sync, "sy",
                                    ks_sp.rearrange("(c p) t -> p c t", p=128)[:, :, i * 128:(i + 1) * 128],
                                    fmst[:, 0:256].rearrange("p (c n) -> p c n", c=2)))
                    toks.append(DMA(sync, "sy",
                                    kl_sp.rearrange("(c p) t -> p c t", p=128)[:, :, i * 128:(i + 1) * 128],
                                    fmst[:, 512:1536].rearrange("p (c n) -> p c n", c=8)))
                    toks.append(DMA(sync, "sy",
                                    vs_sp.rearrange("(hp p) c -> p hp c", p=128)[:, :, i * 130:(i + 1) * 130],
                                    vs65[:].rearrange("p (hp c) -> p hp c", c=130)))
                    toks.append(DMA(sync, "sy",
                                    vl_sp.rearrange("(hp p) c -> p hp c", p=128)[:, :, i * 130:(i + 1) * 130],
                                    vl65[:].rearrange("p (hp c) -> p hp c", c=130)))
                    if own:
                        j = i - 8
                        toks.append(DMA(sync, "sy",
                                        qs_sp.rearrange("(c p) t -> p c t", p=128)[:, :, j * 128:(j + 1) * 128],
                                        fmst[:, 256:512].rearrange("p (c n) -> p c n", c=2)))
                        toks.append(DMA(sync, "sy",
                                        ql_sp.rearrange("(c p) t -> p c t", p=128)[:, :, j * 128:(j + 1) * 128],
                                        fmst[:, 1536:2560].rearrange("p (c n) -> p c n", c=8)))
                    ck6[("spl", i)] = toks

            @block.tensor
            def _(t):
                t.wait_ge(SEM["ve"], mark("ve"))
                t.wait_ge(SEM["ac"], mark("ac"))
                for j in range(4):
                    ins = t.transpose(PS[0][0:8, j * 128:(j + 1) * 128],
                                      nacc[:, j * 8:(j + 1) * 8], ident_fs[:])
                INC(ins, "pe")
            ck_b6 = mark("pe")

            @block.vector
            def _(v):
                v.wait_ge(SEM["pe"], ck_b6)
                for j in range(4):
                    vlink(v, v.reduce_max(ntm6[0:8, j:j + 1],
                                          PS[0][0:8, j * 128:(j + 1) * 128].rearrange(
                                              "h (o t) -> h o t", o=1), AX.X))
                ins = v.engine_nop()
                INC(ins, "ve")
            ck_b6b = mark("ve")

            @block.scalar
            def _(s):
                s.wait_ge(SEM["ve"], ck_b6b)
                ins = s.activation(ntm6[0:8, 4:8], ntm6[0:8, 0:4], AF.Sqrt)
                INC(ins, "ac")
            ck_b6c = mark("ac")

            @block.vector
            def _(v):
                v.wait_ge(SEM["ac"], ck_b6c)
                vlink(v, v.tensor_tensor(ntm6[0:8, 0:1], ntm6[0:8, 4:5], ntm6[0:8, 5:6], ALU.mult))
                vlink(v, v.tensor_tensor(ntm6[0:8, 1:2], ntm6[0:8, 6:7], ntm6[0:8, 7:8], ALU.mult))
                vlink(v, v.tensor_scalar(ntm6[0:8, 0:1], ntm6[0:8, 0:1], -1.0 / math.sqrt(32.0), None, ALU.mult))
                ins = v.tensor_scalar(ntm6[0:8, 1:2], ntm6[0:8, 1:2], -1.0 / math.sqrt(128.0), None, ALU.mult)
                INC(ins, "ve")
            ck_b6d = mark("ve")

            @block.tensor
            def _(t):
                t.wait_ge(SEM["ve"], ck_b6d)
                t.transpose(PS[1][0:1, 0:8], ntm6[0:8, 0:1], ident_fs[0:8, 0:8])
                ins = t.transpose(PS[1][0:1, 8:16], ntm6[0:8, 1:2], ident_fs[0:8, 0:8])
                INC(ins, "pe")
            ck_b6e = mark("pe")

            @block.scalar
            def _(s):
                s.wait_ge(SEM["pe"], ck_b6e)
                ins = s.copy(ntm6[0:1, 0:16], PS[1][0:1, 0:16])
                INC(ins, "ac")
            ck_b6f = mark("ac")

            @block.tensor
            def _(t):
                t.wait_ge(SEM["ac"], ck_b6f)
                t.matmul(PS[2][:, 0:8], ones_fs[:], ntm6[0:1, 0:8], start=True, stop=True)
                ins = t.matmul(PS[2][:, 8:16], ones_fs[:], ntm6[0:1, 8:16], start=True, stop=True)
                INC(ins, "pe")
            ck_b6g = mark("pe")

            @block.scalar
            def _(s):
                s.wait_ge(SEM["pe"], ck_b6g)
                ins = s.copy(bmat_b[:], PS[2][:, 0:16])
                INC(ins, "ac")
            ck_b6h = mark("ac")

            if DEBUG:
                @block.sync
                def _(sync):
                    sync.wait_ge(SEM["ac"], ck_b6h)
                    DMA(sync, "sy", bm_dbg[:, 16:32], bmat_b[:])

        emit_barrier()
        attnb_fm = None
        AF2 = contextlib.ExitStack()
        if PHASES >= 7:
            attnb_fm = AF2.enter_context(nc.sbuf_tensor("attnb_fm", [128, 8 * T_OWN], BF16))
        with contextlib.ExitStack() as P7S:
          if PHASES >= 7:
            E7 = P7S.enter_context
            kbuf = E7(nc.sbuf_tensor("kbuf", [128, 2 * T_PROC], BF16))
            qbuf = E7(nc.sbuf_tensor("qbuf", [128, 2 * T_OWN], BF16))
            vbuf = E7(nc.sbuf_tensor("vbuf", [128, 2 * NT_PROC * 130], BF16))
            PT7 = E7(nc.sbuf_tensor("PT7", [128, 4 * 512], BF16))
            recs7 = E7(nc.sbuf_tensor("recs7", [2, 2 * 512], F32))
            rbc7 = E7(nc.sbuf_tensor("rbc7", [128, 2 * 512], F32))
            wmk = E7(nc.sbuf_tensor("wmk", [128, max(1, N_MASKS) * 512], BF16))
            cmk = E7(nc.sbuf_tensor("cmk", [128, NT_PROC], F32))

            tok_m7 = []

            @block.sync
            def _(sync):
                tok_m7.append(DMA(sync, "sy", wmk[:], wmasks))
                tok_m7.append(DMA(sync, "sy", cmk[:], colmask))

            @block.vector
            def _(v):
                for tk in tok_m7:
                    WTOK(v, tk)
                ins = v.engine_nop()
                INC(ins, "ve")

            units = [("s", u) for u in range(4)] + [("l", u) for u in range(8)]
            bank7 = {0: 0, 1: 0}
            pt7 = {0: 0, 1: 0, 2: 0, 3: 0}
            pv7 = {0: 0, 1: 0}
            recs7_pe = 0
            ck_uload = {}
            ck_updone = {}

            for ui, (kind, u) in enumerate(units):
                bb = ui % 2

                @block.sync
                def _(sync, ui=ui, kind=kind, u=u, bb=bb):
                    if ui >= 2:
                        sync.wait_ge(SEM["pe"], ck_updone[ui - 2])
                    toks = []
                    if kind == "s":
                        toks.append(DMA(sync, "sy", kbuf[0:64, bb * T_PROC:(bb + 1) * T_PROC],
                                        ks_sp[u * 64:(u + 1) * 64, :]))
                        toks.append(DMA(sync, "sy", qbuf[0:64, bb * T_OWN: (bb + 1) * T_OWN],
                                        qs_sp[u * 64:(u + 1) * 64, :]))
                        toks.append(DMA(sync, "sy", vbuf[:, bb * NT_PROC * 130:(bb + 1) * NT_PROC * 130],
                                        vs_sp[u * 128:(u + 1) * 128, :]))
                    else:
                        toks.append(DMA(sync, "sy", kbuf[:, bb * T_PROC:(bb + 1) * T_PROC],
                                        kl_sp[u * 128:(u + 1) * 128, :]))
                        toks.append(DMA(sync, "sy", qbuf[:, bb * T_OWN:(bb + 1) * T_OWN],
                                        ql_sp[u * 128:(u + 1) * 128, :]))
                        toks.append(DMA(sync, "sy", vbuf[:, bb * NT_PROC * 130:(bb + 1) * NT_PROC * 130],
                                        vl_sp[(u // 2) * 128:(u // 2) * 128 + 128, :]))
                    ck_uload[ui] = toks

                wintiles = WIN_S_TILES if kind == "s" else WIN_L_TILES
                scale7 = 1.0 / math.sqrt(32.0) if kind == "s" else 1.0 / math.sqrt(128.0)
                for qc in range(4):
                    ents = wintiles[qc]
                    pvb = 4 + (qc % 2)
                    nk = len(ents)
                    for ki, (kb, msk) in enumerate(ents):
                        bp = ki % 2
                        slA = bp * 2
                        slB = bp * 2 + 1
                        w_bank = bank7[bp]
                        w_slA = pt7[slA]
                        w_slB = pt7[slB]
                        w_pv = pv7[qc % 2] if ki == 0 else 0

                        @block.tensor
                        def _(t, ui=ui, kind=kind, u=u, bb=bb, qc=qc, kb=kb, bp=bp,
                              w_bank=w_bank, qc0=(qc == 0 and ki == 0)):
                            if qc0:
                                for tk in ck_uload[ui]:
                                    WTOK(t, tk)
                            if w_bank:
                                t.wait_ge(SEM["ac"], w_bank)
                            kB = kbuf[:, bb * T_PROC + kb * 128: bb * T_PROC + (kb + 1) * 128]
                            qs = qbuf[:, bb * T_OWN + qc * 512: bb * T_OWN + (qc + 1) * 512]
                            if kind == "s":
                                t.matmul(PS[bp * 2][:, 0:512], kB[0:32, :], qs[0:32, :],
                                         start=True, stop=True)
                                ins = t.matmul(PS[bp * 2 + 1][:, 0:512], kB[32:64, :], qs[32:64, :],
                                               start=True, stop=True, tile_position=(32, 0))
                            else:
                                ins = t.matmul(PS[bp * 2][:, 0:512], kB, qs,
                                               start=True, stop=True)
                            INC(ins, "pe")
                        ck_smm7 = mark("pe")

                        @block.scalar
                        def _(s, ui=ui, kind=kind, u=u, bp=bp, slA=slA, slB=slB,
                              ck=ck_smm7, w_slA=w_slA, w_slB=w_slB, scale7=scale7):
                            s.wait_ge(SEM["pe"], ck)
                            if w_slA:
                                s.wait_ge(SEM["pe"], w_slA)
                            if w_slB:
                                s.wait_ge(SEM["pe"], w_slB)
                            if kind == "s":
                                s.activation(PT7[:, slA * 512:(slA + 1) * 512], PS[bp * 2][:, 0:512],
                                             AF.Exp, bias=bmat_b[:, 2 * u:2 * u + 1], scale=scale7)
                                ins = s.activation(PT7[:, slB * 512:(slB + 1) * 512],
                                                   PS[bp * 2 + 1][:, 0:512],
                                                   AF.Exp, bias=bmat_b[:, 2 * u + 1:2 * u + 2],
                                                   scale=scale7)
                            else:
                                ins = s.activation(PT7[:, slA * 512:(slA + 1) * 512],
                                                   PS[bp * 2][:, 0:512],
                                                   AF.Exp, bias=bmat_b[:, 8 + u:9 + u], scale=scale7)
                            INC(ins, "ac")
                        bank7[bp] = mark("ac")
                        ck_exp7 = mark("ac")

                        @block.vector
                        def _(v, kind=kind, kb=kb, msk=msk, slA=slA, slB=slB, ck=ck_exp7):
                            v.wait_ge(SEM["ac"], ck)
                            ins = None
                            for sl in ([slA, slB] if kind == "s" else [slA]):
                                if msk is not None:
                                    if ins is not None:
                                        vlink(v, ins)
                                    ins = v.tensor_tensor(
                                        PT7[:, sl * 512:(sl + 1) * 512],
                                        PT7[:, sl * 512:(sl + 1) * 512],
                                        wmk[:, _mask_slot(msk) * 512:(_mask_slot(msk) + 1) * 512],
                                        ALU.mult)
                                if kb < 8:
                                    if ins is not None:
                                        vlink(v, ins)
                                    ins = v.tensor_scalar(
                                        PT7[:, sl * 512:(sl + 1) * 512],
                                        PT7[:, sl * 512:(sl + 1) * 512],
                                        cmk[:, kb:kb + 1], None, ALU.mult)
                            if ins is None:
                                ins = v.engine_nop()
                            INC(ins, "ve")
                        ck_msk7 = mark("ve")

                        @block.tensor
                        def _(t, kind=kind, bb=bb, kb=kb, slA=slA, slB=slB, pvb=pvb,
                              ck=ck_msk7, ki=ki, nk=nk, w_pv=w_pv):
                            t.wait_ge(SEM["ve"], ck)
                            if w_pv:
                                t.wait_ge(SEM["ve"], w_pv)
                            vA = vbuf[:, bb * NT_PROC * 130 + kb * 130: bb * NT_PROC * 130 + kb * 130 + 65]
                            vB = vbuf[:, bb * NT_PROC * 130 + kb * 130 + 65: bb * NT_PROC * 130 + (kb + 1) * 130]
                            if kind == "s":
                                t.matmul(PS[pvb][0:65, 0:512], vA, PT7[:, slA * 512:(slA + 1) * 512],
                                         start=(ki == 0), stop=(ki == nk - 1))
                                ins = t.matmul(PS[6 + (pvb - 4)][0:65, 0:512], vB,
                                               PT7[:, slB * 512:(slB + 1) * 512],
                                               start=(ki == 0), stop=(ki == nk - 1))
                            else:
                                vv = vA if u % 2 == 0 else vB
                                ins = t.matmul(PS[pvb][0:65, 0:512], vv,
                                               PT7[:, slA * 512:(slA + 1) * 512],
                                               start=(ki == 0), stop=(ki == nk - 1))
                            INC(ins, "pe")
                        pt7[slA] = mark("pe")
                        pt7[slB] = mark("pe")

                    ck_pvl7 = mark("pe")
                    if qc == 3:
                        ck_updone[ui] = ck_pvl7

                    @block.vector
                    def _(v, ck=ck_pvl7, pvb=pvb, kind=kind, rfp=recs7_pe):
                        v.wait_ge(SEM["pe"], ck)
                        if rfp:
                            v.wait_ge(SEM["pe"], rfp)
                        v.reciprocal(recs7[0:1, 0:512], PS[pvb][64:65, 0:512])
                        if kind == "s":
                            ins = v.reciprocal(recs7[0:1, 512:1024], PS[6 + (pvb - 4)][64:65, 0:512])
                        else:
                            ins = v.engine_nop()
                        INC(ins, "ve")
                    ck_rc7 = mark("ve")

                    @block.tensor
                    def _(t, ck=ck_rc7, kind=kind, b0=bank7[0]):
                        t.wait_ge(SEM["ve"], ck)
                        t.wait_ge(SEM["ac"], b0)
                        if kind == "s":
                            t.matmul(PS[0][:, 0:512], ones_fs[:], recs7[0:1, 0:512],
                                     start=True, stop=True)
                            ins = t.matmul(PS[1][:, 0:512], ones_fs[:], recs7[0:1, 512:1024],
                                           start=True, stop=True)
                        else:
                            ins = t.matmul(PS[0][:, 0:512], ones_fs[:], recs7[0:1, 0:512],
                                           start=True, stop=True)
                        INC(ins, "pe")
                    recs7_pe = mark("pe")
                    ck_bc7 = mark("pe")

                    @block.scalar
                    def _(s, ck=ck_bc7, kind=kind, w0=pv7[0], w1=pv7[1]):
                        s.wait_ge(SEM["pe"], ck)
                        if w0:
                            s.wait_ge(SEM["ve"], w0)
                        if w1:
                            s.wait_ge(SEM["ve"], w1)
                        if kind == "s":
                            s.copy(rbc7[:, 0:512], PS[0][:, 0:512])
                            ins = s.copy(rbc7[:, 512:1024], PS[1][:, 0:512])
                        else:
                            ins = s.copy(rbc7[:, 0:512], PS[0][:, 0:512])
                        INC(ins, "ac")
                    bank7[0] = mark("ac")
                    ck_rb7 = mark("ac")

                    @block.vector
                    def _(v, ck=ck_rb7, kind=kind, u=u, qc=qc, pvb=pvb):
                        v.wait_ge(SEM["ac"], ck)
                        if kind == "s":
                            dst = attnb_fm[:, u * T_OWN + qc * 512: u * T_OWN + (qc + 1) * 512]
                            v.tensor_tensor(dst[0:64, :], PS[pvb][0:64, 0:512],
                                            rbc7[0:64, 0:512], ALU.mult)
                            ins = v.tensor_tensor(dst[64:128, :], PS[6 + (pvb - 4)][0:64, 0:512],
                                                  rbc7[64:128, 512:1024], ALU.mult)
                        else:
                            ch = 4 + u // 2
                            r0 = (u % 2) * 64
                            dst = attnb_fm[:, ch * T_OWN + qc * 512: ch * T_OWN + (qc + 1) * 512]
                            ins = v.tensor_tensor(dst[r0:r0 + 64, :], PS[pvb][0:64, 0:512],
                                                  rbc7[r0:r0 + 64, 0:512], ALU.mult)
                        INC(ins, "ve")
                    pv7[qc % 2] = mark("ve")

        emit_barrier()
        with contextlib.ExitStack() as P8S:
          if PHASES >= 7:
            E8 = P8S.enter_context
            bprojWn = E8(nc.sbuf_tensor("bprojWn", [128, 8 * D], BF16))
            tok_w8 = []

            @block.gpsimd
            def _(g):
                for c in range(8):
                    tok_w8.append(DMA(g, "gp", bprojWn[:, c * D:(c + 1) * D],
                                      bWproj[c * 128:(c + 1) * 128, :]))

            ck_m8, ck_r8 = {}, {}
            for j in range(NT_OWN):
                t_ = 8 + j
                bb = j % 2

                @block.tensor
                def _(t, j=j, t_=t_, bb=bb):
                    if j == 0:
                        for tk in tok_w8:
                            WTOK(t, tk)
                    if j >= 2:
                        t.wait_ge(SEM["ve"], ck_r8[j - 2])
                    for half in range(2):
                        bank = PS[bb * 2 + half]
                        for c in range(8):
                            ins = t.matmul(bank[:, 0:512],
                                           attnb_fm[:, c * T_OWN + j * 128: c * T_OWN + (j + 1) * 128],
                                           bprojWn[:, c * D + half * 512: c * D + (half + 1) * 512],
                                           start=(c == 0), stop=(c == 7))
                    INC(ins, "pe")
                    ck_m8[j] = mark("pe")

                @block.vector
                def _(v, j=j, t_=t_, bb=bb):
                    v.wait_ge(SEM["pe"], ck_m8[j])
                    for half in range(2):
                        ins = v.tensor_tensor(
                            x_buf[:, t_ * D + half * 512: t_ * D + (half + 1) * 512],
                            PS[bb * 2 + half][:, 0:512],
                            x_buf[:, t_ * D + half * 512: t_ * D + (half + 1) * 512],
                            ALU.add)
                    INC(ins, "ve")
                    ck_r8[j] = mark("ve")

        AF2.close()

        if DEBUG and PHASES == 7:
            @block.sync
            def _(sync):
                for sn, vv in CNT.items():
                    if vv:
                        sync.wait_ge(SEM[sn], vv)
                DMA(sync, "sy", x1_dbg, x_buf[:])

        if PHASES >= 8:
            emit_ffn(list(range(8, 8 + NT_OWN)), "f2")

        if DEBUG and PHASES >= 8:
            @block.sync
            def _(sync):
                for sn, vv in CNT.items():
                    if vv:
                        sync.wait_ge(SEM[sn], vv)
                DMA(sync, "sy", x1_dbg, x_buf[:])

        if PHASES >= 8:
            @block.sync
            def _(sync):
                for sn, vv in CNT.items():
                    if vv:
                        sync.wait_ge(SEM[sn], vv)
                for j in range(NT_OWN):
                    DMA(sync, "sy", out[j * 128:(j + 1) * 128, :],
                        x_buf[:, (8 + j) * D:(9 + j) * D])

        @block.sync
        def _(sync):
            for sn, vv in CNT.items():
                if vv:
                    sync.wait_ge(SEM[sn], vv)

    return nc



def kernel(L0_attn_a_x, L0_attn_a_qkv_w, L0_attn_a_qkv_b, L0_attn_a_proj_w,
           L0_attn_a_proj_b, L0_attn_b_Wqk_short, L0_attn_b_Wv_short,
           L0_attn_b_Wqk_long, L0_attn_b_Wv_long, L0_attn_b_Wproj,
           L0_ffn_w1, L0_ffn_b1, L0_ffn_w2, L0_ffn_b2, L0_ffn_gamma, L0_ffn_beta):
    x0 = np.ascontiguousarray(np.asarray(L0_attn_a_x, dtype=np.float32))
    B = x0.shape[0]
    nc = build_program()

    shared = {
        "qkv_w": np.ascontiguousarray(L0_attn_a_qkv_w, dtype=np.float32),
        "qkv_b": np.asarray(L0_attn_a_qkv_b, np.float32).reshape(1, -1),
        "proj_w": np.ascontiguousarray(L0_attn_a_proj_w, dtype=np.float32),
        "proj_b": np.asarray(L0_attn_a_proj_b, np.float32).reshape(1, -1),
        "bWqk_s": np.ascontiguousarray(L0_attn_b_Wqk_short, dtype=np.float32),
        "bWv_s": np.ascontiguousarray(L0_attn_b_Wv_short, dtype=np.float32),
        "bWqk_l": np.ascontiguousarray(L0_attn_b_Wqk_long, dtype=np.float32),
        "bWv_l": np.ascontiguousarray(L0_attn_b_Wv_long, dtype=np.float32),
        "bWproj": np.ascontiguousarray(L0_attn_b_Wproj, dtype=np.float32),
        "w1": np.ascontiguousarray(L0_ffn_w1, dtype=np.float32),
        "b1": np.asarray(L0_ffn_b1, np.float32).reshape(1, -1),
        "w2": np.ascontiguousarray(L0_ffn_w2, dtype=np.float32),
        "b2": np.asarray(L0_ffn_b2, np.float32).reshape(1, -1),
        "gam_rep": np.ascontiguousarray(
            np.broadcast_to(np.asarray(L0_ffn_gamma, np.float32), (128, D))),
        "bet_rep": np.ascontiguousarray(
            np.broadcast_to(np.asarray(L0_ffn_beta, np.float32), (128, D))),
        "ident_f": np.eye(128, dtype=np.float32),
        "ones_rw": np.ones((1, 128), np.float32),
        "wmasks": (np.concatenate([m for m in _MASKS], axis=1)
                   if N_MASKS else np.zeros((128, 512), BF16NP)),
    }
    thk_hi, thk_lo = _theta_tables(np.arange(T_FULL))

    in_maps = []
    for c in range(8):
        b, h = c // 2, c % 2
        if h == 0:
            xp = np.concatenate([np.zeros((T_CTX, D), np.float32), x0[b, :T_OWN]], axis=0)
            qpos = np.arange(-T_CTX, T_OWN)
        else:
            xp = x0[b, T_CTX:T_FULL]
            qpos = np.arange(T_CTX, T_FULL)
        thq_hi, thq_lo = _theta_tables(qpos)
        cm = np.zeros((128, NT_PROC), np.float32)
        for kb in range(NT_PROC):
            alive = (kb * 128 + np.arange(128)) >= (T_CTX if h == 0 else 0)
            cm[:, kb] = alive.astype(np.float32)
        m = dict(shared)
        m.update({
            "x_proc": np.ascontiguousarray(xp),
            "x_full": np.ascontiguousarray(x0[b]),
            "thq_hi": thq_hi, "thq_lo": thq_lo,
            "thk_hi": thk_hi, "thk_lo": thk_lo,
            "colmask": cm,
        })
        in_maps.append(m)

    res = run_bass_kernel_spmd(nc, in_maps, core_ids=list(range(8)))
    outp = np.empty((B, T_FULL, D), np.float32)
    for c in range(8):
        b, h = c // 2, c % 2
        outp[b, h * T_OWN:(h + 1) * T_OWN] = res.results[c]["out"]
    if DEBUG:
        kernel._dbg = res.results
    return outp


# version 07
# speedup vs baseline: 1.0187x; 1.0187x over previous
import contextlib
import math
import numpy as np
import ml_dtypes
import concourse.bass as bass
import concourse.mybir as mybir
from concourse.bass_utils import run_bass_kernel_spmd

F32 = mybir.dt.float32
BF16 = mybir.dt.bfloat16
I32 = mybir.dt.int32
AF = mybir.ActivationFunctionType
ALU = mybir.AluOpType
AX = mybir.AxisListType
BF16NP = ml_dtypes.bfloat16

D = 1024
HEADS = 16
DH = 64
T_FULL = 4096
T_CTX = 1024
T_OWN = 2048
T_PROC = 3072
NT_FULL = 32
NT_PROC = 24
NT_OWN = 16
DHID = 4096
EPS = 1e-5
HS, DS = 8, 32
HL, DL = 8, 128
HD = 64
WIN_S, WIN_L = 256, 1024
PIH = float(np.float32(np.pi / 2))
PI = float(np.float32(np.pi))
TWOPI = float(np.float32(2 * np.pi))

DEBUG = False
PHASES = 9



def _theta_tables(positions):
    n = DH // 2
    j = np.arange(n, dtype=np.float64)
    freqs = (40.0) ** (j / (n - 1))
    pos = positions.astype(np.float64) / float(T_FULL)
    theta = 2.0 * np.pi * pos[:, None] * freqs[None, :]
    red = np.remainder(theta + np.pi, 2.0 * np.pi) - np.pi
    red[positions < 0, :] = 0.0
    full = np.concatenate([-red, red], axis=1)
    hi = full.astype(np.float32)
    lo = (full - hi.astype(np.float64)).astype(np.float32)
    return hi, lo


def _win_mask_tile(qt0, kb, W):
    p = np.arange(128)[:, None]
    j = np.arange(512)[None, :]
    diff = (qt0 + j) - (kb * 128 + p)
    return ((diff >= 0) & (diff < W)).astype(BF16NP)


def _win_tiles(W):
    out = []
    for qc in range(4):
        qt0 = T_CTX + 512 * qc
        lo_kb = max(0, (qt0 - W + 1) // 128)
        hi_kb = min(NT_PROC - 1, (qt0 + 511) // 128)
        ents = []
        for kb in range(lo_kb, hi_kb + 1):
            mn = qt0 - (kb * 128 + 127)
            mx = qt0 + 511 - kb * 128
            if mn >= 0 and mx <= W - 1:
                ents.append((kb, None))
            else:
                m = _win_mask_tile(qt0, kb, W)
                if m.any():
                    ents.append((kb, m))
        out.append(ents)
    return out


WIN_S_TILES = _win_tiles(WIN_S)
WIN_L_TILES = _win_tiles(WIN_L)
_MASKS = []
_MASK_IDX = {}
for _wt in (WIN_S_TILES, WIN_L_TILES):
    for _ents in _wt:
        for _kb, _m in _ents:
            if _m is not None:
                _key = _m.tobytes()
                if _key not in _MASK_IDX:
                    _MASK_IDX[_key] = len(_MASKS)
                    _MASKS.append(_m)
N_MASKS = len(_MASKS)


def _mask_slot(m):
    return _MASK_IDX[m.tobytes()]



def build_program():
    nc = bass.Bass("TRN2", target_bir_lowering=False)

    def din(name, shape, dt=F32):
        return nc.dram_tensor(name, shape, dt, kind="ExternalInput").ap()

    def dscr(name, shape, dt=BF16):
        kind = "ExternalOutput" if DEBUG else "Internal"
        return nc.dram_tensor(name, shape, dt, kind=kind).ap()

    x_proc = din("x_proc", [T_PROC, D])
    x_full = din("x_full", [T_FULL, D])
    qkv_w = din("qkv_w", [3 * D, D])
    qkv_b = din("qkv_b", [1, 3 * D])
    proj_w = din("proj_w", [D, D])
    proj_b = din("proj_b", [1, D])
    bWqk_s = din("bWqk_s", [D, 2 * HS * DS])
    bWv_s = din("bWv_s", [D, HS * HD])
    bWqk_l = din("bWqk_l", [D, 2 * HL * DL])
    bWv_l = din("bWv_l", [D, HL * HD])
    bWproj = din("bWproj", [D, D])
    w1 = din("w1", [DHID, D])
    b1 = din("b1", [1, DHID])
    w2 = din("w2", [D, DHID])
    b2 = din("b2", [1, D])
    gam_rep = din("gam_rep", [128, D])
    bet_rep = din("bet_rep", [128, D])
    thq_hi = din("thq_hi", [T_PROC, DH])
    thq_lo = din("thq_lo", [T_PROC, DH])
    thk_hi = din("thk_hi", [T_FULL, DH])
    thk_lo = din("thk_lo", [T_FULL, DH])
    colmask = din("colmask", [128, NT_PROC])
    wmasks = din("wmasks", [128, max(1, N_MASKS) * 512], BF16)
    ident_f = din("ident_f", [128, 128])
    ones_rw = din("ones_rw", [1, 128])

    out = nc.dram_tensor("out", [T_OWN, D], F32, kind="ExternalOutput").ap()

    q_sp = dscr("q_sp", [D, T_PROC])
    k_sp = dscr("k_sp", [D, T_FULL])
    v_sp = dscr("v_sp", [8 * 128, 32 * 130])
    ks_sp = dscr("ks_sp", [HS * DS, T_PROC])
    qs_sp = dscr("qs_sp", [HS * DS, T_OWN])
    kl_sp = dscr("kl_sp", [HL * DL, T_PROC])
    ql_sp = dscr("ql_sp", [HL * DL, T_OWN])
    vs_sp = dscr("vs_sp", [4 * 128, NT_PROC * 130])
    vl_sp = dscr("vl_sp", [4 * 128, NT_PROC * 130])
    w1q_sp = dscr("w1q_sp", [D, DHID])
    w2q_sp = dscr("w2q_sp", [DHID, D])
    hq_sp = dscr("hq_sp", [NT_PROC * 128, DHID], F32)
    x1_dbg = dscr("x1_dbg", [128, NT_PROC * D], F32) if DEBUG else None
    bm_dbg = dscr("bm_dbg", [128, 2 * HEADS], F32) if DEBUG else None
    wp_dbg = dscr("wp_dbg", [128, 8 * 3 * D], BF16) if DEBUG else None
    tab_dbg = dscr("tab_dbg", [128, 2 * 56 * DH], BF16) if DEBUG else None

    with contextlib.ExitStack() as TOP:
        E = TOP.enter_context

        ident_bf = E(nc.sbuf_tensor("ident_bf", [128, 128], BF16))
        ident_fs = E(nc.sbuf_tensor("ident_fs", [128, 128], F32))
        ones_bf = E(nc.sbuf_tensor("ones_bf", [1, 128], BF16))
        ones_fs = E(nc.sbuf_tensor("ones_fs", [1, 128], F32))
        bmat = E(nc.sbuf_tensor("bmat", [128, HEADS], F32))
        bmat_b = E(nc.sbuf_tensor("bmat_b", [128, HEADS], F32))
        PS = [E(nc.psum_tensor("ps%d" % i, [128, 512], F32)) for i in range(8)]

        def psbf(i):
            return PS[i][:].bitcast(BF16)

        semnames = (["sy", "sc", "gp", "pe", "ac", "ve"]
                    + ["syp%d" % i for i in range(6)]
                    + ["gpp%d" % i for i in range(4)]
                    + ["scp%d" % i for i in range(2)])
        SEM = {n: E(nc.semaphore(n)) for n in semnames}
        CNT = {n: 0 for n in SEM}
        _RR = {"sy": 0, "gp": 0, "sc": 0}
        _POOL = {"sy": ["syp%d" % i for i in range(6)],
                 "gp": ["gpp%d" % i for i in range(4)],
                 "sc": ["scp%d" % i for i in range(2)]}

        block = E(nc.Block())

        def INC(ins, s, k=1):
            ins.then_inc(SEM[s], k)
            CNT[s] += k

        def DMA(eng, q, out, in_):
            pool = _POOL[q]
            sname = pool[_RR[q] % len(pool)]
            _RR[q] += 1
            if CNT[sname]:
                eng.wait_ge(SEM[sname], CNT[sname])
            ins = eng.dma_start(out=out, in_=in_)
            INC(ins, sname, 16)
            return (sname, CNT[sname])

        def WTOK(eng, tok):
            if tok is not None:
                eng.wait_ge(SEM[tok[0]], tok[1])

        def mark(s):
            return CNT[s]

        def vlink(v, ins):
            INC(ins, "ve")
            v.wait_ge(SEM["ve"], CNT["ve"])

        def alink(sc, ins):
            INC(ins, "ac")
            sc.wait_ge(SEM["ac"], CNT["ac"])

        def emit_barrier():
            vals = dict(CNT)
            for ename in ["sync", "gpsimd", "scalar", "tensor", "vector"]:
                def bodyf(eng, vals=vals):
                    for sn, vv in vals.items():
                        if vv:
                            eng.wait_ge(SEM[sn], vv)
                getattr(block, ename)(bodyf)

        tok_const = []

        @block.sync
        def _(sync):
            tok_const.append(DMA(sync, "sy", ident_fs[:], ident_f))
            tok_const.append(DMA(sync, "sy", ones_fs[:], ones_rw))

        tok_const_bf = []

        @block.gpsimd
        def _(g):
            for tk in tok_const:
                WTOK(g, tk)
            tok_const_bf.append(DMA(g, "gp", ident_bf[:], ident_fs[:]))
            tok_const_bf.append(DMA(g, "gp", ones_bf[:], ones_fs[:]))

        WSCOPE = contextlib.ExitStack()
        qkvWp = WSCOPE.enter_context(nc.sbuf_tensor("qkvWp", [128, 8 * 3 * D], BF16))
        qkvB_bf = WSCOPE.enter_context(nc.sbuf_tensor("qkvB_bf", [1, 3 * D], BF16))

        def emit_wt_transpose(w_ap, nrow_tiles, dst_sb, dst_cols, ES):
            wbuf = ES.enter_context(nc.sbuf_tensor("wtb_%s" % dst_sb.name, [128, 2 * D], BF16))
            ck_ld, ck_tp, ck_ev = {}, {}, {}
            for r in range(nrow_tiles):
                @block.gpsimd
                def _(g, r=r):
                    if r >= 2:
                        g.wait_ge(SEM["pe"], ck_tp[r - 2])
                    for tk in tok_const_bf:
                        WTOK(g, tk)
                    ck_ld[r] = DMA(g, "gp", wbuf[:, (r % 2) * D:(r % 2 + 1) * D],
                                   w_ap[r * 128:(r + 1) * 128, :])

                @block.tensor
                def _(t, r=r):
                    WTOK(t, ck_ld[r])
                    if r >= 2:
                        t.wait_ge(SEM["ac"], ck_ev[r - 2])
                    src = wbuf[:, (r % 2) * D:(r % 2 + 1) * D]
                    bank = psbf(6 + (r % 2))
                    for c in range(8):
                        ins = t.transpose(bank[:, c * 128:(c + 1) * 128],
                                          src[:, c * 128:(c + 1) * 128], ident_bf[:])
                    INC(ins, "pe")
                    ck_tp[r] = mark("pe")

                @block.scalar
                def _(s, r=r):
                    s.wait_ge(SEM["pe"], ck_tp[r])
                    bank = psbf(6 + (r % 2))
                    for c in range(8):
                        ins = s.copy(dst_sb[:, c * dst_cols + r * 128: c * dst_cols + (r + 1) * 128],
                                     bank[:, c * 128:(c + 1) * 128])
                    INC(ins, "ac")
                    ck_ev[r] = mark("ac")
            return ck_ev[nrow_tiles - 1]

        with contextlib.ExitStack() as WTMP:
            ck_qkvw = emit_wt_transpose(qkv_w, 24, qkvWp, 3 * D, WTMP)

        ck_biases = []

        @block.gpsimd
        def _(g):
            ck_biases.append(DMA(g, "gp", qkvB_bf[:], qkv_b))

        emit_barrier()

        with contextlib.ExitStack() as P2:
            E2 = P2.enter_context
            xt2 = E2(nc.sbuf_tensor("xt2", [128, 2 * D], F32))
            sqsc = E2(nc.sbuf_tensor("sqsc", [128, D], F32))
            stat = E2(nc.sbuf_tensor("stat", [128, 16], F32))
            xln2 = E2(nc.sbuf_tensor("xln2", [128, 2 * D], BF16))
            xf2 = E2(nc.sbuf_tensor("xf2", [128, 2 * D], BF16))
            ktm2 = E2(nc.sbuf_tensor("ktm2", [128, 2 * D], BF16))
            rt1 = E2(nc.sbuf_tensor("rt1", [128, D], BF16))
            rt2 = E2(nc.sbuf_tensor("rt2", [128, D], BF16))
            kro2 = E2(nc.sbuf_tensor("kro2", [128, 2 * D], BF16))
            kfm2 = E2(nc.sbuf_tensor("kfm2", [128, 2 * D], BF16))
            v1t2 = E2(nc.sbuf_tensor("v1t2", [128, 2 * HEADS * 65], BF16))
            nsq = E2(nc.sbuf_tensor("nsq", [128, D], BF16))
            ntmp = E2(nc.sbuf_tensor("ntmp", [128, HEADS], F32))
            nq_acc = E2(nc.sbuf_tensor("nq_acc", [128, HEADS], F32))
            nk_acc = E2(nc.sbuf_tensor("nk_acc", [128, HEADS], F32))
            kdl = E2(nc.sbuf_tensor("kdl", [128, 2 * D], BF16))
            dprod = E2(nc.sbuf_tensor("dprod", [128, D], BF16))
            ind2 = E2(nc.sbuf_tensor("ind2", [128, 2], BF16))
            dacc = E2(nc.sbuf_tensor("dacc", [128, HEADS], F32))
            Ctab = E2(nc.sbuf_tensor("Ctab", [128, (NT_FULL + NT_PROC) * DH], BF16))
            Stab = E2(nc.sbuf_tensor("Stab", [128, (NT_FULL + NT_PROC) * DH], BF16))

            NTK = NT_FULL

            @block.vector
            def _(v):
                dstp = v1t2[:].rearrange("p (k c) -> p k c", c=65)[:, :, 64:65]
                ins = v.memset(dstp, 1.0)
                INC(ins, "ve")
            ck_v1ones = mark("ve")

            with contextlib.ExitStack() as THS:
                thbuf = THS.enter_context(
                    nc.sbuf_tensor("thbuf", [128, (NT_FULL + NT_PROC) * DH], F32))
                thlo = THS.enter_context(
                    nc.sbuf_tensor("thlo", [128, (NT_FULL + NT_PROC) * DH], F32))
                thw = THS.enter_context(
                    nc.sbuf_tensor("thw", [128, (NT_FULL + NT_PROC) * DH], F32))

                tok_th = []

                @block.sync
                def _(sync):
                    def tiled(ap):
                        return ap.rearrange("(n p) c -> p n c", p=128)

                    def dst3(buf, o, nt):
                        return buf[:, o * DH:(o + nt) * DH].rearrange(
                            "p (n c) -> p n c", n=nt)
                    tok_th.append(DMA(sync, "sy", dst3(thbuf, 0, NTK), tiled(thk_hi)))
                    tok_th.append(DMA(sync, "sy", dst3(thbuf, NTK, NT_PROC), tiled(thq_hi)))
                    tok_th.append(DMA(sync, "sy", dst3(thlo, 0, NTK), tiled(thk_lo)))
                    tok_th.append(DMA(sync, "sy", dst3(thlo, NTK, NT_PROC), tiled(thq_lo)))

                @block.vector
                def _(v):
                    for tk in tok_th:
                        WTOK(v, tk)
                    vlink(v, v.tensor_add(thw[:], thbuf[:], thlo[:]))
                    vlink(v, v.tensor_scalar(thbuf[:], thw[:], PIH, None, ALU.add))
                    vlink(v, v.tensor_single_scalar(thlo[:], thbuf[:], -PI, ALU.is_ge))
                    vlink(v, v.tensor_scalar(thbuf[:], thbuf[:], TWOPI, None, ALU.add))
                    vlink(v, v.scalar_tensor_tensor(thbuf[:], thlo[:], -TWOPI, thbuf[:], ALU.mult, ALU.add))
                    vlink(v, v.tensor_single_scalar(thlo[:], thw[:], PI - PIH, ALU.is_ge))
                    vlink(v, v.scalar_tensor_tensor(thbuf[:], thlo[:], -TWOPI, thbuf[:], ALU.mult, ALU.add))
                    vlink(v, v.tensor_scalar(thbuf[:], thbuf[:], -PI, PI, ALU.max, ALU.min))
                    vlink(v, v.tensor_scalar(thw[:], thw[:], -PI, PI, ALU.max, ALU.min))
                    v.memset(nq_acc[:], 0.0)
                    ins = v.memset(nk_acc[:], 0.0)
                    INC(ins, "ve")
                ck_thv = mark("ve")

                @block.scalar
                def _(s):
                    s.wait_ge(SEM["ve"], ck_thv)
                    s.activation(Stab[:], thw[:], AF.Sin)
                    ins = s.activation(Ctab[:], thbuf[:], AF.Sin)
                    INC(ins, "ac")
                ck_tabs = mark("ac")

            ck_x, ck_rs, ck_sq, ck_ch, ck_rst, ck_ln = {}, {}, {}, {}, {}, {}
            ck_tpx, ck_evx, ck_mm, ck_kvev, ck_rope, ck_nrm = {}, {}, {}, {}, {}, {}
            ck_tpk, ck_evk, ck_spill, ck_nsq = {}, {}, {}, {}

            def p2_tile(i, is_kv):
                b = i % 2
                sb = i % 2
                st = stat[:, sb * 8:(sb + 1) * 8]
                if is_kv:
                    src = x_full[i * 128:(i + 1) * 128, :]
                    tabslice = slice(i * DH, (i + 1) * DH)
                else:
                    j = i - NT_FULL
                    src = x_proc[j * 128:(j + 1) * 128, :]
                    tabslice = slice((NTK + j) * DH, (NTK + j + 1) * DH)

                @block.sync
                def _(sync, i=i):
                    if i >= 2:
                        sync.wait_ge(SEM["ve"], ck_ln[i - 2])
                    ck_x[i] = DMA(sync, "sy", xt2[:, b * D:(b + 1) * D], src)

                @block.vector
                def _(v, i=i):
                    WTOK(v, ck_x[i])
                    if i >= 2:
                        v.wait_ge(SEM["ve"], ck_ch[i - 2])
                    ins = v.reduce_sum(st[:, 0:1], xt2[:, b * D:(b + 1) * D], AX.X)
                    INC(ins, "ve")
                    ck_rs[i] = mark("ve")

                @block.scalar
                def _(s, i=i):
                    WTOK(s, ck_x[i])
                    s.wait_ge(SEM["ve"], ck_rs[i])
                    ins = s.activation(sqsc[:], xt2[:, b * D:(b + 1) * D], AF.Square,
                                       accum_out=st[:, 1:2])
                    INC(ins, "ac")
                    ck_sq[i] = mark("ac")

                @block.vector
                def _(v, i=i):
                    v.wait_ge(SEM["ac"], ck_sq[i])
                    v.tensor_scalar(st[:, 2:3], st[:, 0:1], 1.0 / D, None, ALU.mult)
                    vlink(v, v.tensor_scalar(st[:, 3:4], st[:, 1:2], 1.0 / D, None, ALU.mult))
                    vlink(v, v.tensor_tensor(st[:, 4:5], st[:, 2:3], st[:, 2:3], ALU.mult))
                    vlink(v, v.tensor_tensor(st[:, 5:6], st[:, 3:4], st[:, 4:5], ALU.subtract))
                    vlink(v, v.tensor_scalar(st[:, 6:7], st[:, 5:6], EPS, None, ALU.add))
                    ins = v.reciprocal(st[:, 7:8], st[:, 6:7])
                    INC(ins, "ve")
                    ck_ch[i] = mark("ve")

                @block.scalar
                def _(s, i=i):
                    s.wait_ge(SEM["ve"], ck_ch[i])
                    ins = s.activation(st[:, 6:7], st[:, 7:8], AF.Sqrt)
                    INC(ins, "ac")
                    ck_rst[i] = mark("ac")

                @block.vector
                def _(v, i=i):
                    v.wait_ge(SEM["ac"], ck_rst[i])
                    if i >= 2:
                        v.wait_ge(SEM["pe"], ck_tpx[i - 2])
                    ins = v.tensor_scalar(xln2[:, b * D:(b + 1) * D], xt2[:, b * D:(b + 1) * D],
                                          st[:, 2:3], st[:, 6:7], ALU.subtract, ALU.mult)
                    INC(ins, "ve")
                    ck_ln[i] = mark("ve")

                @block.tensor
                def _(t, i=i):
                    t.wait_ge(SEM["ve"], ck_ln[i])
                    if i >= 2:
                        t.wait_ge(SEM["ac"], ck_evx[i - 2])
                    else:
                        t.wait_ge(SEM["ac"], ck_qkvw)
                    src_ = xln2[:, b * D:(b + 1) * D]
                    bank = psbf(6 + (i % 2))
                    for c in range(8):
                        ins = t.transpose(bank[:, c * 128:(c + 1) * 128],
                                          src_[:, c * 128:(c + 1) * 128], ident_bf[:])
                    INC(ins, "pe")
                    ck_tpx[i] = mark("pe")

                @block.scalar
                def _(s, i=i):
                    s.wait_ge(SEM["pe"], ck_tpx[i])
                    if i >= 2:
                        s.wait_ge(SEM["pe"], ck_mm[i - 2][-1])
                    ins = s.copy(xf2[:, b * D:(b + 1) * D], psbf(6 + (i % 2)))
                    INC(ins, "ac")
                    ck_evx[i] = mark("ac")

                slices = (range(2, 6) if is_kv else range(0, 2))
                ck_mm[i] = []
                for sn, sl in enumerate(slices):
                    bank = PS[sn % 4]

                    @block.tensor
                    def _(t, i=i, sl=sl, sn=sn, bank=bank):
                        t.wait_ge(SEM["ac"], ck_evx[i])
                        WTOK(t, ck_biases[0])
                        if i >= 1 and sn < len(ck_kvev.get(i - 1, [])):
                            t.wait_ge(SEM["ac"], ck_kvev[i - 1][sn])
                        for c in range(8):
                            t.matmul(bank[:, 0:512],
                                     xf2[:, b * D + c * 128: b * D + (c + 1) * 128],
                                     qkvWp[:, c * 3072 + sl * 512:(c * 3072) + (sl + 1) * 512],
                                     start=(c == 0), stop=False)
                        ins = t.matmul(bank[:, 0:512], ones_bf[:],
                                       qkvB_bf[:, sl * 512:(sl + 1) * 512],
                                       start=False, stop=True)
                        INC(ins, "pe")
                        ck_mm[i].append(mark("pe"))

                ck_kvev[i] = []
                for sn, sl in enumerate(slices):
                    bank = PS[sn % 4]

                    @block.scalar
                    def _(s, i=i, sl=sl, sn=sn, bank=bank, is_kv=is_kv):
                        s.wait_ge(SEM["pe"], ck_mm[i][sn])
                        if is_kv and sl >= 4:
                            if i >= 2 and len(ck_spill.get(i - 2, [])) > 1:
                                WTOK(s, ck_spill[i - 2][1])
                            else:
                                s.wait_ge(SEM["ve"], ck_v1ones)
                            half = v1t2[:, b * HEADS * 65:(b + 1) * HEADS * 65]
                            dst = half.rearrange("p (h c) -> p h c", c=65)[
                                :, (sl - 4) * 8:(sl - 4) * 8 + 8, 0:64]
                            ins = s.copy(dst, bank[:, 0:512].rearrange("p (h c) -> p h c", h=8))
                        else:
                            if i >= 2:
                                s.wait_ge(SEM["ve"], ck_rope[i - 2])
                            ins = s.copy(ktm2[:, b * D + sn * 512: b * D + (sn + 1) * 512],
                                         bank[:, 0:512])
                        INC(ins, "ac")
                        ck_kvev[i].append(mark("ac"))


                @block.vector
                def _(v, i=i, tabslice=tabslice):
                    v.wait_ge(SEM["ac"], ck_kvev[i][1])
                    v.wait_ge(SEM["ac"], ck_tabs)
                    if i >= 2:
                        v.wait_ge(SEM["pe"], ck_tpk[i - 2])
                        v.wait_ge(SEM["ac"], ck_nsq[i - 2])
                    C = Ctab[:, tabslice]
                    S = Stab[:, tabslice]

                    def heads_ap(buf, off, lo):
                        return buf[:, off:off + D].rearrange("p (h s c) -> p h s c",
                                                             h=HEADS, s=2)[:, :, lo, :]

                    def tab_ap(tb, lo):
                        t3 = tb.rearrange("p (s c) -> p s c", s=2)[:, lo:lo + 1, :]
                        return t3.broadcast_to([128, HEADS, 32])

                    src_lo = heads_ap(ktm2, b * D, 0)
                    src_hi = heads_ap(ktm2, b * D, 1)
                    t1lo = heads_ap(rt1, 0, 0)
                    t2lo = heads_ap(rt2, 0, 0)
                    t1hi = heads_ap(rt1, 0, 1)
                    t2hi = heads_ap(rt2, 0, 1)
                    v.tensor_tensor(t1lo, src_lo, tab_ap(C, 0), ALU.mult)
                    v.tensor_tensor(t2lo, src_hi, tab_ap(S, 0), ALU.mult)
                    v.tensor_tensor(t1hi, src_hi, tab_ap(C, 1), ALU.mult)
                    vlink(v, v.tensor_tensor(t2hi, src_lo, tab_ap(S, 1), ALU.mult))
                    v.tensor_tensor(heads_ap(kro2, b * D, 0), t1lo, t2lo, ALU.add)
                    ins = v.tensor_tensor(heads_ap(kro2, b * D, 1), t1hi, t2hi, ALU.add)
                    INC(ins, "ve")
                    ck_rope[i] = mark("ve")

                @block.scalar
                def _(s, i=i):
                    s.wait_ge(SEM["ve"], ck_rope[i])
                    ins = s.activation(nsq[:], kro2[:, b * D:(b + 1) * D], AF.Square)
                    INC(ins, "ac")
                    ck_nsq[i] = mark("ac")

                @block.vector
                def _(v, i=i, is_kv=is_kv):
                    v.wait_ge(SEM["ac"], ck_nsq[i])
                    vlink(v, v.reduce_sum(ntmp[:], nsq[:].rearrange("p (h c) -> p h c", h=HEADS), AX.X))
                    acc = nk_acc if is_kv else nq_acc
                    ins = v.tensor_tensor(acc[:], acc[:], ntmp[:], ALU.max)
                    INC(ins, "ve")

                @block.tensor
                def _(t, i=i):
                    t.wait_ge(SEM["ve"], ck_rope[i])
                    if i >= 2:
                        t.wait_ge(SEM["ac"], ck_evk[i - 2])
                    bank = psbf(4 + (i % 2))
                    for c in range(8):
                        ins = t.transpose(bank[:, c * 128:(c + 1) * 128],
                                          kro2[:, b * D + c * 128: b * D + (c + 1) * 128],
                                          ident_bf[:])
                    INC(ins, "pe")
                    ck_tpk[i] = mark("pe")

                @block.scalar
                def _(s, i=i):
                    s.wait_ge(SEM["pe"], ck_tpk[i])
                    if i >= 2 and ck_spill.get(i - 2):
                        WTOK(s, ck_spill[i - 2][0])
                    ins = s.copy(kfm2[:, b * D:(b + 1) * D], psbf(4 + (i % 2)))
                    INC(ins, "ac")
                    ck_evk[i] = mark("ac")

                @block.sync
                def _(sync, i=i, is_kv=is_kv):
                    sync.wait_ge(SEM["ac"], ck_evk[i])
                    dst_sp = k_sp if is_kv else q_sp
                    ti = i if is_kv else i - NT_FULL
                    dst = dst_sp.rearrange("(c p) t -> p c t", p=128)[
                        :, :, ti * 128:(ti + 1) * 128]
                    toks = [DMA(sync, "sy", dst,
                                kfm2[:, b * D:(b + 1) * D].rearrange("p (c n) -> p c n", c=8))]
                    if is_kv:
                        sync.wait_ge(SEM["ac"], ck_kvev[i][-1])
                        vdst = v_sp.rearrange("(hp p) c -> p hp c", p=128)[
                            :, :, i * 130:(i + 1) * 130]
                        vsrc = v1t2[:, b * HEADS * 65:(b + 1) * HEADS * 65].rearrange(
                            "p (hp c) -> p hp c", c=130)
                        toks.append(DMA(sync, "sy", vdst, vsrc))
                    ck_spill[i] = toks

            for i in range(NT_FULL):
                p2_tile(i, True)

            @block.vector
            def _(v):
                v.memset(dacc[:], 1e30)
                vlink(v, v.memset(ind2[:], 0.0))
                vlink(v, v.memset(ind2[0:64, 0:1], 1.0))
                ins = v.memset(ind2[64:128, 1:2], 1.0)
                vlink(v, ins)
                ins2 = v.engine_nop()
                INC(ins2, "ve")
            ck_ind2 = mark("ve")

            ck_kld, ck_dtt, ck_dmm, ck_dev = {}, {}, {}, {}
            for i in range(NT_FULL, NT_FULL + NT_PROC):
                p2_tile(i, False)

                ti = i - NT_FULL
                kt = ti + 8
                b_ = i % 2

                @block.gpsimd
                def _(g, i=i, b_=b_, kt=kt):
                    WTOK(g, ck_spill[kt][0])
                    if i >= NT_FULL + 2:
                        g.wait_ge(SEM["ve"], ck_dtt[i - 2])
                    ck_kld[i] = DMA(g, "gp", kdl[:, b_ * D:(b_ + 1) * D],
                                    k_sp.rearrange("(c p) t -> p c t", p=128)[
                                        :, :, kt * 128:(kt + 1) * 128])

                @block.vector
                def _(v, i=i, b_=b_):
                    v.wait_ge(SEM["ac"], ck_evk[i])
                    WTOK(v, ck_kld[i])
                    if i >= NT_FULL + 1:
                        v.wait_ge(SEM["pe"], ck_dmm[i - 1])
                    ins = v.tensor_tensor(dprod[:], kfm2[:, b_ * D:(b_ + 1) * D],
                                          kdl[:, b_ * D:(b_ + 1) * D], ALU.mult)
                    INC(ins, "ve")
                    ck_dtt[i] = mark("ve")

                @block.tensor
                def _(t, i=i):
                    t.wait_ge(SEM["ve"], ck_dtt[i])
                    if i == NT_FULL:
                        t.wait_ge(SEM["ac"], mark("ac"))
                        t.wait_ge(SEM["ve"], ck_ind2)
                    else:
                        t.wait_ge(SEM["ve"], ck_dev[i - 1])
                    for c in range(8):
                        ins = t.matmul(PS[2][:, 2 * c:2 * c + 2],
                                       dprod[:, c * 128:(c + 1) * 128],
                                       ind2[:], start=True, stop=True)
                    INC(ins, "pe")
                    ck_dmm[i] = mark("pe")

                @block.vector
                def _(v, i=i):
                    v.wait_ge(SEM["pe"], ck_dmm[i])
                    ins = v.tensor_tensor(dacc[:], dacc[:], PS[2][:, 0:16], ALU.min)
                    vlink(v, ins)
                    ins2 = v.engine_nop()
                    INC(ins2, "ve")
                    ck_dev[i] = mark("ve")

            @block.vector
            def _(v):
                v.wait_ge(SEM["ve"], mark("ve"))
                ins = v.tensor_scalar(sqsc[:, 0:16], dacc[:], -1.0, None, ALU.mult)
                INC(ins, "ve")
            ck_bneg = mark("ve")

            @block.tensor
            def _(t):
                t.wait_ge(SEM["ve"], mark("ve"))
                t.transpose(PS[0][0:16, 0:128], nq_acc[:], ident_fs[:])
                t.transpose(PS[0][0:16, 128:256], nk_acc[:], ident_fs[:])
                ins = t.transpose(PS[0][0:16, 256:384], sqsc[:, 0:16], ident_fs[:])
                INC(ins, "pe")
            ck_bt = mark("pe")

            @block.vector
            def _(v):
                v.wait_ge(SEM["pe"], ck_bt)
                v.reduce_max(ntmp[0:16, 0:1], PS[0][0:16, 0:128].rearrange("h (o t) -> h o t", o=1), AX.X)
                vlink(v, v.reduce_max(ntmp[0:16, 1:2], PS[0][0:16, 128:256].rearrange("h (o t) -> h o t", o=1), AX.X))
                ins = v.reduce_max(ntmp[0:16, 6:7], PS[0][0:16, 256:384].rearrange("h (o t) -> h o t", o=1), AX.X)
                INC(ins, "ve")
            ck_bred = mark("ve")

            @block.scalar
            def _(s):
                s.wait_ge(SEM["ve"], ck_bred)
                ins = s.activation(ntmp[0:16, 2:4], ntmp[0:16, 0:2], AF.Sqrt)
                INC(ins, "ac")
            ck_bsq = mark("ac")

            @block.vector
            def _(v):
                v.wait_ge(SEM["ac"], ck_bsq)
                vlink(v, v.tensor_tensor(ntmp[0:16, 4:5], ntmp[0:16, 2:3], ntmp[0:16, 3:4], ALU.mult))
                vlink(v, v.tensor_scalar(ntmp[0:16, 5:6], ntmp[0:16, 4:5], -0.125, None, ALU.mult))
                vlink(v, v.tensor_scalar(ntmp[0:16, 7:8], ntmp[0:16, 6:7], 0.125, None, ALU.mult))
                vlink(v, v.tensor_scalar(ntmp[0:16, 7:8], ntmp[0:16, 7:8], -85.0, None, ALU.add))
                ins = v.tensor_tensor(ntmp[0:16, 5:6], ntmp[0:16, 5:6], ntmp[0:16, 7:8], ALU.max)
                INC(ins, "ve")
            ck_bnb = mark("ve")

            @block.tensor
            def _(t):
                t.wait_ge(SEM["ve"], ck_bnb)
                ins = t.transpose(PS[1][0:1, 0:16], ntmp[0:16, 5:6], ident_fs[0:16, 0:16])
                INC(ins, "pe")
            ck_bt2 = mark("pe")

            @block.scalar
            def _(s):
                s.wait_ge(SEM["pe"], ck_bt2)
                ins = s.copy(ntmp[0:1, 0:16], PS[1][0:1, 0:16])
                INC(ins, "ac")
            ck_bt3 = mark("ac")

            @block.tensor
            def _(t):
                t.wait_ge(SEM["ac"], ck_bt3)
                ins = t.matmul(PS[2][:, 0:16], ones_fs[:], ntmp[0:1, 0:16], start=True, stop=True)
                INC(ins, "pe")
            ck_bt4 = mark("pe")

            @block.scalar
            def _(s):
                s.wait_ge(SEM["pe"], ck_bt4)
                ins = s.copy(bmat[:], PS[2][:, 0:16])
                INC(ins, "ac")
            ck_bmat = mark("ac")

            if DEBUG:
                @block.sync
                def _(sync):
                    sync.wait_ge(SEM["ac"], ck_bmat)
                    DMA(sync, "sy", bm_dbg[:, 0:16], bmat[:])
                    DMA(sync, "sy", bm_dbg[:, 16:32], bmat[:])
                    DMA(sync, "sy", wp_dbg, qkvWp[:])
                    DMA(sync, "sy", tab_dbg[:, 0:56 * DH], Ctab[:])
                    DMA(sync, "sy", tab_dbg[:, 56 * DH:], Stab[:])


        WSCOPE.close()
        x_buf = E(nc.sbuf_tensor("x_buf", [128, NT_PROC * D], F32))
        AF_SCOPE = contextlib.ExitStack()
        attn_fm = AF_SCOPE.enter_context(nc.sbuf_tensor("attn_fm", [128, 8 * T_PROC], BF16))
        emit_barrier()

        with contextlib.ExitStack() as P3:
            E3 = P3.enter_context
            khp = E3(nc.sbuf_tensor("khp", [128, 2 * T_FULL], BF16))
            qhp = E3(nc.sbuf_tensor("qhp", [128, 2 * T_PROC], BF16))
            vhp = E3(nc.sbuf_tensor("vhp", [128, 2 * 32 * 130], BF16))
            PT = E3(nc.sbuf_tensor("PT", [128, 6 * 512], BF16))
            recs = E3(nc.sbuf_tensor("recs", [2, 2 * 512], F32))
            rbc = E3(nc.sbuf_tensor("rbc", [128, 2 * 512], F32))

            NKB = 32
            NQT = 6

            ck_kvload = {}
            bank_free_ac = {0: 0, 1: 0}
            pt_free_pe = {i: 0 for i in range(6)}
            pv_free_ve = {0: 0, 1: 0}
            recs_free_pe = 0
            ck_hp_pv_done = {}
            pend_epi = [None]

            for hp in range(8 if PHASES >= 3 else 0):
                bb = hp % 2

                @block.sync
                def _(sync, hp=hp, bb=bb):
                    if hp >= 2:
                        sync.wait_ge(SEM["pe"], ck_hp_pv_done[hp - 2])
                    toks = [DMA(sync, "sy", khp[:, bb * T_FULL:(bb + 1) * T_FULL],
                                k_sp[hp * 128:(hp + 1) * 128, :]),
                            DMA(sync, "sy", qhp[:, bb * T_PROC:(bb + 1) * T_PROC],
                                q_sp[hp * 128:(hp + 1) * 128, :])]
                    toks.append(DMA(sync, "sy", vhp[:, bb * 32 * 130:(bb + 1) * 32 * 130],
                                    v_sp[hp * 128:(hp + 1) * 128, :]))
                    ck_kvload[hp] = toks

                for qt in range(NQT):
                    pvb = 4 + (qt % 2) * 2
                    cka_k = {}
                    sl_of = {}

                    def emit_pv(kb, qt=qt, bb=bb, pvb=pvb):
                        slA, slB = sl_of[kb]
                        w_pv = pv_free_ve[qt % 2] if kb == 0 else 0

                        @block.tensor
                        def _(t, qt=qt, kb=kb, bb=bb, slA=slA, slB=slB, pvb=pvb,
                              cka=cka_k[kb], w_pv=w_pv):
                            t.wait_ge(SEM["ac"], cka)
                            if w_pv:
                                t.wait_ge(SEM["ve"], w_pv)
                            vA = vhp[:, bb * 32 * 130 + kb * 130: bb * 32 * 130 + kb * 130 + 65]
                            vB = vhp[:, bb * 32 * 130 + kb * 130 + 65: bb * 32 * 130 + (kb + 1) * 130]
                            t.matmul(PS[pvb][0:65, 0:512], vA, PT[:, slA * 512:(slA + 1) * 512],
                                     start=(kb == 0), stop=(kb == NKB - 1))
                            ins = t.matmul(PS[pvb + 1][0:65, 0:512], vB,
                                           PT[:, slB * 512:(slB + 1) * 512],
                                           start=(kb == 0), stop=(kb == NKB - 1))
                            INC(ins, "pe")
                        pt_free_pe[slA] = mark("pe")
                        pt_free_pe[slB] = mark("pe")

                    for kb in range(NKB):
                        bp = kb % 2
                        slA = (kb % 3) * 2
                        slB = slA + 1
                        sl_of[kb] = (slA, slB)
                        w_bank = bank_free_ac[bp]
                        w_slA = pt_free_pe[slA]
                        w_slB = pt_free_pe[slB]

                        @block.tensor
                        def _(t, hp=hp, qt=qt, kb=kb, bp=bp, bb=bb,
                              w_bank=w_bank, w_slA=w_slA, w_slB=w_slB):
                            if qt == 0 and kb == 0:
                                for tk in ck_kvload[hp]:
                                    WTOK(t, tk)
                            if w_bank:
                                t.wait_ge(SEM["ac"], w_bank)
                            kB = khp[:, bb * T_FULL + kb * 128: bb * T_FULL + (kb + 1) * 128]
                            qA = qhp[0:64, bb * T_PROC + qt * 512: bb * T_PROC + (qt + 1) * 512]
                            qB = qhp[64:128, bb * T_PROC + qt * 512: bb * T_PROC + (qt + 1) * 512]
                            t.matmul(PS[bp * 2][:, 0:512], kB[0:64, :], qA, start=True, stop=True)
                            ins = t.matmul(PS[bp * 2 + 1][:, 0:512], kB[64:128, :], qB,
                                           start=True, stop=True, tile_position=(64, 0))
                            INC(ins, "pe")
                        ck_smm = mark("pe")

                        @block.scalar
                        def _(s, hp=hp, bp=bp, slA=slA, slB=slB, ck_smm=ck_smm,
                              w_slA=w_slA, w_slB=w_slB):
                            s.wait_ge(SEM["pe"], ck_smm)
                            if w_slA:
                                s.wait_ge(SEM["pe"], w_slA)
                            if w_slB:
                                s.wait_ge(SEM["pe"], w_slB)
                            s.activation(PT[:, slA * 512:(slA + 1) * 512], PS[bp * 2][:, 0:512],
                                         AF.Exp, bias=bmat[:, 2 * hp:2 * hp + 1], scale=0.125)
                            ins = s.activation(PT[:, slB * 512:(slB + 1) * 512],
                                               PS[bp * 2 + 1][:, 0:512],
                                               AF.Exp, bias=bmat[:, 2 * hp + 1:2 * hp + 2],
                                               scale=0.125)
                            INC(ins, "ac")
                        bank_free_ac[bp] = mark("ac")
                        cka_k[kb] = mark("ac")

                        if kb >= 2:
                            emit_pv(kb - 2)

                    emit_pv(NKB - 2)
                    emit_pv(NKB - 1)

                    ck_pv_last = mark("pe")
                    if qt == NQT - 1:
                        ck_hp_pv_done[hp] = ck_pv_last

                    def mk_epi(ck_pv_last=ck_pv_last, pvb=pvb, qt=qt, hp=hp):
                      def emit_epi():
                        nonlocal recs_free_pe

                        @block.vector
                        def _(v, ck=ck_pv_last, pvb=pvb, rfp=recs_free_pe):
                            v.wait_ge(SEM["pe"], ck)
                            if rfp:
                                v.wait_ge(SEM["pe"], rfp)
                            v.reciprocal(recs[0:1, 0:512], PS[pvb][64:65, 0:512])
                            ins = v.reciprocal(recs[0:1, 512:1024], PS[pvb + 1][64:65, 0:512])
                            INC(ins, "ve")
                        ck_recip = mark("ve")

                        @block.tensor
                        def _(t, ck=ck_recip, b0=bank_free_ac[0]):
                            t.wait_ge(SEM["ve"], ck)
                            t.wait_ge(SEM["ac"], b0)
                            t.matmul(PS[0][:, 0:512], ones_fs[:], recs[0:1, 0:512],
                                     start=True, stop=True)
                            ins = t.matmul(PS[1][:, 0:512], ones_fs[:], recs[0:1, 512:1024],
                                           start=True, stop=True)
                            INC(ins, "pe")
                        recs_free_pe = mark("pe")
                        ck_bc = mark("pe")

                        @block.scalar
                        def _(s, ck=ck_bc, w0=pv_free_ve[0], w1=pv_free_ve[1]):
                            s.wait_ge(SEM["pe"], ck)
                            if w0:
                                s.wait_ge(SEM["ve"], w0)
                            if w1:
                                s.wait_ge(SEM["ve"], w1)
                            s.copy(rbc[:, 0:512], PS[0][:, 0:512])
                            ins = s.copy(rbc[:, 512:1024], PS[1][:, 0:512])
                            INC(ins, "ac")
                        bank_free_ac[0] = mark("ac")
                        ck_rbc = mark("ac")

                        @block.vector
                        def _(v, hp=hp, qt=qt, pvb=pvb, ck=ck_rbc):
                            v.wait_ge(SEM["ac"], ck)
                            dst = attn_fm[:, hp * T_PROC + qt * 512: hp * T_PROC + (qt + 1) * 512]
                            v.tensor_tensor(dst[0:64, :], PS[pvb][0:64, 0:512],
                                            rbc[0:64, 0:512], ALU.mult)
                            ins = v.tensor_tensor(dst[64:128, :], PS[pvb + 1][0:64, 0:512],
                                                  rbc[64:128, 512:1024], ALU.mult)
                            INC(ins, "ve")
                        pv_free_ve[qt % 2] = mark("ve")
                      return emit_epi

                    if pend_epi[0] is not None:
                        pend_epi[0]()
                    pend_epi[0] = mk_epi()

            if pend_epi[0] is not None:
                pend_epi[0]()
                pend_epi[0] = None

        emit_barrier()

        with contextlib.ExitStack() as P4:
            E4 = P4.enter_context
            projWp = E4(nc.sbuf_tensor("projWp", [128, 8 * D], BF16))
            projB_bf = E4(nc.sbuf_tensor("projB_bf", [1, D], BF16))
            with contextlib.ExitStack() as WTMP2:
                if PHASES >= 4:
                    emit_wt_transpose(proj_w, 8, projWp, D, WTMP2)

            ck_pbias = []

            @block.gpsimd
            def _(g):
                ck_pbias.append(DMA(g, "gp", projB_bf[:], proj_b))
            emit_barrier()

            xp2 = E4(nc.sbuf_tensor("xp2", [128, 2 * D], F32))
            ck_xp, ck_pmm, ck_pres = {}, {}, {}
            for t_ in range(NT_PROC if PHASES >= 4 else 0):
                bb = t_ % 2

                @block.sync
                def _(sync, t_=t_, bb=bb):
                    if t_ >= 2:
                        sync.wait_ge(SEM["ve"], ck_pres[t_ - 2])
                    ck_xp[t_] = DMA(sync, "sy", xp2[:, bb * D:(bb + 1) * D],
                                    x_proc[t_ * 128:(t_ + 1) * 128, :])

                @block.tensor
                def _(t, t_=t_, bb=bb):
                    WTOK(t, ck_pbias[0])
                    if t_ >= 2:
                        t.wait_ge(SEM["ve"], ck_pres[t_ - 2])
                    for half in range(2):
                        bank = PS[bb * 2 + half]
                        for c in range(8):
                            t.matmul(bank[:, 0:512],
                                     attn_fm[:, c * T_PROC + t_ * 128: c * T_PROC + (t_ + 1) * 128],
                                     projWp[:, c * D + half * 512: c * D + (half + 1) * 512],
                                     start=(c == 0), stop=False)
                        ins = t.matmul(bank[:, 0:512], ones_bf[:],
                                       projB_bf[:, half * 512:(half + 1) * 512],
                                       start=False, stop=True)
                    INC(ins, "pe")
                    ck_pmm[t_] = mark("pe")

                @block.vector
                def _(v, t_=t_, bb=bb):
                    v.wait_ge(SEM["pe"], ck_pmm[t_])
                    WTOK(v, ck_xp[t_])
                    for half in range(2):
                        ins = v.tensor_tensor(
                            x_buf[:, t_ * D + half * 512: t_ * D + (half + 1) * 512],
                            PS[bb * 2 + half][:, 0:512],
                            xp2[:, bb * D + half * 512: bb * D + (half + 1) * 512],
                            ALU.add)
                    INC(ins, "ve")
                    ck_pres[t_] = mark("ve")

            if DEBUG and PHASES >= 4:
                @block.sync
                def _(sync):
                    sync.wait_ge(SEM["ve"], mark("ve"))
                    DMA(sync, "sy", x1_dbg, x_buf[:])


        AF_SCOPE.close()
        swv = E(nc.sbuf_tensor("swv", [128, 2], F32))

        emit_barrier()
        with contextlib.ExitStack() as P5W:
          if PHASES >= 5:
            E5 = P5W.enter_context
            wld = E5(nc.sbuf_tensor("wld", [128, 2 * DHID], F32))
            zq = E5(nc.sbuf_tensor("zq", [128, DHID], F32))
            iq = E5(nc.sbuf_tensor("iq", [128, DHID], I32))
            wq = E5(nc.sbuf_tensor("wq", [128, DHID], F32))
            wqt = E5(nc.sbuf_tensor("wqt", [128, DHID], BF16))
            sacc = E5(nc.sbuf_tensor("sacc", [128, 4], F32))
            srow = E5(nc.sbuf_tensor("srow", [1, 128], F32))
            sred = E5(nc.sbuf_tensor("sred", [1, 4], F32))
            ck_wx, ck_q, ck_tq, ck_sp5 = {}, {}, {}, {}

            @block.vector
            def _(v):
                ins = v.memset(sacc[:], 0.0)
                INC(ins, "ve")
            ck_z5 = mark("ve")

            passA = [(w1, 32, D, 0), (w2, 8, DHID, 1)]
            ck_redA = {}
            redseq = []
            for wi, (w_ap, ntile, wcols, col) in enumerate(passA):
                for r in range(ntile):
                    @block.sync
                    def _(sync, w_ap=w_ap, r=r, wcols=wcols, wi=wi):
                        if len(redseq) >= 2:
                            sync.wait_ge(SEM["ve"], redseq[-2])
                        ck_wx[(wi, r)] = DMA(
                            sync, "sy", wld[:, (r % 2) * DHID:(r % 2) * DHID + wcols],
                            w_ap[r * 128:(r + 1) * 128, :])

                    @block.vector
                    def _(v, w_ap=w_ap, r=r, wcols=wcols, col=col, wi=wi):
                        WTOK(v, ck_wx[(wi, r)])
                        v.wait_ge(SEM["ve"], ck_z5)
                        v.wait_ge(SEM["ve"], CNT["ve"])
                        vlink(v, v.reduce_sum(sacc[:, 2:3], wld[:, (r % 2) * DHID:(r % 2) * DHID + wcols],
                                              AX.X, apply_absolute_value=True))
                        ins = v.tensor_tensor(sacc[:, col:col + 1], sacc[:, col:col + 1],
                                              sacc[:, 2:3], ALU.add)
                        INC(ins, "ve")
                        ck_redA[(wi, r)] = mark("ve")
                        redseq.append(mark("ve"))
            ck_pA = mark("ve")

            for col in range(2):
                @block.tensor
                def _(t, col=col):
                    t.wait_ge(SEM["ve"], mark("ve"))
                    ins = t.transpose(PS[0][0:1, 0:128], sacc[:, col:col + 1], ident_fs[:])
                    INC(ins, "pe")
                ck_t5 = mark("pe")

                @block.scalar
                def _(s, ck=ck_t5):
                    s.wait_ge(SEM["pe"], ck)
                    ins = s.copy(srow[:], PS[0][0:1, 0:128])
                    INC(ins, "ac")
                ck_e5 = mark("ac")

                @block.vector
                def _(v, ck=ck_e5, col=col):
                    v.wait_ge(SEM["ac"], ck)
                    n = float(DHID * D)
                    vlink(v, v.reduce_sum(sred[0:1, 0:1], srow[:].rearrange("o (a c) -> o a c", a=1), AX.X))
                    vlink(v, v.tensor_scalar(sred[0:1, 1:2], sred[0:1, 0:1], 1.0 / n, EPS,
                                             ALU.mult, ALU.max))
                    ins = v.reciprocal(sred[0:1, 2:3], sred[0:1, 1:2])
                    INC(ins, "ve")
                ck_r5 = mark("ve")

                @block.tensor
                def _(t, ck=ck_r5):
                    t.wait_ge(SEM["ve"], ck)
                    ins = t.matmul(PS[1][:, 0:1], ones_fs[:], sred[0:1, 2:3], start=True, stop=True)
                    INC(ins, "pe")
                ck_b5 = mark("pe")

                @block.scalar
                def _(s, ck=ck_b5, col=col):
                    s.wait_ge(SEM["pe"], ck)
                    s.copy(sacc[:, 2 + col:3 + col], PS[1][:, 0:1])
                    ins = s.copy(swv[:, col:col + 1], PS[1][:, 0:1])
                    INC(ins, "ac")
            ck_sw = mark("ac")

            wqt2 = E5(nc.sbuf_tensor("wqt2", [128, DHID], BF16))
            ck_zB, ck_qB, ck_tB, ck_eB, ck_spB = {}, {}, {}, {}, {}
            bank_ev = {6: 0, 7: 0}

            zBseq = []

            def wquant_tile(wi, w_ap, r, wcols, col, dst_sp):
                @block.sync
                def _(sync, wi=wi, w_ap=w_ap, r=r, wcols=wcols):
                    if len(zBseq) >= 2:
                        sync.wait_ge(SEM["ac"], zBseq[-2])
                    else:
                        sync.wait_ge(SEM["ve"], redseq[-1])
                    ck_wx[("B", wi, r)] = DMA(
                        sync, "sy", wld[:, (r % 2) * DHID:(r % 2) * DHID + wcols],
                        w_ap[r * 128:(r + 1) * 128, :])

                @block.scalar
                def _(s, wi=wi, r=r, wcols=wcols, col=col):
                    WTOK(s, ck_wx[("B", wi, r)])
                    s.wait_ge(SEM["ac"], ck_sw)
                    prev = ck_qB.get((wi, r - 1)) or ck_qB.get((1 - wi, "last"))
                    if prev:
                        s.wait_ge(SEM["ve"], prev)
                    ins = s.activation(zq[:, 0:wcols], wld[:, (r % 2) * DHID:(r % 2) * DHID + wcols],
                                       AF.Copy, scale=sacc[:, 2 + col:3 + col], bias=0.5)
                    INC(ins, "ac")
                    ck_zB[(wi, r)] = mark("ac")
                    zBseq.append(mark("ac"))

                @block.vector
                def _(v, wi=wi, r=r, wcols=wcols):
                    v.wait_ge(SEM["ac"], ck_zB[(wi, r)])
                    prevt = ck_tB.get((wi, r - 1))
                    if prevt:
                        v.wait_ge(SEM["pe"], prevt)
                    vlink(v, v.tensor_copy(iq[:, 0:wcols], zq[:, 0:wcols]))
                    vlink(v, v.tensor_copy(wq[:, 0:wcols], iq[:, 0:wcols]))
                    vlink(v, v.tensor_tensor(zq[:, 0:wcols], zq[:, 0:wcols], wq[:, 0:wcols], ALU.subtract))
                    vlink(v, v.scalar_tensor_tensor(wq[:, 0:wcols], zq[:, 0:wcols], 1.0,
                                                    wq[:, 0:wcols], ALU.is_ge, ALU.add))
                    vlink(v, v.scalar_tensor_tensor(wq[:, 0:wcols], zq[:, 0:wcols], 0.0,
                                                    wq[:, 0:wcols], ALU.is_ge, ALU.add))
                    vlink(v, v.tensor_scalar(wq[:, 0:wcols], wq[:, 0:wcols], 1.0, None, ALU.subtract))
                    ins = v.tensor_scalar(wqt[:, 0:wcols], wq[:, 0:wcols], -1.0, 1.0, ALU.max, ALU.min)
                    INC(ins, "ve")
                    ck_qB[(wi, r)] = mark("ve")
                    ck_qB[(wi, "last")] = mark("ve")

                nblk = wcols // 128
                for g in range(nblk // 8):
                    bk = 6 + (g % 2)

                    @block.tensor
                    def _(t, wi=wi, r=r, g=g, bk=bk):
                        t.wait_ge(SEM["ve"], ck_qB[(wi, r)])
                        if bank_ev[bk]:
                            t.wait_ge(SEM["ac"], bank_ev[bk])
                        bank = psbf(bk)
                        for j in range(8):
                            c = g * 8 + j
                            ins = t.transpose(bank[:, j * 128:(j + 1) * 128],
                                              wqt[:, c * 128:(c + 1) * 128], ident_bf[:])
                        INC(ins, "pe")
                        ck_tB[(wi, r, g)] = mark("pe")
                        ck_tB[(wi, r)] = mark("pe")

                    @block.scalar
                    def _(s, wi=wi, r=r, g=g, bk=bk):
                        s.wait_ge(SEM["pe"], ck_tB[(wi, r, g)])
                        for tk in ck_spB.get("last", []):
                            WTOK(s, tk)
                        ins = s.copy(wqt2[:, g * 1024:(g + 1) * 1024], psbf(bk))
                        INC(ins, "ac")
                        ck_eB[(wi, r, g)] = mark("ac")
                    bank_ev[bk] = mark("ac")

                @block.sync
                def _(sync, wi=wi, r=r, nblk=nblk, dst_sp=dst_sp):
                    sync.wait_ge(SEM["ac"], ck_eB[(wi, r, nblk // 8 - 1)])
                    toks = []
                    for g in range(nblk // 8):
                        dst = dst_sp.rearrange("(c p) n -> p c n", p=128)[
                            :, g * 8:(g + 1) * 8, r * 128:(r + 1) * 128]
                        src = wqt2[:, g * 1024:(g + 1) * 1024].rearrange("p (c n) -> p c n", c=8)
                        toks.append(DMA(sync, "sy", dst, src))
                    ck_spB[(wi, r)] = toks[-1]
                    ck_spB["last"] = toks

            for r in range(32):
                wquant_tile(0, w1, r, D, 0, w1q_sp)
            for r in range(8):
                wquant_tile(1, w2, r, DHID, 1, w2q_sp)

        shtab_g = E(nc.sbuf_tensor("shtab_g", [128, 32], F32))

        def emit_ffn(tiles, tag):
            emit_barrier()
            with contextlib.ExitStack() as PF:
                EF = PF.enter_context
                gamr = EF(nc.sbuf_tensor("gamr_%s" % tag, [128, D], BF16))
                betr = EF(nc.sbuf_tensor("betr_%s" % tag, [128, D], BF16))
                b1r = EF(nc.sbuf_tensor("b1r_%s" % tag, [1, DHID], BF16))
                stat = EF(nc.sbuf_tensor("stat_%s" % tag, [128, 2 * 16], F32))
                yf = EF(nc.sbuf_tensor("yf_%s" % tag, [128, D], F32))
                sqs = yf
                strow = EF(nc.sbuf_tensor("strow_%s" % tag, [1, 256], BF16))
                xqb = EF(nc.sbuf_tensor("xqb_%s" % tag, [128, D], BF16))
                xqf = EF(nc.sbuf_tensor("xqf_%s" % tag, [128, D], BF16))
                hbf = EF(nc.sbuf_tensor("hbf_%s" % tag, [128, DHID], F32))
                zp = EF(nc.sbuf_tensor("zp_%s" % tag, [128, 512], F32))
                ip_ = EF(nc.sbuf_tensor("ip_%s" % tag, [128, 512], I32))
                wp_ = EF(nc.sbuf_tensor("wp_%s" % tag, [128, 512], F32))
                w1qr = EF(nc.sbuf_tensor("w1qr_%s" % tag, [128, 8 * DHID], BF16))

                tok_c = []

                @block.sync
                def _(sync):
                    for c in range(8):
                        tok_c.append(DMA(sync, "sy", w1qr[:, c * DHID:(c + 1) * DHID],
                                         w1q_sp[c * 128:(c + 1) * 128, :]))

                @block.gpsimd
                def _(g):
                    tok_c.append(DMA(g, "gp", gamr[:], gam_rep))
                    tok_c.append(DMA(g, "gp", betr[:], bet_rep))
                    tok_c.append(DMA(g, "gp", b1r[:], b1))

                ck_st5, ck_sq5, ck_ch5, ck_s25 = {}, {}, {}, {}
                ck_sx5, ck_qx5, ck_tx5, ck_ex5, ck_zx5 = {}, {}, {}, {}, {}
                ck_m15, ck_g5, ck_hm5, ck_sp5 = {}, {}, {}, {}

                for ti, t_ in enumerate(tiles):
                    bb = ti % 2
                    st = stat[:, bb * 16:(bb + 1) * 16]
                    xa = x_buf[:, t_ * D:(t_ + 1) * D]

                    @block.vector
                    def _(v, ti=ti, t_=t_, st=st, xa=xa):
                        if ti >= 2:
                            v.wait_ge(SEM["ve"], ck_ch5[ti - 2])
                        if ti == 0:
                            for tk in tok_c:
                                WTOK(v, tk)
                        ins = v.reduce_sum(st[:, 0:1], xa, AX.X)
                        INC(ins, "ve")
                        ck_st5[ti] = mark("ve")

                    @block.scalar
                    def _(s, ti=ti, st=st, xa=xa):
                        s.wait_ge(SEM["ve"], ck_st5[ti])
                        ins = s.activation(sqs[:], xa, AF.Square, accum_out=st[:, 1:2])
                        INC(ins, "ac")
                        ck_sq5[ti] = mark("ac")

                    @block.vector
                    def _(v, ti=ti, st=st):
                        v.wait_ge(SEM["ac"], ck_sq5[ti])
                        v.tensor_scalar(st[:, 2:3], st[:, 0:1], 1.0 / D, None, ALU.mult)
                        vlink(v, v.tensor_scalar(st[:, 3:4], st[:, 1:2], 1.0 / D, None, ALU.mult))
                        vlink(v, v.tensor_tensor(st[:, 4:5], st[:, 2:3], st[:, 2:3], ALU.mult))
                        vlink(v, v.tensor_tensor(st[:, 5:6], st[:, 3:4], st[:, 4:5], ALU.subtract))
                        vlink(v, v.tensor_scalar(st[:, 6:7], st[:, 5:6], EPS, None, ALU.add))
                        vlink(v, v.reciprocal(st[:, 7:8], st[:, 6:7]))
                        vlink(v, v.tensor_tensor(st[:, 8:9], st[:, 5:6], st[:, 7:8], ALU.mult))
                        vlink(v, v.tensor_scalar(st[:, 9:10], st[:, 8:9], EPS, None, ALU.add))
                        ins = v.reciprocal(st[:, 10:11], st[:, 9:10])
                        INC(ins, "ve")
                        ck_ch5[ti] = mark("ve")

                    @block.scalar
                    def _(s, ti=ti, st=st):
                        s.wait_ge(SEM["ve"], ck_ch5[ti])
                        s.activation(st[:, 11:12], st[:, 7:8], AF.Sqrt)
                        ins = s.activation(st[:, 12:13], st[:, 10:11], AF.Sqrt)
                        INC(ins, "ac")
                        ck_s25[ti] = mark("ac")

                    @block.vector
                    def _(v, ti=ti, st=st, xa=xa, bb=bb):
                        v.wait_ge(SEM["ac"], ck_s25[ti])
                        if ti >= 1:
                            v.wait_ge(SEM["ac"], ck_zx5[(ti - 1, 1)])
                        vlink(v, v.tensor_tensor(st[:, 13:14], st[:, 11:12], st[:, 12:13], ALU.mult))
                        vlink(v, v.tensor_scalar(yf[:], xa, st[:, 2:3], st[:, 13:14],
                                                 ALU.subtract, ALU.mult))
                        vlink(v, v.tensor_tensor(yf[:], yf[:], gamr[:], ALU.mult))
                        vlink(v, v.tensor_tensor(yf[:], yf[:], betr[:], ALU.add))
                        vlink(v, v.reduce_max(st[:, 14:15], yf[:],
                                              AX.X, apply_absolute_value=True))
                        vlink(v, v.tensor_scalar(st[:, 14:15], st[:, 14:15], EPS, None, ALU.max))
                        vlink(v, v.reciprocal(st[:, 15:16], st[:, 14:15]))
                        vlink(v, v.tensor_scalar(st[:, 15:16], st[:, 15:16], 127.0, None, ALU.mult))
                        vlink(v, v.tensor_tensor(st[:, 4:5], st[:, 15:16], swv[:, 0:1], ALU.mult))
                        ins = v.reciprocal(st[:, 5:6], st[:, 4:5])
                        INC(ins, "ve")
                        ck_sx5[ti] = mark("ve")

                    for sub in range(2):
                        @block.scalar
                        def _(s, ti=ti, st=st, sub=sub):
                            s.wait_ge(SEM["ve"], ck_sx5[ti])
                            if sub == 1:
                                s.wait_ge(SEM["ve"], ck_qx5[(ti, 0)])
                            elif ti >= 1:
                                s.wait_ge(SEM["ve"], ck_qx5[(ti - 1, 1)])
                            ins = s.activation(zp[:], yf[:, sub * 512:(sub + 1) * 512],
                                               AF.Copy, scale=st[:, 15:16], bias=0.5)
                            INC(ins, "ac")
                            ck_zx5[(ti, sub)] = mark("ac")

                        @block.vector
                        def _(v, ti=ti, sub=sub):
                            v.wait_ge(SEM["ac"], ck_zx5[(ti, sub)])
                            if sub == 0 and ti >= 1:
                                v.wait_ge(SEM["pe"], ck_tx5[ti - 1])
                            vlink(v, v.tensor_copy(ip_[:], zp[:]))
                            vlink(v, v.tensor_copy(wp_[:], ip_[:]))
                            vlink(v, v.tensor_tensor(zp[:], zp[:], wp_[:], ALU.subtract))
                            vlink(v, v.scalar_tensor_tensor(wp_[:], zp[:], 1.0, wp_[:], ALU.is_ge, ALU.add))
                            vlink(v, v.scalar_tensor_tensor(wp_[:], zp[:], 0.0, wp_[:], ALU.is_ge, ALU.add))
                            ins = v.tensor_scalar(xqb[:, sub * 512:(sub + 1) * 512], wp_[:],
                                                  1.0, None, ALU.subtract)
                            INC(ins, "ve")
                            ck_qx5[(ti, sub)] = mark("ve")

                    @block.tensor
                    def _(t, ti=ti, st=st, bb=bb):
                        t.wait_ge(SEM["ve"], ck_qx5[(ti, 1)])
                        if ti >= 2:
                            t.wait_ge(SEM["ac"], ck_ex5[ti - 2])
                        bank = psbf(6 + (ti % 2))
                        for c in range(8):
                            ins = t.transpose(bank[:, c * 128:(c + 1) * 128],
                                              xqb[:, c * 128:(c + 1) * 128],
                                              ident_bf[:])
                        ins = t.transpose(PS[5][0:1, (ti % 2) * 128:(ti % 2) * 128 + 128],
                                          st[:, 4:5], ident_fs[:])
                        INC(ins, "pe")
                        ck_tx5[ti] = mark("pe")

                    @block.scalar
                    def _(s, ti=ti, bb=bb):
                        s.wait_ge(SEM["pe"], ck_tx5[ti])
                        if ti >= 1:
                            s.wait_ge(SEM["pe"], ck_m15[ti - 1][-1])
                        s.copy(xqf[:], psbf(6 + (ti % 2)))
                        ins = s.copy(strow[0:1, (ti % 2) * 128:(ti % 2) * 128 + 128],
                                     PS[5][0:1, (ti % 2) * 128:(ti % 2) * 128 + 128])
                        INC(ins, "ac")
                        ck_ex5[ti] = mark("ac")

                    ck_m15[ti] = []
                    for sl in range(8):
                        bank = PS[sl % 4]

                        @block.tensor
                        def _(t, ti=ti, sl=sl, bb=bb, bank=bank):
                            t.wait_ge(SEM["ac"], ck_ex5[ti])
                            if ti >= 1 and sl < len(ck_g5.get(ti - 1, [])):
                                t.wait_ge(SEM["ac"], ck_g5[ti - 1][sl])
                            if sl >= 4:
                                t.wait_ge(SEM["ac"], ck_g5[ti][sl - 4])
                            for c in range(8):
                                t.matmul(bank[:, 0:512],
                                         xqf[:, c * 128:(c + 1) * 128],
                                         w1qr[:, c * DHID + sl * 512: c * DHID + (sl + 1) * 512],
                                         start=(c == 0), stop=False)
                            ins = t.matmul(bank[:, 0:512],
                                           strow[0:1, (ti % 2) * 128:(ti % 2) * 128 + 128],
                                           b1r[:, sl * 512:(sl + 1) * 512], start=False, stop=True)
                            INC(ins, "pe")
                            ck_m15[ti].append(mark("pe"))

                        ck_g5.setdefault(ti, [])

                        @block.scalar
                        def _(s, ti=ti, sl=sl, bb=bb, bank=bank, st=st):
                            s.wait_ge(SEM["pe"], ck_m15[ti][sl])
                            if ti >= 1 and sl == 0:
                                WTOK(s, ck_sp5[ti - 1])
                                s.wait_ge(SEM["ve"], ck_hm5[ti - 1])
                            ins = s.activation(hbf[:, sl * 512:(sl + 1) * 512],
                                               bank[:, 0:512], AF.Gelu, scale=st[:, 5:6])
                            INC(ins, "ac")
                            ck_g5[ti].append(mark("ac"))

                    @block.vector
                    def _(v, ti=ti):
                        v.wait_ge(SEM["ac"], ck_g5[ti][7])
                        ins = v.reduce_max(shtab_g[:, ti:ti + 1], hbf[:],
                                           AX.X, apply_absolute_value=True)
                        INC(ins, "ve")
                        ck_hm5[ti] = mark("ve")

                    @block.sync
                    def _(sync, ti=ti, t_=t_):
                        sync.wait_ge(SEM["ac"], ck_g5[ti][7])
                        ck_sp5[ti] = DMA(sync, "sy",
                                         hq_sp.rearrange("(n p) c -> p n c", p=128)[:, t_, :],
                                         hbf[:])

            emit_barrier()
            with contextlib.ExitStack() as PF2:
                EF = PF2.enter_context
                w2qr = EF(nc.sbuf_tensor("w2qr_%s" % tag, [128, 32 * D], BF16))
                b2r2 = EF(nc.sbuf_tensor("b2r2_%s" % tag, [1, D], BF16))
                hld = EF(nc.sbuf_tensor("hld_%s" % tag, [128, 2 * D], F32))
                hqq = EF(nc.sbuf_tensor("hqq_%s" % tag, [128, D], BF16))
                hqi = EF(nc.sbuf_tensor("hqi2_%s" % tag, [128, DHID], BF16))
                zp = EF(nc.sbuf_tensor("zp2_%s" % tag, [128, 512], F32))
                ip_ = EF(nc.sbuf_tensor("ip2_%s" % tag, [128, 512], I32))
                wp_ = EF(nc.sbuf_tensor("wp2_%s" % tag, [128, 512], F32))
                st2 = EF(nc.sbuf_tensor("st2_%s" % tag, [128, 8], F32))
                strow2 = EF(nc.sbuf_tensor("strow2_%s" % tag, [1, 256], BF16))
                tok_c2 = []

                @block.sync
                def _(sync):
                    for c in range(32):
                        tok_c2.append(DMA(sync, "sy", w2qr[:, c * D:(c + 1) * D],
                                          w2q_sp[c * 128:(c + 1) * 128, :]))

                @block.gpsimd
                def _(g):
                    tok_c2.append(DMA(g, "gp", b2r2[:], b2))

                ck_hx, ck_m25, ck_r25 = {}, {}, {}
                ck_ah2, ck_zq2, ck_vq2, ck_tq2b, ck_eq2 = {}, {}, {}, {}, {}
                def emit_qload(ti, t_, qtr):
                    @block.sync
                    def _(sync, ti=ti, t_=t_, qtr=qtr):
                        prev = ck_zq2.get((ti, qtr - 2, 1)) if qtr >= 2 else (
                            ck_zq2.get((ti - 1, qtr + 2, 1)) if ti >= 1 else None)
                        if prev:
                            sync.wait_ge(SEM["ac"], prev)
                        ck_hx[(ti, qtr)] = DMA(
                            sync, "sy", hld[:, (qtr % 2) * D:(qtr % 2 + 1) * D],
                            hq_sp.rearrange("(n p) c -> p n c", p=128)[
                                :, t_, qtr * D:(qtr + 1) * D])

                for ti, t_ in enumerate(tiles):
                    bb = ti % 2
                    emit_qload(ti, t_, 0)
                    emit_qload(ti, t_, 1)

                    @block.vector
                    def _(v, ti=ti, bb=bb):
                        if ti >= 2:
                            v.wait_ge(SEM["ve"], ck_r25[ti - 2])
                        if ti == 0:
                            for tk in tok_c2:
                                WTOK(v, tk)
                        vlink(v, v.tensor_copy(st2[:, bb * 4:bb * 4 + 1], shtab_g[:, ti:ti + 1]))
                        vlink(v, v.tensor_scalar(st2[:, bb * 4:bb * 4 + 1], st2[:, bb * 4:bb * 4 + 1],
                                                 EPS, None, ALU.max))
                        vlink(v, v.reciprocal(st2[:, bb * 4 + 1:bb * 4 + 2], st2[:, bb * 4:bb * 4 + 1]))
                        vlink(v, v.tensor_scalar(st2[:, bb * 4 + 1:bb * 4 + 2], st2[:, bb * 4 + 1:bb * 4 + 2],
                                                 127.0, None, ALU.mult))
                        vlink(v, v.tensor_tensor(st2[:, bb * 4 + 2:bb * 4 + 3],
                                                 st2[:, bb * 4 + 1:bb * 4 + 2], swv[:, 1:2], ALU.mult))
                        ins = v.reciprocal(st2[:, bb * 4 + 3:bb * 4 + 4], st2[:, bb * 4 + 2:bb * 4 + 3])
                        INC(ins, "ve")
                        ck_ah2[ti] = mark("ve")

                    for qtr in range(4):
                        for sub in range(2):
                            @block.scalar
                            def _(s, ti=ti, bb=bb, qtr=qtr, sub=sub):
                                s.wait_ge(SEM["ve"], ck_ah2[ti])
                                WTOK(s, ck_hx[(ti, qtr)])
                                prev = ck_vq2.get((ti, qtr, sub - 1)) if sub > 0 else (
                                    ck_vq2.get((ti, qtr - 1, 1)) if qtr > 0 else (
                                        ck_vq2.get((ti - 1, 3, 1)) if ti >= 1 else None))
                                if prev:
                                    s.wait_ge(SEM["ve"], prev)
                                ins = s.activation(
                                    zp[:],
                                    hld[:, (qtr % 2) * D + sub * 512: (qtr % 2) * D + (sub + 1) * 512],
                                    AF.Copy, scale=st2[:, bb * 4 + 1:bb * 4 + 2], bias=0.5)
                                INC(ins, "ac")
                                ck_zq2[(ti, qtr, sub)] = mark("ac")

                            @block.vector
                            def _(v, ti=ti, bb=bb, qtr=qtr, sub=sub):
                                v.wait_ge(SEM["ac"], ck_zq2[(ti, qtr, sub)])
                                if sub == 0:
                                    prev = ck_tq2b.get((ti, qtr - 1)) if qtr > 0 else (
                                        ck_tq2b.get((ti - 1, 3)) if ti >= 1 else None)
                                    if prev:
                                        v.wait_ge(SEM["pe"], prev)
                                vlink(v, v.tensor_copy(ip_[:], zp[:]))
                                vlink(v, v.tensor_copy(wp_[:], ip_[:]))
                                vlink(v, v.tensor_tensor(zp[:], zp[:], wp_[:], ALU.subtract))
                                vlink(v, v.scalar_tensor_tensor(wp_[:], zp[:], 1.0, wp_[:], ALU.is_ge, ALU.add))
                                vlink(v, v.scalar_tensor_tensor(wp_[:], zp[:], 0.0, wp_[:], ALU.is_ge, ALU.add))
                                ins = v.tensor_scalar(hqq[:, sub * 512:(sub + 1) * 512],
                                                      wp_[:], 1.0, None, ALU.subtract)
                                INC(ins, "ve")
                                ck_vq2[(ti, qtr, sub)] = mark("ve")

                        @block.tensor
                        def _(t, ti=ti, bb=bb, g=qtr):
                            t.wait_ge(SEM["ve"], ck_vq2[(ti, g, 1)])
                            prev = ck_eq2.get((ti, g - 2)) if g >= 2 else (
                                ck_eq2.get((ti - 1, 3)) if ti >= 1 else None)
                            if prev:
                                t.wait_ge(SEM["ac"], prev)
                            bank = psbf(6 + (g % 2))
                            for j in range(8):
                                ins = t.transpose(bank[:, j * 128:(j + 1) * 128],
                                                  hqq[:, j * 128:(j + 1) * 128],
                                                  ident_bf[:])
                            if g == 0:
                                ins = t.transpose(PS[5][0:1, bb * 128:bb * 128 + 128],
                                                  st2[:, bb * 4 + 2:bb * 4 + 3], ident_fs[:])
                            INC(ins, "pe")
                            ck_tq2b[(ti, g)] = mark("pe")

                        @block.scalar
                        def _(s, ti=ti, g=qtr):
                            s.wait_ge(SEM["pe"], ck_tq2b[(ti, g)])
                            if ti >= 1 and g == 0:
                                s.wait_ge(SEM["pe"], ck_m25[ti - 1][-1])
                            ins = s.copy(hqi[:, g * 1024:(g + 1) * 1024], psbf(6 + (g % 2)))
                            if g == 0:
                                ins = s.copy(strow2[0:1, (ti % 2) * 128:(ti % 2) * 128 + 128],
                                             PS[5][0:1, (ti % 2) * 128:(ti % 2) * 128 + 128])
                            INC(ins, "ac")
                            ck_eq2[(ti, g)] = mark("ac")

                        if qtr + 2 < 4:
                            emit_qload(ti, t_, qtr + 2)

                    ck_m25[ti] = []
                    for half in range(2):
                        @block.tensor
                        def _(t, ti=ti, half=half, bb=bb):
                            t.wait_ge(SEM["ac"], ck_eq2[(ti, 3)])
                            if ti >= 2:
                                t.wait_ge(SEM["ve"], ck_r25[ti - 2])
                            bank = PS[bb * 2 + half]
                            for c in range(32):
                                t.matmul(bank[:, 0:512],
                                         hqi[:, c * 128:(c + 1) * 128],
                                         w2qr[:, c * D + half * 512: c * D + (half + 1) * 512],
                                         start=(c == 0), stop=False)
                            ins = t.matmul(bank[:, 0:512],
                                           strow2[0:1, (ti % 2) * 128:(ti % 2) * 128 + 128],
                                           b2r2[:, half * 512:(half + 1) * 512], start=False, stop=True)
                            INC(ins, "pe")
                            ck_m25[ti].append(mark("pe"))

                    @block.vector
                    def _(v, ti=ti, t_=t_, bb=bb):
                        v.wait_ge(SEM["pe"], ck_m25[ti][-1])
                        for half in range(2):
                            ins = v.scalar_tensor_tensor(
                                x_buf[:, t_ * D + half * 512: t_ * D + (half + 1) * 512],
                                PS[bb * 2 + half][:, 0:512],
                                st2[:, bb * 4 + 3:bb * 4 + 4],
                                x_buf[:, t_ * D + half * 512: t_ * D + (half + 1) * 512],
                                ALU.mult, ALU.add)
                        INC(ins, "ve")
                        ck_r25[ti] = mark("ve")

        if PHASES >= 5:
            emit_ffn(list(range(NT_PROC)), "f1")

        if DEBUG and 5 <= PHASES <= 6:
            @block.sync
            def _(sync):
                for sn, vv in CNT.items():
                    if vv:
                        sync.wait_ge(SEM[sn], vv)
                DMA(sync, "sy", x1_dbg, x_buf[:])


        emit_barrier()
        with contextlib.ExitStack() as P6S:
          if PHASES >= 6:
            E6 = P6S.enter_context
            bQKs = E6(nc.sbuf_tensor("bQKs", [128, 8 * 512], BF16))
            bVs = E6(nc.sbuf_tensor("bVs", [128, 8 * 512], BF16))
            bQKl = E6(nc.sbuf_tensor("bQKl", [128, 8 * 2048], BF16))
            bVl = E6(nc.sbuf_tensor("bVl", [128, 8 * 512], BF16))
            xt6 = E6(nc.sbuf_tensor("xt6", [128, D], BF16))
            x6f = E6(nc.sbuf_tensor("x6f", [128, D], BF16))
            st6 = E6(nc.sbuf_tensor("st6", [128, 16], F32))
            sq6 = E6(nc.sbuf_tensor("sq6", [128, 2560], F32))
            tmS = E6(nc.sbuf_tensor("tmS", [128, 512], BF16))
            tmL = E6(nc.sbuf_tensor("tmL", [128, 2048], BF16))
            vs65 = E6(nc.sbuf_tensor("vs65", [128, 520], BF16))
            vl65 = E6(nc.sbuf_tensor("vl65", [128, 520], BF16))
            fmst = E6(nc.sbuf_tensor("fmst", [128, 2560], BF16))
            ntm6 = E6(nc.sbuf_tensor("ntm6", [128, 16], F32))
            nacc = E6(nc.sbuf_tensor("nacc", [128, 32], F32))

            tok_w6 = []

            @block.gpsimd
            def _(g):
                for c in range(8):
                    tok_w6.append(DMA(g, "gp", bQKs[:, c * 512:(c + 1) * 512],
                                      bWqk_s[c * 128:(c + 1) * 128, :]))
                    tok_w6.append(DMA(g, "gp", bVs[:, c * 512:(c + 1) * 512],
                                      bWv_s[c * 128:(c + 1) * 128, :]))
                    tok_w6.append(DMA(g, "gp", bQKl[:, c * 2048:(c + 1) * 2048],
                                      bWqk_l[c * 128:(c + 1) * 128, :]))
                    tok_w6.append(DMA(g, "gp", bVl[:, c * 512:(c + 1) * 512],
                                      bWv_l[c * 128:(c + 1) * 128, :]))

            @block.vector
            def _(v):
                v.memset(nacc[:], 0.0)
                v.memset(vs65[:].rearrange("p (h c) -> p h c", c=65)[:, :, 64:65], 1.0)
                ins = v.memset(vl65[:].rearrange("p (h c) -> p h c", c=65)[:, :, 64:65], 1.0)
                INC(ins, "ve")
            ck_z6 = mark("ve")

            ck6 = {}
            for i in range(NT_PROC):
                own = i >= 8
                xa = x_buf[:, i * D:(i + 1) * D]

                @block.vector
                def _(v, i=i, xa=xa):
                    if i >= 1:
                        v.wait_ge(SEM["ve"], ck6[("ch", i - 1)])
                    ins = v.reduce_sum(st6[:, 0:1], xa, AX.X)
                    INC(ins, "ve")
                    ck6[("rs", i)] = mark("ve")

                @block.scalar
                def _(s, i=i, xa=xa):
                    s.wait_ge(SEM["ve"], ck6[("rs", i)])
                    ins = s.activation(sq6[:, 0:D], xa, AF.Square,
                                       accum_out=st6[:, 1:2])
                    INC(ins, "ac")
                    ck6[("sq", i)] = mark("ac")

                @block.vector
                def _(v, i=i):
                    v.wait_ge(SEM["ac"], ck6[("sq", i)])
                    v.tensor_scalar(st6[:, 2:3], st6[:, 0:1], 1.0 / D, None, ALU.mult)
                    vlink(v, v.tensor_scalar(st6[:, 3:4], st6[:, 1:2], 1.0 / D, None, ALU.mult))
                    vlink(v, v.tensor_tensor(st6[:, 4:5], st6[:, 2:3], st6[:, 2:3], ALU.mult))
                    vlink(v, v.tensor_tensor(st6[:, 5:6], st6[:, 3:4], st6[:, 4:5], ALU.subtract))
                    vlink(v, v.tensor_scalar(st6[:, 6:7], st6[:, 5:6], EPS, None, ALU.add))
                    ins = v.reciprocal(st6[:, 7:8], st6[:, 6:7])
                    INC(ins, "ve")
                    ck6[("ch", i)] = mark("ve")

                @block.scalar
                def _(s, i=i):
                    s.wait_ge(SEM["ve"], ck6[("ch", i)])
                    ins = s.activation(st6[:, 8:9], st6[:, 7:8], AF.Sqrt)
                    INC(ins, "ac")
                    ck6[("rst", i)] = mark("ac")

                @block.vector
                def _(v, i=i, xa=xa):
                    v.wait_ge(SEM["ac"], ck6[("rst", i)])
                    if i >= 1:
                        v.wait_ge(SEM["pe"], ck6[("tpx", i - 1)])
                    ins = v.tensor_scalar(xt6[:], xa, st6[:, 2:3], st6[:, 8:9],
                                          ALU.subtract, ALU.mult)
                    INC(ins, "ve")
                    ck6[("ln", i)] = mark("ve")

                @block.tensor
                def _(t, i=i):
                    t.wait_ge(SEM["ve"], ck6[("ln", i)])
                    if i >= 1:
                        t.wait_ge(SEM["ac"], ck6[("evx", i - 1)])
                    bank = psbf(6)
                    for c in range(8):
                        ins = t.transpose(bank[:, c * 128:(c + 1) * 128],
                                          xt6[:, c * 128:(c + 1) * 128], ident_bf[:])
                    INC(ins, "pe")
                    ck6[("tpx", i)] = mark("pe")

                @block.scalar
                def _(s, i=i):
                    s.wait_ge(SEM["pe"], ck6[("tpx", i)])
                    if i >= 1:
                        s.wait_ge(SEM["pe"], ck6[("mm", i - 1)][-1])
                    ins = s.copy(x6f[:], psbf(6))
                    INC(ins, "ac")
                    ck6[("evx", i)] = mark("ac")

                slmap = [("qks", bQKs, 0, 512), ("vs", bVs, 0, 512),
                         ("ql0", bQKl, 0, 512), ("ql1", bQKl, 512, 512),
                         ("kl0", bQKl, 1024, 512), ("kl1", bQKl, 1536, 512),
                         ("vl", bVl, 0, 512)]
                ck6[("mm", i)] = []
                ck6[("ev", i)] = []
                for sn, (nm, wt, off, wid) in enumerate(slmap):
                    bank = PS[sn % 4]

                    @block.tensor
                    def _(t, i=i, sn=sn, wt=wt, off=off, bank=bank):
                        t.wait_ge(SEM["ac"], ck6[("evx", i)])
                        if i == 0 and sn == 0:
                            for tk in tok_w6:
                                WTOK(t, tk)
                        if sn >= 4:
                            t.wait_ge(SEM["ac"], ck6[("ev", i)][sn - 4])
                        elif i >= 1:
                            prev_user = {0: 4, 1: 5, 2: 6, 3: 3}[sn]
                            t.wait_ge(SEM["ac"], ck6[("ev", i - 1)][prev_user])
                        for c in range(8):
                            base = c * (2048 if wt is bQKl else 512)
                            ins = t.matmul(bank[:, 0:512], x6f[:, c * 128:(c + 1) * 128],
                                           wt[:, base + off: base + off + 512],
                                           start=(c == 0), stop=(c == 7))
                        INC(ins, "pe")
                        ck6[("mm", i)].append(mark("pe"))

                    @block.scalar
                    def _(s, i=i, sn=sn, nm=nm, bank=bank):
                        s.wait_ge(SEM["pe"], ck6[("mm", i)][sn])
                        if i >= 1 and sn == 0:
                            s.wait_ge(SEM["pe"], ck6[("tpf", i - 1)][-1])
                            for t2 in ck6[("spl", i - 1)]:
                                WTOK(s, t2)
                        if nm == "qks":
                            ins = s.copy(tmS[:], bank[:, 0:512])
                        elif nm == "vs":
                            ins = s.copy(vs65[:].rearrange("p (h c) -> p h c", c=65)[:, :, 0:64],
                                         bank[:, 0:512].rearrange("p (h c) -> p h c", h=8))
                        elif nm.startswith("ql"):
                            ins = s.copy(tmL[:, int(nm[2]) * 512:(int(nm[2]) + 1) * 512], bank[:, 0:512])
                        elif nm.startswith("kl"):
                            ins = s.copy(tmL[:, 1024 + int(nm[2]) * 512: 1024 + (int(nm[2]) + 1) * 512],
                                         bank[:, 0:512])
                        else:
                            ins = s.copy(vl65[:].rearrange("p (h c) -> p h c", c=65)[:, :, 0:64],
                                         bank[:, 0:512].rearrange("p (h c) -> p h c", h=8))
                        INC(ins, "ac")
                        ck6[("ev", i)].append(mark("ac"))

                @block.scalar
                def _(s, i=i, own=own):
                    s.wait_ge(SEM["ac"], ck6[("ev", i)][6])
                    s.activation(sq6[:, 0:512], tmS[:], AF.Square)
                    ins = s.activation(sq6[:, 512:2560], tmL[:], AF.Square)
                    INC(ins, "ac")
                    ck6[("nsq", i)] = mark("ac")

                @block.vector
                def _(v, i=i, own=own):
                    v.wait_ge(SEM["ac"], ck6[("nsq", i)])
                    if own:
                        vlink(v, v.reduce_sum(ntm6[:, 0:8], sq6[:, 0:256].rearrange(
                            "p (h c) -> p h c", h=8), AX.X))
                        vlink(v, v.tensor_tensor(nacc[:, 0:8], nacc[:, 0:8], ntm6[:, 0:8], ALU.max))
                        vlink(v, v.reduce_sum(ntm6[:, 0:8], sq6[:, 512:1536].rearrange(
                            "p (h c) -> p h c", h=8), AX.X))
                        vlink(v, v.tensor_tensor(nacc[:, 16:24], nacc[:, 16:24], ntm6[:, 0:8], ALU.max))
                    vlink(v, v.reduce_sum(ntm6[:, 0:8], sq6[:, 256:512].rearrange(
                        "p (h c) -> p h c", h=8), AX.X))
                    vlink(v, v.tensor_tensor(nacc[:, 8:16], nacc[:, 8:16], ntm6[:, 0:8], ALU.max))
                    vlink(v, v.reduce_sum(ntm6[:, 0:8], sq6[:, 1536:2560].rearrange(
                        "p (h c) -> p h c", h=8), AX.X))
                    ins = v.tensor_tensor(nacc[:, 24:32], nacc[:, 24:32], ntm6[:, 0:8], ALU.max)
                    INC(ins, "ve")
                    ck6[("nrm", i)] = mark("ve")

                nt_list = [("ks", tmS, 256, 2, 0)] + ([("qs", tmS, 0, 2, 256)] if own else [])
                nt_list += [("kl", tmL, 1024, 8, 512)] + ([("ql", tmL, 0, 8, 2560 - 2048)] if own else [])
                fmoff = {"ks": 0, "qs": 256, "kl": 512, "ql": 1536}
                ck6[("tpf", i)] = []
                for gi, (nm, src, soff, nblk, _x) in enumerate(nt_list):
                    @block.tensor
                    def _(t, i=i, gi=gi, nm=nm, src=src, soff=soff, nblk=nblk):
                        t.wait_ge(SEM["ve"], ck6[("nrm", i)])
                        prev = ck6.get(("evf", i, gi - 1)) or (ck6.get(("evf", i - 1, "last")) if i >= 1 else None)
                        if prev:
                            t.wait_ge(SEM["ac"], prev)
                        bank = psbf(7)
                        for j in range(nblk):
                            ins = t.transpose(bank[:, j * 128:(j + 1) * 128],
                                              src[:, soff + j * 128: soff + (j + 1) * 128],
                                              ident_bf[:])
                        INC(ins, "pe")
                        ck6[("tpf", i)].append(mark("pe"))

                    @block.scalar
                    def _(s, i=i, gi=gi, nm=nm, nblk=nblk):
                        s.wait_ge(SEM["pe"], ck6[("tpf", i)][gi])
                        if i >= 1 and gi == 0:
                            for t2 in ck6[("spl", i - 1)]:
                                WTOK(s, t2)
                        ins = s.copy(fmst[:, fmoff[nm]:fmoff[nm] + nblk * 128], psbf(7)[:, 0:nblk * 128])
                        INC(ins, "ac")
                        ck6[("evf", i, gi)] = mark("ac")
                        ck6[("evf", i, "last")] = mark("ac")

                @block.sync
                def _(sync, i=i, own=own):
                    sync.wait_ge(SEM["ac"], ck6[("evf", i, "last")])
                    sync.wait_ge(SEM["ac"], ck6[("ev", i)][-1])
                    toks = []
                    toks.append(DMA(sync, "sy",
                                    ks_sp.rearrange("(c p) t -> p c t", p=128)[:, :, i * 128:(i + 1) * 128],
                                    fmst[:, 0:256].rearrange("p (c n) -> p c n", c=2)))
                    toks.append(DMA(sync, "sy",
                                    kl_sp.rearrange("(c p) t -> p c t", p=128)[:, :, i * 128:(i + 1) * 128],
                                    fmst[:, 512:1536].rearrange("p (c n) -> p c n", c=8)))
                    toks.append(DMA(sync, "sy",
                                    vs_sp.rearrange("(hp p) c -> p hp c", p=128)[:, :, i * 130:(i + 1) * 130],
                                    vs65[:].rearrange("p (hp c) -> p hp c", c=130)))
                    toks.append(DMA(sync, "sy",
                                    vl_sp.rearrange("(hp p) c -> p hp c", p=128)[:, :, i * 130:(i + 1) * 130],
                                    vl65[:].rearrange("p (hp c) -> p hp c", c=130)))
                    if own:
                        j = i - 8
                        toks.append(DMA(sync, "sy",
                                        qs_sp.rearrange("(c p) t -> p c t", p=128)[:, :, j * 128:(j + 1) * 128],
                                        fmst[:, 256:512].rearrange("p (c n) -> p c n", c=2)))
                        toks.append(DMA(sync, "sy",
                                        ql_sp.rearrange("(c p) t -> p c t", p=128)[:, :, j * 128:(j + 1) * 128],
                                        fmst[:, 1536:2560].rearrange("p (c n) -> p c n", c=8)))
                    ck6[("spl", i)] = toks

            @block.tensor
            def _(t):
                t.wait_ge(SEM["ve"], mark("ve"))
                t.wait_ge(SEM["ac"], mark("ac"))
                for j in range(4):
                    ins = t.transpose(PS[0][0:8, j * 128:(j + 1) * 128],
                                      nacc[:, j * 8:(j + 1) * 8], ident_fs[:])
                INC(ins, "pe")
            ck_b6 = mark("pe")

            @block.vector
            def _(v):
                v.wait_ge(SEM["pe"], ck_b6)
                for j in range(4):
                    vlink(v, v.reduce_max(ntm6[0:8, j:j + 1],
                                          PS[0][0:8, j * 128:(j + 1) * 128].rearrange(
                                              "h (o t) -> h o t", o=1), AX.X))
                ins = v.engine_nop()
                INC(ins, "ve")
            ck_b6b = mark("ve")

            @block.scalar
            def _(s):
                s.wait_ge(SEM["ve"], ck_b6b)
                ins = s.activation(ntm6[0:8, 4:8], ntm6[0:8, 0:4], AF.Sqrt)
                INC(ins, "ac")
            ck_b6c = mark("ac")

            @block.vector
            def _(v):
                v.wait_ge(SEM["ac"], ck_b6c)
                vlink(v, v.tensor_tensor(ntm6[0:8, 0:1], ntm6[0:8, 4:5], ntm6[0:8, 5:6], ALU.mult))
                vlink(v, v.tensor_tensor(ntm6[0:8, 1:2], ntm6[0:8, 6:7], ntm6[0:8, 7:8], ALU.mult))
                vlink(v, v.tensor_scalar(ntm6[0:8, 0:1], ntm6[0:8, 0:1], -1.0 / math.sqrt(32.0), None, ALU.mult))
                ins = v.tensor_scalar(ntm6[0:8, 1:2], ntm6[0:8, 1:2], -1.0 / math.sqrt(128.0), None, ALU.mult)
                INC(ins, "ve")
            ck_b6d = mark("ve")

            @block.tensor
            def _(t):
                t.wait_ge(SEM["ve"], ck_b6d)
                t.transpose(PS[1][0:1, 0:8], ntm6[0:8, 0:1], ident_fs[0:8, 0:8])
                ins = t.transpose(PS[1][0:1, 8:16], ntm6[0:8, 1:2], ident_fs[0:8, 0:8])
                INC(ins, "pe")
            ck_b6e = mark("pe")

            @block.scalar
            def _(s):
                s.wait_ge(SEM["pe"], ck_b6e)
                ins = s.copy(ntm6[0:1, 0:16], PS[1][0:1, 0:16])
                INC(ins, "ac")
            ck_b6f = mark("ac")

            @block.tensor
            def _(t):
                t.wait_ge(SEM["ac"], ck_b6f)
                t.matmul(PS[2][:, 0:8], ones_fs[:], ntm6[0:1, 0:8], start=True, stop=True)
                ins = t.matmul(PS[2][:, 8:16], ones_fs[:], ntm6[0:1, 8:16], start=True, stop=True)
                INC(ins, "pe")
            ck_b6g = mark("pe")

            @block.scalar
            def _(s):
                s.wait_ge(SEM["pe"], ck_b6g)
                ins = s.copy(bmat_b[:], PS[2][:, 0:16])
                INC(ins, "ac")
            ck_b6h = mark("ac")

            if DEBUG:
                @block.sync
                def _(sync):
                    sync.wait_ge(SEM["ac"], ck_b6h)
                    DMA(sync, "sy", bm_dbg[:, 16:32], bmat_b[:])

        emit_barrier()
        attnb_fm = None
        AF2 = contextlib.ExitStack()
        if PHASES >= 7:
            attnb_fm = AF2.enter_context(nc.sbuf_tensor("attnb_fm", [128, 8 * T_OWN], BF16))
        with contextlib.ExitStack() as P7S:
          if PHASES >= 7:
            E7 = P7S.enter_context
            kbuf = E7(nc.sbuf_tensor("kbuf", [128, 2 * T_PROC], BF16))
            qbuf = E7(nc.sbuf_tensor("qbuf", [128, 2 * T_OWN], BF16))
            vbuf = E7(nc.sbuf_tensor("vbuf", [128, 2 * NT_PROC * 130], BF16))
            PT7 = E7(nc.sbuf_tensor("PT7", [128, 4 * 512], BF16))
            recs7 = E7(nc.sbuf_tensor("recs7", [2, 2 * 512], F32))
            rbc7 = E7(nc.sbuf_tensor("rbc7", [128, 2 * 512], F32))
            wmk = E7(nc.sbuf_tensor("wmk", [128, max(1, N_MASKS) * 512], BF16))
            cmk = E7(nc.sbuf_tensor("cmk", [128, NT_PROC], F32))

            tok_m7 = []

            @block.sync
            def _(sync):
                tok_m7.append(DMA(sync, "sy", wmk[:], wmasks))
                tok_m7.append(DMA(sync, "sy", cmk[:], colmask))

            @block.vector
            def _(v):
                for tk in tok_m7:
                    WTOK(v, tk)
                ins = v.engine_nop()
                INC(ins, "ve")

            units = [("s", u) for u in range(4)] + [("l", u) for u in range(8)]
            bank7 = {0: 0, 1: 0}
            pt7 = {0: 0, 1: 0, 2: 0, 3: 0}
            pv7 = {0: 0, 1: 0}
            recs7_pe = 0
            ck_uload = {}
            ck_updone = {}

            for ui, (kind, u) in enumerate(units):
                bb = ui % 2

                @block.sync
                def _(sync, ui=ui, kind=kind, u=u, bb=bb):
                    if ui >= 2:
                        sync.wait_ge(SEM["pe"], ck_updone[ui - 2])
                    toks = []
                    if kind == "s":
                        toks.append(DMA(sync, "sy", kbuf[0:64, bb * T_PROC:(bb + 1) * T_PROC],
                                        ks_sp[u * 64:(u + 1) * 64, :]))
                        toks.append(DMA(sync, "sy", qbuf[0:64, bb * T_OWN: (bb + 1) * T_OWN],
                                        qs_sp[u * 64:(u + 1) * 64, :]))
                        toks.append(DMA(sync, "sy", vbuf[:, bb * NT_PROC * 130:(bb + 1) * NT_PROC * 130],
                                        vs_sp[u * 128:(u + 1) * 128, :]))
                    else:
                        toks.append(DMA(sync, "sy", kbuf[:, bb * T_PROC:(bb + 1) * T_PROC],
                                        kl_sp[u * 128:(u + 1) * 128, :]))
                        toks.append(DMA(sync, "sy", qbuf[:, bb * T_OWN:(bb + 1) * T_OWN],
                                        ql_sp[u * 128:(u + 1) * 128, :]))
                        toks.append(DMA(sync, "sy", vbuf[:, bb * NT_PROC * 130:(bb + 1) * NT_PROC * 130],
                                        vl_sp[(u // 2) * 128:(u // 2) * 128 + 128, :]))
                    ck_uload[ui] = toks

                wintiles = WIN_S_TILES if kind == "s" else WIN_L_TILES
                scale7 = 1.0 / math.sqrt(32.0) if kind == "s" else 1.0 / math.sqrt(128.0)
                for qc in range(4):
                    ents = wintiles[qc]
                    pvb = 4 + (qc % 2)
                    nk = len(ents)
                    for ki, (kb, msk) in enumerate(ents):
                        bp = ki % 2
                        slA = bp * 2
                        slB = bp * 2 + 1
                        w_bank = bank7[bp]
                        w_slA = pt7[slA]
                        w_slB = pt7[slB]
                        w_pv = pv7[qc % 2] if ki == 0 else 0

                        @block.tensor
                        def _(t, ui=ui, kind=kind, u=u, bb=bb, qc=qc, kb=kb, bp=bp,
                              w_bank=w_bank, qc0=(qc == 0 and ki == 0)):
                            if qc0:
                                for tk in ck_uload[ui]:
                                    WTOK(t, tk)
                            if w_bank:
                                t.wait_ge(SEM["ac"], w_bank)
                            kB = kbuf[:, bb * T_PROC + kb * 128: bb * T_PROC + (kb + 1) * 128]
                            qs = qbuf[:, bb * T_OWN + qc * 512: bb * T_OWN + (qc + 1) * 512]
                            if kind == "s":
                                t.matmul(PS[bp * 2][:, 0:512], kB[0:32, :], qs[0:32, :],
                                         start=True, stop=True)
                                ins = t.matmul(PS[bp * 2 + 1][:, 0:512], kB[32:64, :], qs[32:64, :],
                                               start=True, stop=True, tile_position=(32, 0))
                            else:
                                ins = t.matmul(PS[bp * 2][:, 0:512], kB, qs,
                                               start=True, stop=True)
                            INC(ins, "pe")
                        ck_smm7 = mark("pe")

                        @block.scalar
                        def _(s, ui=ui, kind=kind, u=u, bp=bp, slA=slA, slB=slB,
                              ck=ck_smm7, w_slA=w_slA, w_slB=w_slB, scale7=scale7):
                            s.wait_ge(SEM["pe"], ck)
                            if w_slA:
                                s.wait_ge(SEM["pe"], w_slA)
                            if w_slB:
                                s.wait_ge(SEM["pe"], w_slB)
                            if kind == "s":
                                s.activation(PT7[:, slA * 512:(slA + 1) * 512], PS[bp * 2][:, 0:512],
                                             AF.Exp, bias=bmat_b[:, 2 * u:2 * u + 1], scale=scale7)
                                ins = s.activation(PT7[:, slB * 512:(slB + 1) * 512],
                                                   PS[bp * 2 + 1][:, 0:512],
                                                   AF.Exp, bias=bmat_b[:, 2 * u + 1:2 * u + 2],
                                                   scale=scale7)
                            else:
                                ins = s.activation(PT7[:, slA * 512:(slA + 1) * 512],
                                                   PS[bp * 2][:, 0:512],
                                                   AF.Exp, bias=bmat_b[:, 8 + u:9 + u], scale=scale7)
                            INC(ins, "ac")
                        bank7[bp] = mark("ac")
                        ck_exp7 = mark("ac")

                        @block.vector
                        def _(v, kind=kind, kb=kb, msk=msk, slA=slA, slB=slB, ck=ck_exp7):
                            v.wait_ge(SEM["ac"], ck)
                            ins = None
                            for sl in ([slA, slB] if kind == "s" else [slA]):
                                if msk is not None:
                                    if ins is not None:
                                        vlink(v, ins)
                                    ins = v.tensor_tensor(
                                        PT7[:, sl * 512:(sl + 1) * 512],
                                        PT7[:, sl * 512:(sl + 1) * 512],
                                        wmk[:, _mask_slot(msk) * 512:(_mask_slot(msk) + 1) * 512],
                                        ALU.mult)
                                if kb < 8:
                                    if ins is not None:
                                        vlink(v, ins)
                                    ins = v.tensor_scalar(
                                        PT7[:, sl * 512:(sl + 1) * 512],
                                        PT7[:, sl * 512:(sl + 1) * 512],
                                        cmk[:, kb:kb + 1], None, ALU.mult)
                            if ins is None:
                                ins = v.engine_nop()
                            INC(ins, "ve")
                        ck_msk7 = mark("ve")

                        @block.tensor
                        def _(t, kind=kind, bb=bb, kb=kb, slA=slA, slB=slB, pvb=pvb,
                              ck=ck_msk7, ki=ki, nk=nk, w_pv=w_pv):
                            t.wait_ge(SEM["ve"], ck)
                            if w_pv:
                                t.wait_ge(SEM["ve"], w_pv)
                            vA = vbuf[:, bb * NT_PROC * 130 + kb * 130: bb * NT_PROC * 130 + kb * 130 + 65]
                            vB = vbuf[:, bb * NT_PROC * 130 + kb * 130 + 65: bb * NT_PROC * 130 + (kb + 1) * 130]
                            if kind == "s":
                                t.matmul(PS[pvb][0:65, 0:512], vA, PT7[:, slA * 512:(slA + 1) * 512],
                                         start=(ki == 0), stop=(ki == nk - 1))
                                ins = t.matmul(PS[6 + (pvb - 4)][0:65, 0:512], vB,
                                               PT7[:, slB * 512:(slB + 1) * 512],
                                               start=(ki == 0), stop=(ki == nk - 1))
                            else:
                                vv = vA if u % 2 == 0 else vB
                                ins = t.matmul(PS[pvb][0:65, 0:512], vv,
                                               PT7[:, slA * 512:(slA + 1) * 512],
                                               start=(ki == 0), stop=(ki == nk - 1))
                            INC(ins, "pe")
                        pt7[slA] = mark("pe")
                        pt7[slB] = mark("pe")

                    ck_pvl7 = mark("pe")
                    if qc == 3:
                        ck_updone[ui] = ck_pvl7

                    @block.vector
                    def _(v, ck=ck_pvl7, pvb=pvb, kind=kind, rfp=recs7_pe):
                        v.wait_ge(SEM["pe"], ck)
                        if rfp:
                            v.wait_ge(SEM["pe"], rfp)
                        v.reciprocal(recs7[0:1, 0:512], PS[pvb][64:65, 0:512])
                        if kind == "s":
                            ins = v.reciprocal(recs7[0:1, 512:1024], PS[6 + (pvb - 4)][64:65, 0:512])
                        else:
                            ins = v.engine_nop()
                        INC(ins, "ve")
                    ck_rc7 = mark("ve")

                    @block.tensor
                    def _(t, ck=ck_rc7, kind=kind, b0=bank7[0]):
                        t.wait_ge(SEM["ve"], ck)
                        t.wait_ge(SEM["ac"], b0)
                        if kind == "s":
                            t.matmul(PS[0][:, 0:512], ones_fs[:], recs7[0:1, 0:512],
                                     start=True, stop=True)
                            ins = t.matmul(PS[1][:, 0:512], ones_fs[:], recs7[0:1, 512:1024],
                                           start=True, stop=True)
                        else:
                            ins = t.matmul(PS[0][:, 0:512], ones_fs[:], recs7[0:1, 0:512],
                                           start=True, stop=True)
                        INC(ins, "pe")
                    recs7_pe = mark("pe")
                    ck_bc7 = mark("pe")

                    @block.scalar
                    def _(s, ck=ck_bc7, kind=kind, w0=pv7[0], w1=pv7[1]):
                        s.wait_ge(SEM["pe"], ck)
                        if w0:
                            s.wait_ge(SEM["ve"], w0)
                        if w1:
                            s.wait_ge(SEM["ve"], w1)
                        if kind == "s":
                            s.copy(rbc7[:, 0:512], PS[0][:, 0:512])
                            ins = s.copy(rbc7[:, 512:1024], PS[1][:, 0:512])
                        else:
                            ins = s.copy(rbc7[:, 0:512], PS[0][:, 0:512])
                        INC(ins, "ac")
                    bank7[0] = mark("ac")
                    ck_rb7 = mark("ac")

                    @block.vector
                    def _(v, ck=ck_rb7, kind=kind, u=u, qc=qc, pvb=pvb):
                        v.wait_ge(SEM["ac"], ck)
                        if kind == "s":
                            dst = attnb_fm[:, u * T_OWN + qc * 512: u * T_OWN + (qc + 1) * 512]
                            v.tensor_tensor(dst[0:64, :], PS[pvb][0:64, 0:512],
                                            rbc7[0:64, 0:512], ALU.mult)
                            ins = v.tensor_tensor(dst[64:128, :], PS[6 + (pvb - 4)][0:64, 0:512],
                                                  rbc7[64:128, 512:1024], ALU.mult)
                        else:
                            ch = 4 + u // 2
                            r0 = (u % 2) * 64
                            dst = attnb_fm[:, ch * T_OWN + qc * 512: ch * T_OWN + (qc + 1) * 512]
                            ins = v.tensor_tensor(dst[r0:r0 + 64, :], PS[pvb][0:64, 0:512],
                                                  rbc7[r0:r0 + 64, 0:512], ALU.mult)
                        INC(ins, "ve")
                    pv7[qc % 2] = mark("ve")

        emit_barrier()
        with contextlib.ExitStack() as P8S:
          if PHASES >= 7:
            E8 = P8S.enter_context
            bprojWn = E8(nc.sbuf_tensor("bprojWn", [128, 8 * D], BF16))
            tok_w8 = []

            @block.gpsimd
            def _(g):
                for c in range(8):
                    tok_w8.append(DMA(g, "gp", bprojWn[:, c * D:(c + 1) * D],
                                      bWproj[c * 128:(c + 1) * 128, :]))

            ck_m8, ck_r8 = {}, {}
            for j in range(NT_OWN):
                t_ = 8 + j
                bb = j % 2

                @block.tensor
                def _(t, j=j, t_=t_, bb=bb):
                    if j == 0:
                        for tk in tok_w8:
                            WTOK(t, tk)
                    if j >= 2:
                        t.wait_ge(SEM["ve"], ck_r8[j - 2])
                    for half in range(2):
                        bank = PS[bb * 2 + half]
                        for c in range(8):
                            ins = t.matmul(bank[:, 0:512],
                                           attnb_fm[:, c * T_OWN + j * 128: c * T_OWN + (j + 1) * 128],
                                           bprojWn[:, c * D + half * 512: c * D + (half + 1) * 512],
                                           start=(c == 0), stop=(c == 7))
                    INC(ins, "pe")
                    ck_m8[j] = mark("pe")

                @block.vector
                def _(v, j=j, t_=t_, bb=bb):
                    v.wait_ge(SEM["pe"], ck_m8[j])
                    for half in range(2):
                        ins = v.tensor_tensor(
                            x_buf[:, t_ * D + half * 512: t_ * D + (half + 1) * 512],
                            PS[bb * 2 + half][:, 0:512],
                            x_buf[:, t_ * D + half * 512: t_ * D + (half + 1) * 512],
                            ALU.add)
                    INC(ins, "ve")
                    ck_r8[j] = mark("ve")

        AF2.close()

        if DEBUG and PHASES == 7:
            @block.sync
            def _(sync):
                for sn, vv in CNT.items():
                    if vv:
                        sync.wait_ge(SEM[sn], vv)
                DMA(sync, "sy", x1_dbg, x_buf[:])

        if PHASES >= 8:
            emit_ffn(list(range(8, 8 + NT_OWN)), "f2")

        if DEBUG and PHASES >= 8:
            @block.sync
            def _(sync):
                for sn, vv in CNT.items():
                    if vv:
                        sync.wait_ge(SEM[sn], vv)
                DMA(sync, "sy", x1_dbg, x_buf[:])

        if PHASES >= 8:
            @block.sync
            def _(sync):
                for sn, vv in CNT.items():
                    if vv:
                        sync.wait_ge(SEM[sn], vv)
                for j in range(NT_OWN):
                    DMA(sync, "sy", out[j * 128:(j + 1) * 128, :],
                        x_buf[:, (8 + j) * D:(9 + j) * D])

        @block.sync
        def _(sync):
            for sn, vv in CNT.items():
                if vv:
                    sync.wait_ge(SEM[sn], vv)

    return nc



def kernel(L0_attn_a_x, L0_attn_a_qkv_w, L0_attn_a_qkv_b, L0_attn_a_proj_w,
           L0_attn_a_proj_b, L0_attn_b_Wqk_short, L0_attn_b_Wv_short,
           L0_attn_b_Wqk_long, L0_attn_b_Wv_long, L0_attn_b_Wproj,
           L0_ffn_w1, L0_ffn_b1, L0_ffn_w2, L0_ffn_b2, L0_ffn_gamma, L0_ffn_beta):
    x0 = np.ascontiguousarray(np.asarray(L0_attn_a_x, dtype=np.float32))
    B = x0.shape[0]
    nc = build_program()

    shared = {
        "qkv_w": np.ascontiguousarray(L0_attn_a_qkv_w, dtype=np.float32),
        "qkv_b": np.asarray(L0_attn_a_qkv_b, np.float32).reshape(1, -1),
        "proj_w": np.ascontiguousarray(L0_attn_a_proj_w, dtype=np.float32),
        "proj_b": np.asarray(L0_attn_a_proj_b, np.float32).reshape(1, -1),
        "bWqk_s": np.ascontiguousarray(L0_attn_b_Wqk_short, dtype=np.float32),
        "bWv_s": np.ascontiguousarray(L0_attn_b_Wv_short, dtype=np.float32),
        "bWqk_l": np.ascontiguousarray(L0_attn_b_Wqk_long, dtype=np.float32),
        "bWv_l": np.ascontiguousarray(L0_attn_b_Wv_long, dtype=np.float32),
        "bWproj": np.ascontiguousarray(L0_attn_b_Wproj, dtype=np.float32),
        "w1": np.ascontiguousarray(L0_ffn_w1, dtype=np.float32),
        "b1": np.asarray(L0_ffn_b1, np.float32).reshape(1, -1),
        "w2": np.ascontiguousarray(L0_ffn_w2, dtype=np.float32),
        "b2": np.asarray(L0_ffn_b2, np.float32).reshape(1, -1),
        "gam_rep": np.ascontiguousarray(
            np.broadcast_to(np.asarray(L0_ffn_gamma, np.float32), (128, D))),
        "bet_rep": np.ascontiguousarray(
            np.broadcast_to(np.asarray(L0_ffn_beta, np.float32), (128, D))),
        "ident_f": np.eye(128, dtype=np.float32),
        "ones_rw": np.ones((1, 128), np.float32),
        "wmasks": (np.concatenate([m for m in _MASKS], axis=1)
                   if N_MASKS else np.zeros((128, 512), BF16NP)),
    }
    thk_hi, thk_lo = _theta_tables(np.arange(T_FULL))

    in_maps = []
    for c in range(8):
        b, h = c // 2, c % 2
        if h == 0:
            xp = np.concatenate([np.zeros((T_CTX, D), np.float32), x0[b, :T_OWN]], axis=0)
            qpos = np.arange(-T_CTX, T_OWN)
        else:
            xp = x0[b, T_CTX:T_FULL]
            qpos = np.arange(T_CTX, T_FULL)
        thq_hi, thq_lo = _theta_tables(qpos)
        cm = np.zeros((128, NT_PROC), np.float32)
        for kb in range(NT_PROC):
            alive = (kb * 128 + np.arange(128)) >= (T_CTX if h == 0 else 0)
            cm[:, kb] = alive.astype(np.float32)
        m = dict(shared)
        m.update({
            "x_proc": np.ascontiguousarray(xp),
            "x_full": np.ascontiguousarray(x0[b]),
            "thq_hi": thq_hi, "thq_lo": thq_lo,
            "thk_hi": thk_hi, "thk_lo": thk_lo,
            "colmask": cm,
        })
        in_maps.append(m)

    res = run_bass_kernel_spmd(nc, in_maps, core_ids=list(range(8)))
    outp = np.empty((B, T_FULL, D), np.float32)
    for c in range(8):
        b, h = c // 2, c % 2
        outp[b, h * T_OWN:(h + 1) * T_OWN] = res.results[c]["out"]
    if DEBUG:
        kernel._dbg = res.results
    return outp
